# version 05
# baseline (speedup 1.0000x reference)
import numpy as np

import concourse.bass as bass
import concourse.mybir as mybir
from concourse.bass_utils import run_bass_kernel_spmd

F32 = mybir.dt.float32
BF16 = mybir.dt.bfloat16
AF = mybir.ActivationFunctionType
OP = mybir.AluOpType
AX = mybir.AxisListType

B = 4
S = 4096
DM = 4096
NTOK = 2048
CH = 256
NCH = NTOK // CH
NF = DM // 128
NMQ = 32
NMK = 8
HQ = 32
KVH = 8
MODC = 6 * DM
NPAN = MODC // 128
RSCALE = float(np.float32(1.0) / np.float32(np.sqrt(np.float32(128.0))))
EPS = 1e-5
SCALE_FFN = 4.0


class Builder:

    def __init__(self, mode, events=None, engines=None, sems=None):
        self.mode = mode
        self.events = events if events is not None else {}
        self.engines = engines
        self.sems = sems
        self.counts = {"pe": 0, "dve": 0, "act": 0, "syn": 0, "gp": 0}

    CHAIN = ("dve", "act")

    def op(self, eng, kind, key=None, waits=(), sem=None, **args):
        track = sem if kind == "dma" else eng
        if self.mode == "count":
            if eng in self.CHAIN:
                self.counts[eng] = self.counts.get(eng, 0) + 1
                if key is not None:
                    assert key not in self.events, key
                    self.events[key] = (eng, self.counts[eng])
            elif key is not None:
                self.counts[track] = self.counts.get(track, 0) + (
                    16 if kind == "dma" else 1)
                assert key not in self.events, key
                self.events[key] = (track, self.counts[track])
            return
        if self.mode != eng:
            return
        h = self.engines[eng]
        for k in waits:
            peng, val = self.events[k]
            if peng != eng or peng not in ("pe", "dve", "act"):
                h.wait_ge(self.sems[peng], val)
        if eng in self.CHAIN:
            pre = self.counts.get(eng, 0)
            self.counts[eng] = pre + 1
            if pre > 0:
                h.wait_ge(self.sems[eng], pre)
        if kind == "dma":
            pre = self.events[key][1] - 16
            if pre > 0:
                h.wait_ge(self.sems[sem], pre)
            ins = h.dma_start(out=args["out"], in_=args["in_"])
        elif kind == "mm":
            ins = h.matmul(args["out"], args["lhsT"], args["rhs"],
                           start=args["start"], stop=args["stop"],
                           tile_position=args.get("tp"))
        elif kind == "tp":
            ins = h.transpose(args["out"], args["in_"], args["ident"])
        elif kind == "act":
            ins = h.activation(args["out"], args["in_"], args["func"],
                               bias=args.get("bias", 0.0), scale=args.get("scale", 1.0))
        elif kind == "sq":
            ins = h.square(args["out"], args["in_"])
        elif kind == "scp":
            ins = h.copy(args["out"], args["in_"])
        elif kind == "cp":
            ins = h.tensor_copy(args["out"], args["in_"])
        elif kind == "tt":
            ins = h.tensor_tensor(args["out"], args["in0"], args["in1"], args["op"])
        elif kind == "ts":
            ins = h.tensor_scalar(args["out"], args["in0"], args["s1"],
                                  args.get("s2"), args["op0"],
                                  args.get("op1", OP.bypass))
        elif kind == "tss":
            ins = h.tensor_single_scalar(args["out"], args["in_"], args["s"], args["op"])
        elif kind == "stt":
            ins = h.scalar_tensor_tensor(args["out"], args["in0"], args["s"],
                                         args["in1"], args["op0"], args["op1"])
        elif kind == "red":
            ins = h.tensor_reduce(args["out"], args["in_"], axis=AX.X, op=args["op"],
                                  negate=args.get("negate", False))
        elif kind == "rec":
            ins = h.reciprocal(args["out"], args["in_"])
        elif kind == "bar":
            return
        else:
            raise ValueError(kind)
        if eng in self.CHAIN:
            ins.then_inc(self.sems[eng], 1)
        elif key is not None:
            ins.then_inc(self.sems[track], 16 if kind == "dma" else 1)

    def dma(self, eng, out, in_, key=None, waits=(), sem=None):
        assert key is not None and sem is not None
        self.op(eng, "dma", key=key, waits=waits, sem=sem, out=out, in_=in_)

    def mm(self, out, lhsT, rhs, start, stop, key=None, waits=(), tp=None):
        self.op("pe", "mm", key=key, waits=waits, out=out, lhsT=lhsT, rhs=rhs,
                start=start, stop=stop, tp=tp)

    def tp(self, out, in_, ident, key=None, waits=()):
        self.op("pe", "tp", key=key, waits=waits, out=out, in_=in_, ident=ident)


def build_nc():
    nc = bass.Bass("TRN2", target_bir_lowering=False)

    X2d = nc.dram_tensor("X2", [NCH, 128, NF * CH], F32, kind="ExternalInput").ap()
    W2q = nc.dram_tensor("W2q", [NMQ, 128, NF * 128], F32, kind="ExternalInput").ap()
    W2k = nc.dram_tensor("W2k", [NMK, 128, NF * 128], F32, kind="ExternalInput").ap()
    W2o = nc.dram_tensor("W2o", [NMQ, 128, NF * 128], F32, kind="ExternalInput").ap()
    W2m = nc.dram_tensor("W2m", [NPAN, 128, NF * 128], F32, kind="ExternalInput").ap()
    WvD = nc.dram_tensor("WvD", [NF, 128, KVH * 128], F32, kind="ExternalInput").ap()
    WrTD = nc.dram_tensor("WrTD", [128, NF * 4], F32, kind="ExternalInput").ap()
    ATD = nc.dram_tensor("ATD", [128, NF * 64], F32, kind="ExternalInput").ap()
    BsD = nc.dram_tensor("BsD", [64, DM], F32, kind="ExternalInput").ap()
    cb2D = nc.dram_tensor("cb2D", [128, 32], F32, kind="ExternalInput").ap()
    ConeD = nc.dram_tensor("ConeD", [128, 255], F32, kind="ExternalInput").ap()
    Id128D = nc.dram_tensor("Id128D", [128, 128], F32, kind="ExternalInput").ap()
    Id4D = nc.dram_tensor("Id4D", [4, 4], F32, kind="ExternalInput").ap()
    Id1D = nc.dram_tensor("Id1D", [1, 1], F32, kind="ExternalInput").ap()
    Ones128D = nc.dram_tensor("Ones128D", [128, 1], F32, kind="ExternalInput").ap()
    Ones1xD = nc.dram_tensor("Ones1xD", [1, 128], F32, kind="ExternalInput").ap()
    Sum8D = nc.dram_tensor("Sum8D", [128, 16], F32, kind="ExternalInput").ap()
    Exp16D = nc.dram_tensor("Exp16D", [16, 128], F32, kind="ExternalInput").ap()
    Exp464D = nc.dram_tensor("Exp464D", [4, 64], F32, kind="ExternalInput").ap()
    OUTd = nc.dram_tensor("OUT", [NCH, 128, NF * CH], F32, kind="ExternalOutput").ap()
    QH = nc.dram_tensor("QH", [NMQ, 128, NF * 128], BF16).ap()
    QL = nc.dram_tensor("QL", [NMQ, 128, NF * 128], BF16).ap()
    KH = nc.dram_tensor("KH", [NMK, 128, NF * 128], BF16).ap()
    KL = nc.dram_tensor("KL", [NMK, 128, NF * 128], BF16).ap()
    OH = nc.dram_tensor("OHs", [NMQ, 128, NF * 128], BF16).ap()
    OL = nc.dram_tensor("OLs", [NMQ, 128, NF * 128], BF16).ap()
    VH = nc.dram_tensor("VH", [NF, 128, KVH * 128], BF16).ap()
    VL = nc.dram_tensor("VL", [NF, 128, KVH * 128], BF16).ap()
    ATH = nc.dram_tensor("ATH", [128, NF * 64], BF16).ap()
    ATL = nc.dram_tensor("ATL", [128, NF * 64], BF16).ap()

    A = nc.alloc_sbuf_tensor
    xh = A("xh", [128, NF * CH], F32).ap()
    hbh = A("hbh", [128, NF * CH], BF16).ap()
    hbl = A("hbl", [128, NF * CH], BF16).ap()
    qb = A("qb", [128, NF * CH], F32).ap()
    x2b = A("x2b", [128, NF * CH], F32).ap()
    whiA = A("whiA", [128, NF * 128], BF16).ap()
    wloA = A("wloA", [128, NF * 128], BF16).ap()
    whiB = A("whiB", [128, NF * 128], BF16).ap()
    wloB = A("wloB", [128, NF * 128], BF16).ap()
    wvhiA = A("wvhiA", [128, KVH * 128], BF16).ap()
    wvloA = A("wvloA", [128, KVH * 128], BF16).ap()
    wvhiB = A("wvhiB", [128, KVH * 128], BF16).ap()
    wvloB = A("wvloB", [128, KVH * 128], BF16).ap()
    wrtb = A("wrtb", [128, NF * 4], F32).ap()
    mb0 = A("mb0", [128, CH], F32).ap()
    mb1 = A("mb1", [128, CH], F32).ap()
    e1b = A("e1b", [128, CH], F32).ap()
    e2b = A("e2b", [128, CH], F32).ap()
    ptok = A("ptok", [128, 2 * 256], F32).ap()
    sqb0 = A("sqb0", [128, CH], F32).ap()
    sqb1 = A("sqb1", [128, CH], F32).ap()
    avt0 = A("avt0", [128, 128], F32).ap()
    avt1 = A("avt1", [128, 128], F32).ap()
    stt_ = A("stats", [1, 13 * CH], F32).ap()
    recb = A("recb", [16, 2 * CH], F32).ap()
    modv = A("modv", [128, NPAN], F32).ap()
    modv2 = A("modv2", [128, 128], F32).ap()
    sct = A("sct", [128, 32], F32).ap()
    cbb = A("cbb", [128, 32], F32).ap()
    gat = A("gat", [128, 96], F32).ap()
    gtt = A("gtt", [4, 2 * CH], F32).ap()
    geb = A("geb", [64, CH], F32).ap()
    gmb = A("gmb", [64, CH], F32).ap()
    cone = A("cone", [128, 255], F32).ap()
    id128 = A("id128", [128, 128], F32).ap()
    id4 = A("id4", [4, 4], F32).ap()
    id1 = A("id1", [1, 1], F32).ap()
    ones128 = A("ones128", [128, 1], F32).ap()
    ones1x = A("ones1x", [1, 128], F32).ap()
    sum8 = A("sum8", [128, 16], F32).ap()
    exp16 = A("exp16", [16, 128], F32).ap()
    exp464 = A("exp464", [4, 64], F32).ap()

    ktb = x2b[:, 0:NMK * CH]
    vtb = x2b[:, NMK * CH:NMK * CH + 2048]
    mp0 = whiA.bitcast(F32)
    mp1 = wloA.bitcast(F32)
    spo0 = whiB
    spo1 = wloB
    bsv0 = whiB.bitcast(F32)[0:64, 0:2048]
    bsv1 = wloB.bitcast(F32)[0:64, 0:2048]
    pab = mb0
    pbb = mb1

    PS = [nc.alloc_psum_tensor(f"ps{i}", [128, 512], F32).ap() for i in range(8)]
    sems = {nm: nc.alloc_semaphore(f"sem_{nm}")
            for nm in ("pe", "dve", "act", "wa", "wb", "va", "vb", "x", "c", "st",
                       "sw")}

    def stv(i):
        return stt_[0:1, i * CH:(i + 1) * CH]

    def schedule(b: Builder):
        b.dma("syn", cone[:, :], ConeD[:, :], key="cst0", sem="c")
        b.dma("syn", id128[:, :], Id128D[:, :], key="cst1", sem="c")
        b.dma("syn", id4[:, :], Id4D[:, :], key="cst2", sem="c")
        b.dma("syn", id1[:, :], Id1D[:, :], key="cst3", sem="c")
        b.dma("syn", ones128[:, :], Ones128D[:, :], key="cst4", sem="c")
        b.dma("syn", ones1x[:, :], Ones1xD[:, :], key="cst5", sem="c")
        b.dma("syn", sum8[:, :], Sum8D[:, :], key="cst6", sem="c")
        b.dma("syn", exp16[:, :], Exp16D[:, :], key="cst7", sem="c")
        b.dma("syn", exp464[:, :], Exp464D[:, :], key="cst8", sem="c")
        b.dma("syn", cbb[:, :], cb2D[:, :], key="ld_c", sem="c")
        b.dma("syn", xh[:, :], X2d[0], key="ld_x_0", sem="x")
        b.op("act", "act", key="silu", waits=("ld_c",),
             out=sct[:, :], in_=cbb[:, :], func=AF.Silu)

        for n in range(NPAN):
            psm = PS[n % 2]
            for half in range(2):
                i2 = 2 * n + half
                sl = mp0 if i2 % 2 == 0 else mp1
                ww = [f"g0h_{i2-2}"] if i2 >= 2 else []
                b.dma("syn", sl[:, :], W2m[n][:, half * 2048:(half + 1) * 2048],
                      key=f"w0_{i2}", waits=ww, sem=("wa" if i2 % 2 == 0 else "wb"))
                mmw = [f"w0_{i2}", "silu"]
                if n >= 2 and half == 0:
                    mmw.append(f"c0_{n-2}")
                for kq in range(4):
                    for j in range(4):
                        kk = kq * 4 + j
                        k = half * 16 + kk
                        b.mm(psm[32 * j:32 * j + 1, 0:128], sct[:, k:k + 1],
                             sl[:, kk * 128:(kk + 1) * 128],
                             start=(half == 0 and kq == 0),
                             stop=(half == 1 and kq == 3),
                             key=(f"g0h_{i2}" if (kq == 3 and j == 3) else None),
                             waits=(mmw if (kq == 0 and j == 0) else ()),
                             tp=(0, 32 * j))
            tf = mb0[:, 0:128] if n % 2 == 0 else mb1[:, 0:128]
            tw = [f"g0h_{2 * n + 1}"]
            if n >= 2:
                tw.append(f"t0_{n-2}")
            for j in range(4):
                b.op("dve", "cp", key=(f"c0_{n}" if j == 3 else None),
                     waits=(tw if j == 0 else ()),
                     out=tf[32 * j:32 * j + 1, 0:128],
                     in_=psm[32 * j:32 * j + 1, 0:128])
            pst = PS[2 + (n % 2)]
            tpw = [f"c0_{n}"]
            if n >= 2:
                tpw.append(f"c1_{n-2}")
            b.tp(pst[0:128, 0:128], tf[:, :], id128[:, :], key=f"t0_{n}", waits=tpw)
            b.op("dve", "red", key=f"c1_{n}", waits=(f"t0_{n}",),
                 out=modv[:, n:n + 1], in_=pst[0:128, 0:128:32], op=OP.add)
        jobs = []
        for m in range(NMQ):
            for hf in range(2):
                jobs.append((f"spq_{m}_{hf}", W2q[m][:, hf * 2048:(hf + 1) * 2048],
                             QH[m][:, hf * 2048:(hf + 1) * 2048],
                             QL[m][:, hf * 2048:(hf + 1) * 2048], 2048))
        for m in range(NMK):
            for hf in range(2):
                jobs.append((f"spk_{m}_{hf}", W2k[m][:, hf * 2048:(hf + 1) * 2048],
                             KH[m][:, hf * 2048:(hf + 1) * 2048],
                             KL[m][:, hf * 2048:(hf + 1) * 2048], 2048))
        for f in range(NF):
            jobs.append((f"spv_{f}", WvD[f], VH[f], VL[f], 1024))
        for m in range(NMQ):
            for hf in range(2):
                jobs.append((f"spo_{m}_{hf}", W2o[m][:, hf * 2048:(hf + 1) * 2048],
                             OH[m][:, hf * 2048:(hf + 1) * 2048],
                             OL[m][:, hf * 2048:(hf + 1) * 2048], 2048))
        jobs.append(("spat", ATD[:, :], ATH[:, :], ATL[:, :], 2048))
        nJ = len(jobs)
        for i, (jk, srcap, hidst, lodst, w) in enumerate(jobs):
            sl = mp0 if i % 2 == 0 else mp1
            so = spo0 if i % 2 == 0 else spo1
            ww = []
            if i < 2:
                ww.append(f"g0h_{2 * NPAN - 2 + i}")
            else:
                ww += [f"spc_{i-2}", f"spl_{i-2}"]
            b.dma("syn", sl[:, 0:w], srcap, key=f"spld_{i}", waits=ww,
                  sem=("wa" if i % 2 == 0 else "wb"))
            cw = [f"spld_{i}"]
            if i >= 2:
                cw.append(f"spst_{i-2}")
            b.op("act", "scp", key=f"spc_{i}", waits=cw,
                 out=so[:, 0:w], in_=sl[:, 0:w])
            b.op("dve", "tt", key=f"spl_{i}", waits=(f"spc_{i}",),
                 out=so[:, w:2 * w], in0=sl[:, 0:w], in1=so[:, 0:w],
                 op=OP.subtract)
            b.dma("gp", hidst, so[:, 0:w], key=f"spsh_{i}", waits=(f"spl_{i}",),
                  sem="sw")
            b.dma("gp", lodst, so[:, w:2 * w], key=f"spst_{i}", sem="sw")
        b.op("dve", "ts", waits=(f"c1_{NPAN-1}",), out=modv2[:, 0:32],
             in0=modv[:, 32:64], s1=1.0, op0=OP.add)
        b.op("dve", "ts", out=modv2[:, 64:96], in0=modv[:, 128:160], s1=1.0, op0=OP.add)
        for (src0, dst0) in ((64, 32), (160, 96)):
            b.op("dve", "tss", out=gat[:, 0:32], in_=modv[:, src0:src0 + 32], s=0.0,
                 op=OP.is_ge)
            b.op("dve", "ts", out=gat[:, 32:64], in0=gat[:, 0:32], s1=2.0, s2=-1.0,
                 op0=OP.mult, op1=OP.add)
            b.op("dve", "tt", out=gat[:, 64:96], in0=modv[:, src0:src0 + 32],
                 in1=gat[:, 32:64], op=OP.mult)
            b.op("dve", "tss", out=gat[:, 0:32], in_=gat[:, 64:96], s=0.25,
                 op=OP.max)
            b.op("dve", "tt", out=modv2[:, dst0:dst0 + 32], in0=gat[:, 0:32],
                 in1=gat[:, 32:64], op=OP.mult)
        b.op("dve", "ts", key="modv2", out=modv2[:, 96:128], in0=modv2[:, 96:128],
             s1=SCALE_FFN, op0=OP.mult)

        sa_col = modv[:, 0:32]
        sb_col = modv[:, 96:128]
        caP = modv2[:, 0:32]
        gaC = modv2[:, 32:64]
        cbP = modv2[:, 64:96]
        gbC4 = modv2[:, 96:128]

        for t in range(NCH):
            stats_w = [f"ld_x_{t}"]
            if t > 0:
                stats_w.append(f"varep2_{t-1}")
            else:
                stats_w.append(f"c0_{NPAN-1}")
            for f in range(NF):
                xt = xh[:, f * CH:(f + 1) * CH]
                sq = sqb0 if f % 2 == 0 else sqb1
                sqw = [f"ld_x_{t}"]
                if f >= 2:
                    sqw.append(f"sqmm_{t}_{f-2}")
                elif t > 0:
                    sqw.append(f"stats2_{t-1}")
                    sqw.append(f"h2_{t-1}")
                b.op("act", "sq", key=f"sq_{t}_{f}", waits=sqw, out=sq[:, :], in_=xt)
                b.mm(PS[0][0:1, 0:CH], ones128[:, 0:1], xt,
                     start=(f == 0), stop=(f == NF - 1),
                     waits=(stats_w if f == 0 else ()))
                b.mm(PS[1][0:1, 0:CH], ones128[:, 0:1], sq[:, :],
                     start=(f == 0), stop=(f == NF - 1),
                     key=(f"stats_{t}" if f == NF - 1 else f"sqmm_{t}_{f}"),
                     waits=(f"sq_{t}_{f}",))
            def rstd_chain(tag, statskey):
                b.op("dve", "ts", waits=(statskey,), out=stv(0), in0=PS[0][0:1, 0:CH],
                     s1=1.0 / DM, op0=OP.mult)
                b.op("dve", "ts", out=stv(1), in0=PS[1][0:1, 0:CH], s1=1.0 / DM,
                     op0=OP.mult)
                b.op("dve", "ts", out=stv(2), in0=stv(0), s1=-1.0, op0=OP.mult)
                b.op("dve", "tt", out=stv(3), in0=stv(0), in1=stv(0), op=OP.mult)
                b.op("dve", "tt", out=stv(4), in0=stv(1), in1=stv(3), op=OP.subtract)
                b.op("dve", "ts", key=f"varep{tag}", out=stv(5), in0=stv(4), s1=EPS,
                     op0=OP.add)
                b.op("act", "act", waits=(f"varep{tag}",), out=stv(6), in_=stv(5),
                     func=AF.Ln)
                b.op("act", "act", key=f"rstd0{tag}", out=stv(7), in_=stv(6),
                     func=AF.Exp, scale=-0.5)
                src = 7
                for it in range(2):
                    b.op("dve", "tt", waits=((f"rstd0{tag}",) if it == 0 else ()),
                         out=stv(8), in0=stv(src), in1=stv(src), op=OP.mult)
                    b.op("dve", "tt", out=stv(9), in0=stv(8), in1=stv(5), op=OP.mult)
                    b.op("dve", "ts", out=stv(10), in0=stv(9), s1=-0.5, s2=1.5,
                         op0=OP.mult, op1=OP.add)
                    dst = 11 if it == 0 else 12
                    b.op("dve", "tt", key=(f"rstd{tag}" if it == 1 else None),
                         out=stv(dst), in0=stv(src), in1=stv(10), op=OP.mult)
                    src = dst
                return src

            src = rstd_chain(f"_{t}", f"stats_{t}")
            bc_w = [f"rstd_{t}", "modv2" if t == 0 else f"gebc_{t-1}"]
            b.mm(PS[2][:, 0:CH], ones1x[0:1, :], stv(2), start=True, stop=True,
                 waits=bc_w)
            b.mm(PS[3][:, 0:CH], ones1x[0:1, :], stv(src), start=True, stop=True,
                 key=f"bcast_{t}")
            hw = ["modv2", f"bcast_{t}"]
            if t == 0:
                hw.append("spst_0")
            else:
                hw.append(f"pemid_{t-1}")
            for f in range(NF):
                xt = xh[:, f * CH:(f + 1) * CH]
                hh = hbh[:, f * CH:(f + 1) * CH]
                hl = hbl[:, f * CH:(f + 1) * CH]
                b.op("dve", "tt", waits=(hw if f == 0 else ()), out=sqb1[:, :], in0=xt,
                     in1=PS[2][:, 0:CH], op=OP.add)
                b.op("dve", "tt", out=sqb0[:, :],
                     in0=sqb1[:, :], in1=PS[3][:, 0:CH], op=OP.mult)
                b.op("dve", "ts", key=f"hf_{t}_{f}", out=sqb1[:, :],
                     in0=sqb0[:, :], s1=caP[:, f:f + 1], s2=sa_col[:, f:f + 1],
                     op0=OP.mult, op1=OP.add)
                b.op("act", "scp", key=f"hh_{t}_{f}", waits=(f"hf_{t}_{f}",),
                     out=hh, in_=sqb1[:, :])
                b.op("dve", "tt", key=(f"h_{t}" if f == NF - 1 else None),
                     waits=(f"hh_{t}_{f}",), out=hl, in0=sqb1[:, :], in1=hh,
                     op=OP.subtract)
            for (npan, WHs, WLs, dst, tag) in ((NMQ, QH, QL, qb, "q"),
                                               (NMK, KH, KL, ktb, "k")):
                for m in range(npan):
                    whi = whiA if m % 2 == 0 else whiB
                    wlo = wloA if m % 2 == 0 else wloB
                    wkey = f"w{tag}_{t}_{m}"
                    if tag == "q":
                        if m >= 2:
                            ww = [f"peq_{t}_{m-2}"]
                        elif t == 0:
                            ww = [f"spl_{nJ-1}", f"spst_{nJ-1}"]
                        else:
                            ww = [f"peo_{t-1}_{NMQ-2}" if m == 0
                                  else f"ped_{t-1}_{NMQ-1}"]
                    else:
                        ww = [f"peq_{t}_{NMQ-2+m}"] if m < 2 else [f"pek_{t}_{m-2}"]
                    b.dma("syn", whi[:, :], WHs[m], key=wkey + "h", waits=ww,
                          sem=("wa" if m % 2 == 0 else "wb"))
                    b.dma("syn", wlo[:, :], WLs[m], key=wkey,
                          sem=("wa" if m % 2 == 0 else "wb"))
                    psq = PS[4 + m % 2]
                    mw = [wkey, f"h_{t}"]
                    if tag == "q":
                        if m >= 2:
                            mw.append(f"evq_{t}_{m-2}")
                        elif t > 0:
                            mw.append(f"gmid_{t-1}")
                            mw.append(f"out_{t-1}_{NMQ-1}")
                            mw.append(f"st_{t-1}")
                    else:
                        mw.append(f"evq_{t}_{NMQ-2+m}" if m < 2 else f"evk_{t}_{m-2}")
                        if m < 2:
                            mw.append(f"out_{t-1}_{NMQ-1}" if t > 0 else "modv2")
                    for f in range(NF):
                        hh = hbh[:, f * CH:(f + 1) * CH]
                        hl = hbl[:, f * CH:(f + 1) * CH]
                        wh = whi[:, f * 128:(f + 1) * 128]
                        wl = wlo[:, f * 128:(f + 1) * 128]
                        for pi, (lh, rr) in enumerate(((wh, hh), (wh, hl), (wl, hh))):
                            b.mm(psq[:, 0:CH], lh, rr,
                                 start=(f == 0 and pi == 0),
                                 stop=(f == NF - 1 and pi == 2),
                                 key=(f"pe{tag}_{t}_{m}"
                                      if (f == NF - 1 and pi == 2) else None),
                                 waits=(mw if (f == 0 and pi == 0) else ()))
                    b.op("act", "scp", key=f"ev{tag}_{t}_{m}",
                         waits=(f"pe{tag}_{t}_{m}",),
                         out=dst[:, m * CH:(m + 1) * CH], in_=psq[:, 0:CH])
            for f in range(NF):
                wvh = wvhiA if f % 2 == 0 else wvhiB
                wvl = wvloA if f % 2 == 0 else wvloB
                vw = [f"pev_{t}_{f-2}"] if f >= 2 else []
                if f < 2 and t > 0:
                    vw.append(f"pemid_{t-1}")
                b.dma("syn", wvh[:, :], VH[f], key=f"wv_{t}_{f}h", waits=vw,
                      sem=("va" if f % 2 == 0 else "vb"))
                b.dma("syn", wvl[:, :], VL[f], key=f"wv_{t}_{f}",
                      sem=("va" if f % 2 == 0 else "vb"))
                mw0 = [f"wv_{t}_{f}"]
                if f == 0:
                    mw0.append(f"evk_{t}_{NMK-1}")
                    if t > 0:
                        mw0.append(f"out_{t-1}_{NMQ-1}")
                        mw0.append(f"av_{t-1}_1")
                for s2 in range(2):
                    lhh = hbh[:, f * CH + s2 * 128: f * CH + (s2 + 1) * 128]
                    lhl = hbl[:, f * CH + s2 * 128: f * CH + (s2 + 1) * 128]
                    for half in range(2):
                        for pi, (lh, rr) in enumerate(
                                ((lhh, wvh), (lhl, wvh), (lhh, wvl))):
                            b.mm(PS[4 + 2 * s2 + half][:, 0:512], lh,
                                 rr[:, half * 512:(half + 1) * 512],
                                 start=(f == 0 and pi == 0),
                                 stop=(f == NF - 1 and pi == 2),
                                 key=(f"pev_{t}_{f}"
                                      if (s2 == 1 and half == 1 and pi == 2) else None),
                                 waits=(mw0 if (s2 == 0 and half == 0 and pi == 0)
                                        else ()))
            for i in range(4):
                s2, half = divmod(i, 2)
                b.op("act", "scp", key=f"evv_{t}_{i}",
                     waits=((f"pev_{t}_{NF-1}",) if i == 0 else ()),
                     out=vtb[:, s2 * 1024 + half * 512: s2 * 1024 + (half + 1) * 512],
                     in_=PS[4 + i][:, 0:512])
            pair = 0
            for h in range(HQ):
                for j in range(KVH):
                    mbuf = mb0 if pair % 2 == 0 else mb1
                    gw = [f"evq_{t}_{h}", f"evk_{t}_{j}"]
                    if pair >= 2:
                        gw.append(f"gmm_{t}_{pair-2}")
                    b.op("dve", "tt", key=f"gm_{t}_{pair}", waits=gw, out=mbuf[:, :],
                         in0=qb[:, h * CH:(h + 1) * CH],
                         in1=ktb[:, j * CH:(j + 1) * CH], op=OP.mult)
                    tile = h // 16
                    r = (h % 16) * 8 + j
                    first = (h % 16 == 0 and j == 0)
                    last = (h % 16 == 15 and j == 7)
                    mmw = [f"gm_{t}_{pair}"]
                    if first:
                        mmw.append(f"varep_{t}")
                    b.mm(PS[tile][:, 0:CH], cone[:, 127 - r:255 - r], mbuf[:, :],
                         start=first, stop=last, key=f"gmm_{t}_{pair}", waits=mmw)
                    pair += 1
            b.op("act", "act", waits=(f"gmm_{t}_{128 * 1 - 1}",), out=e1b[:, :],
                 in_=PS[0][:, 0:CH], func=AF.Exp)
            b.op("act", "act", key=f"exp_{t}", waits=(f"gmm_{t}_{255}",),
                 out=e2b[:, :], in_=PS[1][:, 0:CH], func=AF.Exp)
            b.mm(PS[2][0:16, 0:CH], sum8[:, :], e1b[:, :], start=True, stop=True,
                 waits=[f"exp_{t}", f"h_{t}"])
            b.mm(PS[3][0:16, 0:CH], sum8[:, :], e2b[:, :], start=True, stop=True,
                 key=f"gs_{t}")
            b.op("dve", "rec", waits=(f"gs_{t}",), out=recb[0:16, 0:CH],
                 in_=PS[2][0:16, 0:CH])
            b.op("dve", "rec", key=f"rec_{t}", out=recb[0:16, CH:2 * CH],
                 in_=PS[3][0:16, 0:CH])
            b.mm(PS[4][:, 0:CH], exp16[:, :], recb[0:16, 0:CH], start=True, stop=True,
                 waits=[f"rec_{t}", f"evv_{t}_3"])
            b.mm(PS[5][:, 0:CH], exp16[:, :], recb[0:16, CH:2 * CH], start=True,
                 stop=True, key=f"pexp_{t}")
            b.op("dve", "tt", waits=(f"pexp_{t}",), out=pab[:, :], in0=e1b[:, :],
                 in1=PS[4][:, 0:CH], op=OP.mult)
            b.op("dve", "tt", key=f"pnorm_{t}", out=pbb[:, :], in0=e2b[:, :],
                 in1=PS[5][:, 0:CH], op=OP.mult)
            for s2 in range(2):
                tpa = [f"pnorm_{t}"] if s2 == 0 else [f"ptkb_{t}_0"]
                if s2 == 0:
                    tpa.append(f"evv_{t}_3")
                b.tp(PS[6][:, s2 * 128:(s2 + 1) * 128], pab[:, s2 * 128:(s2 + 1) * 128],
                     id128[:, :], waits=tpa)
                b.tp(PS[7][:, s2 * 128:(s2 + 1) * 128], pbb[:, s2 * 128:(s2 + 1) * 128],
                     id128[:, :], key=f"ptp_{t}_{s2}")
                b.op("dve", "cp", key=f"ptka_{t}_{s2}", waits=(f"ptp_{t}_{s2}",),
                     out=ptok[:, s2 * 256:s2 * 256 + 128],
                     in_=PS[6][:, s2 * 128:(s2 + 1) * 128])
                b.op("dve", "cp", key=f"ptkb_{t}_{s2}",
                     out=ptok[:, s2 * 256 + 128:(s2 + 1) * 256],
                     in_=PS[7][:, s2 * 128:(s2 + 1) * 128])
            for s2 in range(2):
                for h in range(HQ):
                    pcol0 = s2 * 256 + (0 if h < 16 else 128) + (h % 16) * 8
                    avw = [f"ptkb_{t}_1"] if (s2 == 0 and h == 0) else ()
                    b.op("dve", "ts", waits=avw, out=avt0[:, :],
                         in0=vtb[:, s2 * 1024:s2 * 1024 + 128],
                         s1=ptok[:, pcol0:pcol0 + 1], op0=OP.mult)
                    cur = avt0
                    for j in range(1, 8):
                        if j == 7:
                            dsta = qb[:, h * CH + s2 * 128: h * CH + (s2 + 1) * 128]
                        else:
                            dsta = avt1[:, :] if j % 2 == 1 else avt0[:, :]
                        b.op("dve", "stt",
                             key=(f"av_{t}_{s2}" if (j == 7 and h == HQ - 1) else None),
                             out=dsta,
                             in0=vtb[:, s2 * 1024 + j * 128:s2 * 1024 + (j + 1) * 128],
                             s=ptok[:, pcol0 + j:pcol0 + j + 1],
                             in1=cur[:, :], op0=OP.mult, op1=OP.add)
                        cur = avt1 if j % 2 == 1 else avt0
            grp = 0
            for s2 in range(2):
                for hh in range(HQ):
                    g, q4 = divmod(grp, 4)
                    psb = PS[4 + (g % 2)]
                    tw = []
                    if q4 == 0:
                        if g == 0:
                            tw = [f"av_{t}_0", f"pnorm_{t}"]
                        elif g == 8:
                            tw = [f"av_{t}_1"]
                        if g >= 2:
                            tw.append(f"avev_{t}_{g-2}")
                    b.tp(psb[:, q4 * 128:(q4 + 1) * 128],
                         qb[:, hh * CH + s2 * 128: hh * CH + (s2 + 1) * 128],
                         id128[:, :], key=(f"avtp_{t}_{g}" if q4 == 3 else None),
                         waits=tw)
                    if q4 == 3:
                        for u in range(4):
                            gi = g * 4 + u
                            ss2, hh2 = gi // HQ, gi % HQ
                            c0 = hh2 * CH + ss2 * 128
                            b.op("act", "scp",
                                 key=(f"avevh_{t}_{g}" if u == 3 else None),
                                 waits=((f"avtp_{t}_{g}",) if u == 0 else ()),
                                 out=hbh[:, c0:c0 + 128],
                                 in_=psb[:, u * 128:(u + 1) * 128])
                        for u in range(4):
                            gi = g * 4 + u
                            ss2, hh2 = gi // HQ, gi % HQ
                            c0 = hh2 * CH + ss2 * 128
                            b.op("dve", "tt",
                                 key=(f"avev_{t}_{g}" if u == 3 else None),
                                 waits=((f"avevh_{t}_{g}",) if u == 0 else ()),
                                 out=hbl[:, c0:c0 + 128],
                                 in0=psb[:, u * 128:(u + 1) * 128],
                                 in1=hbh[:, c0:c0 + 128], op=OP.subtract)
                    grp += 1
            nG = grp // 4
            for m in range(NMQ):
                whi = whiA if m % 2 == 0 else whiB
                wlo = wloA if m % 2 == 0 else wloB
                ww = [f"pek_{t}_{NMK-2+m}"] if m < 2 else [f"peo_{t}_{m-2}"]
                b.dma("syn", whi[:, :], OH[m], key=f"wo_{t}_{m}h", waits=ww,
                      sem=("wa" if m % 2 == 0 else "wb"))
                b.dma("syn", wlo[:, :], OL[m], key=f"wo_{t}_{m}",
                      sem=("wa" if m % 2 == 0 else "wb"))
                pso = PS[6 + m % 2]
                mw = [f"wo_{t}_{m}"]
                if m < 2:
                    mw += [f"avev_{t}_{nG-1}", f"ptkb_{t}_1"]
                else:
                    mw.append(f"x2_{t}_{m-2}")
                for f in range(NF):
                    aoh = hbh[:, f * CH:(f + 1) * CH]
                    aol = hbl[:, f * CH:(f + 1) * CH]
                    wh = whi[:, f * 128:(f + 1) * 128]
                    wl = wlo[:, f * 128:(f + 1) * 128]
                    for pi, (lh, rr) in enumerate(((wh, aoh), (wh, aol), (wl, aoh))):
                        b.mm(pso[:, 0:CH], lh, rr,
                             start=(f == 0 and pi == 0),
                             stop=(f == NF - 1 and pi == 2),
                             key=(f"peo_{t}_{m}" if (f == NF - 1 and pi == 2) else None),
                             waits=(mw if (f == 0 and pi == 0) else ()))
                x2w = [f"peo_{t}_{m}"]
                if m < 2:
                    x2w.append(f"gmm_{t}_{255}")
                b.op("dve", "stt", key=f"x2_{t}_{m}", waits=x2w,
                     out=x2b[:, m * CH:(m + 1) * CH], in0=pso[:, 0:CH],
                     s=gaC[:, m:m + 1], in1=xh[:, m * CH:(m + 1) * CH],
                     op0=OP.mult, op1=OP.add)
                sq = sqb0 if m % 2 == 0 else sqb1
                sqw = [f"x2_{t}_{m}"]
                if m >= 2:
                    sqw.append(f"sqmm2_{t}_{m-2}")
                b.op("act", "sq", key=f"sq2_{t}_{m}", waits=sqw, out=sq[:, :],
                     in_=x2b[:, m * CH:(m + 1) * CH])
                st_w = [f"x2_{t}_{m}"]
                if m == 0:
                    st_w.append(f"exp_{t}")
                b.mm(PS[0][0:1, 0:CH], ones128[:, 0:1], x2b[:, m * CH:(m + 1) * CH],
                     start=(m == 0), stop=(m == NMQ - 1), waits=st_w)
                b.mm(PS[1][0:1, 0:CH], ones128[:, 0:1], sq[:, :],
                     start=(m == 0), stop=(m == NMQ - 1),
                     key=(f"stats2_{t}" if m == NMQ - 1 else f"sqmm2_{t}_{m}"),
                     waits=(f"sq2_{t}_{m}",))
            src = rstd_chain(f"2_{t}", f"stats2_{t}")
            b.mm(PS[2][:, 0:CH], ones1x[0:1, :], stv(2), start=True, stop=True,
                 waits=[f"rstd2_{t}", f"rec_{t}"])
            b.mm(PS[3][:, 0:CH], ones1x[0:1, :], stv(src), start=True, stop=True,
                 key=f"bcast2_{t}")
            for f in range(NF):
                x2t = x2b[:, f * CH:(f + 1) * CH]
                ht = xh[:, f * CH:(f + 1) * CH]
                b.op("dve", "tt", waits=((f"bcast2_{t}",) if f == 0 else ()),
                     out=sqb0[:, :], in0=x2t, in1=PS[2][:, 0:CH], op=OP.add)
                b.op("dve", "tt", out=sqb1[:, :], in0=sqb0[:, :], in1=PS[3][:, 0:CH],
                     op=OP.mult)
                b.op("dve", "ts", key=f"h2f_{t}_{f}", out=ht,
                     in0=sqb1[:, :], s1=cbP[:, f:f + 1], s2=sb_col[:, f:f + 1],
                     op0=OP.mult, op1=OP.add)
                b.op("act", "scp", key=f"h2h_{t}_{f}", waits=(f"h2f_{t}_{f}",),
                     out=hbh[:, f * CH:(f + 1) * CH], in_=ht)
                b.op("dve", "tt", key=(f"h2_{t}" if f == NF - 1 else None),
                     waits=(f"h2h_{t}_{f}",),
                     out=hbl[:, f * CH:(f + 1) * CH], in0=ht,
                     in1=hbh[:, f * CH:(f + 1) * CH], op=OP.subtract)
            atw = [f"pev_{t}_{NF-1}"]
            b.dma("syn", wvhiA[:, :], ATH[:, 0:1024], key=f"at0_{t}", waits=atw,
                  sem="va")
            b.dma("syn", wvloA[:, :], ATH[:, 1024:2048], key=f"at1_{t}", sem="va")
            b.dma("syn", wvhiB[:, :], ATL[:, 0:1024], key=f"at2_{t}", waits=atw,
                  sem="vb")
            b.dma("syn", wvloB[:, :], ATL[:, 1024:2048], key=f"at_{t}", sem="vb")
            if t == 0:
                b.dma("syn", wrtb[:, :], WrTD[:, :], key="wrt_0", sem="c")
            bsw = [f"peo_{t}_{NMQ-1}"]
            b.dma("syn", bsv0[:, :], BsD[:, 0:2048], key=f"bs0_{t}", waits=bsw,
                  sem="wa")
            b.dma("syn", bsv1[:, :], BsD[:, 2048:4096], key=f"bs_{t}",
                  waits=bsw, sem="wb")
            mw = [f"h2_{t}", f"at1_{t}", f"at_{t}", f"avev_{t}_{nG-1}",
                  f"pnorm_{t}"]
            if t == 0:
                mw.append("wrt_0")
            for f in range(NF):
                ath = (wvhiA if f < 16 else wvloA)[:, (f % 16) * 64:(f % 16 + 1) * 64]
                atl = (wvhiB if f < 16 else wvloB)[:, (f % 16) * 64:(f % 16 + 1) * 64]
                h2h = hbh[:, f * CH:(f + 1) * CH]
                h2l = hbl[:, f * CH:(f + 1) * CH]
                for pi, (lh, rr) in enumerate(((ath, h2h), (ath, h2l), (atl, h2h))):
                    b.mm(PS[4][0:64, 0:CH], lh, rr,
                         start=(f == 0 and pi == 0), stop=(f == NF - 1 and pi == 2),
                         key=(f"pemid_{t}" if (f == NF - 1 and pi == 2) else None),
                         waits=(mw if (f == 0 and pi == 0) else ()))
                b.mm(PS[5][0:4, 0:CH], wrtb[:, f * 4:(f + 1) * 4],
                     xh[:, f * CH:(f + 1) * CH], start=(f == 0), stop=(f == NF - 1),
                     key=(f"pelg_{t}" if f == NF - 1 else None))
            b.op("dve", "cp", key=f"lsb_{t}", waits=(f"pelg_{t}",),
                 out=gtt[0:4, 0:CH], in_=PS[5][0:4, 0:CH])
            for s2 in range(2):
                tw = [f"lsb_{t}"] if s2 == 0 else [f"gtt_{t}_0"]
                b.tp(PS[5][:, 256 + 4 * s2:256 + 4 * (s2 + 1)],
                     gtt[0:4, s2 * 128:(s2 + 1) * 128], id4[0:4, 0:4],
                     key=f"lgt_{t}_{s2}", waits=tw)
                L = gat[:, 0:4]
                GT = gat[:, 4:8]
                RK = gat[:, 8:12]
                MS = gat[:, 12:16]
                E4 = gat[:, 16:20]
                EM = gat[:, 20:24]
                M4 = gat[:, 24:25]
                S4 = gat[:, 25:26]
                R4 = gat[:, 26:27]
                GS = gat[:, 28 + 4 * s2:32 + 4 * s2]
                b.op("dve", "cp", waits=(f"lgt_{t}_{s2}",), out=L,
                     in_=PS[5][:, 256 + 4 * s2:256 + 4 * (s2 + 1)])
                for e in range(4):
                    b.op("dve", "tss", out=GT, in_=L, s=L[:, e:e + 1], op=OP.is_gt)
                    b.op("dve", "red", out=RK[:, e:e + 1], in_=GT, op=OP.add)
                    if e > 0:
                        b.op("dve", "tss", out=GT[:, 0:e], in_=L[:, 0:e],
                             s=L[:, e:e + 1], op=OP.is_equal)
                        b.op("dve", "red", out=M4, in_=GT[:, 0:e], op=OP.add)
                        b.op("dve", "tt", out=RK[:, e:e + 1], in0=RK[:, e:e + 1],
                             in1=M4, op=OP.add)
                b.op("dve", "tss", out=MS, in_=RK, s=1.5, op=OP.is_lt)
                b.op("dve", "red", key=f"m4_{t}_{s2}", out=M4, in_=L, op=OP.max,
                     negate=True)
                b.op("act", "act", key=f"e4_{t}_{s2}", waits=(f"m4_{t}_{s2}",),
                     out=E4, in_=L, func=AF.Exp, bias=M4)
                b.op("dve", "tt", waits=(f"e4_{t}_{s2}",), out=EM, in0=E4, in1=MS,
                     op=OP.mult)
                b.op("dve", "red", out=S4, in_=EM, op=OP.add)
                b.op("dve", "rec", out=R4, in_=S4)
                b.op("dve", "ts", key=f"gates_{t}_{s2}", out=GS, in0=EM,
                     s1=R4[:, 0:1], op0=OP.mult)
                b.tp(PS[5][0:4, s2 * 128:(s2 + 1) * 128], GS, id128[:, :],
                     key=f"gt_{t}_{s2}", waits=(f"gates_{t}_{s2}",))
                b.op("dve", "cp", key=f"gtt_{t}_{s2}", waits=(f"gt_{t}_{s2}",),
                     out=gtt[0:4, CH + s2 * 128:CH + (s2 + 1) * 128],
                     in_=PS[5][0:4, s2 * 128:(s2 + 1) * 128])
            b.mm(PS[2][0:64, 256:256 + CH], exp464[0:4, :], gtt[0:4, CH:2 * CH],
                 start=True, stop=True, key=f"ge_{t}",
                 waits=[f"gtt_{t}_1", f"h2_{t}"])
            b.op("dve", "cp", key=f"gebc_{t}", waits=(f"ge_{t}",), out=geb[:, :],
                 in_=PS[2][0:64, 256:256 + CH])
            b.op("dve", "tt", key=f"gmid_{t}", waits=(f"pemid_{t}",), out=gmb[:, :],
                 in0=PS[4][0:64, 0:CH], in1=geb[:, :], op=OP.mult)
            for m in range(NMQ):
                psd = PS[6 + m % 2]
                bsl = bsv0 if m < 16 else bsv1
                mw = [f"gmid_{t}", f"bs0_{t}" if m < 16 else f"bs_{t}"]
                mw.append(f"x2_{t}_{NMQ-1}" if m < 2 else f"out_{t}_{m-2}")
                b.mm(psd[:, 0:CH], bsl[:, (m % 16) * 128:(m % 16 + 1) * 128],
                     gmb[:, :], start=True, stop=True, key=f"ped_{t}_{m}", waits=mw)
                ow = [f"ped_{t}_{m}"]
                if m == 0 and t > 0:
                    ow.append(f"st_{t-1}")
                b.op("dve", "stt", key=f"out_{t}_{m}", waits=ow,
                     out=qb[:, m * CH:(m + 1) * CH], in0=psd[:, 0:CH],
                     s=gbC4[:, m:m + 1], in1=x2b[:, m * CH:(m + 1) * CH],
                     op0=OP.mult, op1=OP.add)
            b.dma("gp", OUTd[t], qb[:, :], key=f"st_{t}", waits=(f"out_{t}_{NMQ-1}",), sem="st")
            if t + 1 < NCH:
                b.dma("syn", xh[:, :], X2d[t + 1], key=f"ld_x_{t+1}",
                      waits=(f"pelg_{t}", f"h2_{t}"), sem="x")

    cb = Builder("count")
    schedule(cb)
    events = cb.events

    with nc.Block() as block:

        @block.sync
        def _(sync):
            schedule(Builder("syn", events, {"syn": sync}, sems))

        @block.tensor
        def _(te):
            schedule(Builder("pe", events, {"pe": te}, sems))

        @block.vector
        def _(ve):
            schedule(Builder("dve", events, {"dve": ve}, sems))

        @block.scalar
        def _(sc):
            schedule(Builder("act", events, {"act": sc}, sems))

        @block.gpsimd
        def _(gp):
            schedule(Builder("gp", events, {"gp": gp}, sems))
            gp.wait_ge(sems["st"], events[f"st_{NCH-1}"][1])

    return nc


def _consts():
    cone = np.zeros((128, 255), dtype=np.float32)
    cone[:, 127] = np.float32(RSCALE)
    sum8 = np.zeros((128, 16), dtype=np.float32)
    for p in range(128):
        sum8[p, p // 8] = 1.0
    exp16 = np.zeros((16, 128), dtype=np.float32)
    for p in range(128):
        exp16[p // 8, p] = 1.0
    exp464 = np.zeros((4, 64), dtype=np.float32)
    for e in range(4):
        exp464[e, e * 16:(e + 1) * 16] = 1.0
    return dict(ConeD=cone, Id128D=np.eye(128, dtype=np.float32),
                Id4D=np.eye(4, dtype=np.float32), Id1D=np.eye(1, dtype=np.float32),
                Ones128D=np.ones((128, 1), dtype=np.float32),
                Ones1xD=np.ones((1, 128), dtype=np.float32),
                Sum8D=sum8, Exp16D=exp16, Exp464D=exp464)


def _stage(x, Wq, Wk, Wv, Wo, Af, Bf, Wr, c, wmod):
    W2q = np.ascontiguousarray(
        Wq.reshape(32, 128, 32, 128).transpose(2, 1, 0, 3)).reshape(32, 128, 32 * 128)
    W2k = np.ascontiguousarray(
        Wk.reshape(32, 128, 8, 128).transpose(2, 1, 0, 3)).reshape(8, 128, 32 * 128)
    W2o = np.ascontiguousarray(
        Wo.reshape(32, 128, 32, 128).transpose(2, 1, 0, 3)).reshape(32, 128, 32 * 128)
    W2m = np.ascontiguousarray(
        wmod.reshape(32, 128, 192, 128).transpose(2, 1, 0, 3)).reshape(192, 128, 32 * 128)
    WvD = np.ascontiguousarray(Wv.reshape(32, 128, 1024))
    WrTD = np.ascontiguousarray(
        Wr.reshape(4, 32, 128).transpose(2, 1, 0)).reshape(128, 128)
    ATD = np.ascontiguousarray(
        Af.transpose(2, 0, 1).reshape(32, 128, 64).transpose(1, 0, 2)).reshape(128, 2048)
    BsD = np.ascontiguousarray(Bf.transpose(0, 2, 1).reshape(64, 4096))
    return W2q, W2k, W2o, W2m, WvD, WrTD, ATD, BsD


def kernel(L0_attn_hidden_states, L0_attn_Wq, L0_attn_Wk, L0_attn_Wv, L0_attn_Wo,
           L0_ffn_A, L0_ffn_B, L0_ffn_Wr, L0_c, L0_wmod):
    x = np.ascontiguousarray(L0_attn_hidden_states, dtype=np.float32)
    Wq = np.ascontiguousarray(L0_attn_Wq, dtype=np.float32)
    Wk = np.ascontiguousarray(L0_attn_Wk, dtype=np.float32)
    Wv = np.ascontiguousarray(L0_attn_Wv, dtype=np.float32)
    Wo = np.ascontiguousarray(L0_attn_Wo, dtype=np.float32)
    Af = np.ascontiguousarray(L0_ffn_A, dtype=np.float32)
    Bf = np.ascontiguousarray(L0_ffn_B, dtype=np.float32)
    Wr = np.ascontiguousarray(L0_ffn_Wr, dtype=np.float32)
    c = np.ascontiguousarray(L0_c, dtype=np.float32)
    wmod = np.ascontiguousarray(L0_wmod, dtype=np.float32)

    W2q, W2k, W2o, W2m, WvD, WrTD, ATD, BsD = _stage(x, Wq, Wk, Wv, Wo, Af, Bf, Wr,
                                                     c, wmod)
    consts = _consts()
    nc = build_nc()
    in_maps = []
    for core in range(8):
        bsel, half = core // 2, core % 2
        xs = x[bsel, half * NTOK:(half + 1) * NTOK, :]
        X2 = np.ascontiguousarray(
            xs.reshape(NCH, CH, 32, 128).transpose(0, 3, 2, 1)).reshape(NCH, 128,
                                                                        32 * CH)
        cb2 = np.ascontiguousarray(c[bsel].reshape(32, 128).T)
        in_maps.append(dict(X2=X2, W2q=W2q, W2k=W2k, W2o=W2o, W2m=W2m, WvD=WvD,
                            WrTD=WrTD, ATD=ATD, BsD=BsD, cb2D=cb2, **consts))

    res = run_bass_kernel_spmd(nc, in_maps, core_ids=list(range(8)))
    out = np.empty((B, S, DM), dtype=np.float32)
    for core in range(8):
        bsel, half = core // 2, core % 2
        o = np.asarray(res.results[core]["OUT"]).reshape(NCH, 128, 32, CH)
        out[bsel, half * NTOK:(half + 1) * NTOK, :] = (
            o.transpose(0, 3, 2, 1).reshape(NTOK, DM))
    return out


# version 06
# speedup vs baseline: 1.0419x; 1.0419x over previous
import numpy as np

import concourse.bass as bass
import concourse.mybir as mybir
from concourse.bass_utils import run_bass_kernel_spmd

F32 = mybir.dt.float32
BF16 = mybir.dt.bfloat16
AF = mybir.ActivationFunctionType
OP = mybir.AluOpType
AX = mybir.AxisListType

B = 4
S = 4096
DM = 4096
NTOK = 2048
CH = 256
NCH = NTOK // CH
NF = DM // 128
NMQ = 32
NMK = 8
HQ = 32
KVH = 8
MODC = 6 * DM
NPAN = MODC // 128
RSCALE = float(np.float32(1.0) / np.float32(np.sqrt(np.float32(128.0))))
EPS = 1e-5
SCALE_FFN = 4.0


class Builder:

    def __init__(self, mode, events=None, engines=None, sems=None):
        self.mode = mode
        self.events = events if events is not None else {}
        self.engines = engines
        self.sems = sems
        self.counts = {"pe": 0, "dve": 0, "act": 0, "syn": 0, "gp": 0}

    CHAIN = ("dve", "act")

    def op(self, eng, kind, key=None, waits=(), sem=None, **args):
        track = sem if kind == "dma" else eng
        if self.mode == "count":
            if eng in self.CHAIN:
                self.counts[eng] = self.counts.get(eng, 0) + 1
                if key is not None:
                    assert key not in self.events, key
                    self.events[key] = (eng, self.counts[eng])
            elif key is not None:
                self.counts[track] = self.counts.get(track, 0) + (
                    16 if kind == "dma" else 1)
                assert key not in self.events, key
                self.events[key] = (track, self.counts[track])
            return
        if self.mode != eng:
            return
        h = self.engines[eng]
        for k in waits:
            peng, val = self.events[k]
            if peng != eng or peng not in ("pe", "dve", "act"):
                h.wait_ge(self.sems[peng], val)
        if eng in self.CHAIN:
            pre = self.counts.get(eng, 0)
            self.counts[eng] = pre + 1
            if pre > 0:
                h.wait_ge(self.sems[eng], pre)
        if kind == "dma":
            pre = self.events[key][1] - 16
            if pre > 0:
                h.wait_ge(self.sems[sem], pre)
            ins = h.dma_start(out=args["out"], in_=args["in_"])
        elif kind == "mm":
            ins = h.matmul(args["out"], args["lhsT"], args["rhs"],
                           start=args["start"], stop=args["stop"],
                           tile_position=args.get("tp"))
        elif kind == "tp":
            ins = h.transpose(args["out"], args["in_"], args["ident"])
        elif kind == "act":
            ins = h.activation(args["out"], args["in_"], args["func"],
                               bias=args.get("bias", 0.0), scale=args.get("scale", 1.0))
        elif kind == "sq":
            ins = h.square(args["out"], args["in_"])
        elif kind == "scp":
            ins = h.copy(args["out"], args["in_"])
        elif kind == "cp":
            ins = h.tensor_copy(args["out"], args["in_"])
        elif kind == "tt":
            ins = h.tensor_tensor(args["out"], args["in0"], args["in1"], args["op"])
        elif kind == "ts":
            ins = h.tensor_scalar(args["out"], args["in0"], args["s1"],
                                  args.get("s2"), args["op0"],
                                  args.get("op1", OP.bypass))
        elif kind == "tss":
            ins = h.tensor_single_scalar(args["out"], args["in_"], args["s"], args["op"])
        elif kind == "stt":
            ins = h.scalar_tensor_tensor(args["out"], args["in0"], args["s"],
                                         args["in1"], args["op0"], args["op1"])
        elif kind == "red":
            ins = h.tensor_reduce(args["out"], args["in_"], axis=AX.X, op=args["op"],
                                  negate=args.get("negate", False))
        elif kind == "rec":
            ins = h.reciprocal(args["out"], args["in_"])
        elif kind == "bar":
            return
        else:
            raise ValueError(kind)
        if eng in self.CHAIN:
            ins.then_inc(self.sems[eng], 1)
        elif key is not None:
            ins.then_inc(self.sems[track], 16 if kind == "dma" else 1)

    def dma(self, eng, out, in_, key=None, waits=(), sem=None):
        assert key is not None and sem is not None
        self.op(eng, "dma", key=key, waits=waits, sem=sem, out=out, in_=in_)

    def mm(self, out, lhsT, rhs, start, stop, key=None, waits=(), tp=None):
        self.op("pe", "mm", key=key, waits=waits, out=out, lhsT=lhsT, rhs=rhs,
                start=start, stop=stop, tp=tp)

    def tp(self, out, in_, ident, key=None, waits=()):
        self.op("pe", "tp", key=key, waits=waits, out=out, in_=in_, ident=ident)


def build_nc():
    nc = bass.Bass("TRN2", target_bir_lowering=False)

    X2d = nc.dram_tensor("X2", [NCH, 128, NF * CH], F32, kind="ExternalInput").ap()
    W2q = nc.dram_tensor("W2q", [NMQ, 128, NF * 128], F32, kind="ExternalInput").ap()
    W2k = nc.dram_tensor("W2k", [NMK, 128, NF * 128], F32, kind="ExternalInput").ap()
    W2o = nc.dram_tensor("W2o", [NMQ, 128, NF * 128], F32, kind="ExternalInput").ap()
    W2m = nc.dram_tensor("W2m", [NPAN, 128, NF * 128], F32, kind="ExternalInput").ap()
    WvD = nc.dram_tensor("WvD", [NF, 128, KVH * 128], F32, kind="ExternalInput").ap()
    WrTD = nc.dram_tensor("WrTD", [128, NF * 4], F32, kind="ExternalInput").ap()
    ATD = nc.dram_tensor("ATD", [128, NF * 64], F32, kind="ExternalInput").ap()
    BsD = nc.dram_tensor("BsD", [64, DM], F32, kind="ExternalInput").ap()
    cb2D = nc.dram_tensor("cb2D", [128, 32], F32, kind="ExternalInput").ap()
    ConeD = nc.dram_tensor("ConeD", [128, 255], F32, kind="ExternalInput").ap()
    Id128D = nc.dram_tensor("Id128D", [128, 128], F32, kind="ExternalInput").ap()
    Id4D = nc.dram_tensor("Id4D", [4, 4], F32, kind="ExternalInput").ap()
    Id1D = nc.dram_tensor("Id1D", [1, 1], F32, kind="ExternalInput").ap()
    Ones128D = nc.dram_tensor("Ones128D", [128, 1], F32, kind="ExternalInput").ap()
    Ones1xD = nc.dram_tensor("Ones1xD", [1, 128], F32, kind="ExternalInput").ap()
    Sum8D = nc.dram_tensor("Sum8D", [128, 16], F32, kind="ExternalInput").ap()
    Exp16D = nc.dram_tensor("Exp16D", [16, 128], F32, kind="ExternalInput").ap()
    Exp464D = nc.dram_tensor("Exp464D", [4, 64], F32, kind="ExternalInput").ap()
    OUTd = nc.dram_tensor("OUT", [NCH, 128, NF * CH], F32, kind="ExternalOutput").ap()
    QHL = nc.dram_tensor("QHL", [NMQ, 128, 2 * NF * 128], BF16).ap()
    KHL = nc.dram_tensor("KHL", [NMK, 128, 2 * NF * 128], BF16).ap()
    OHL = nc.dram_tensor("OHL", [NMQ, 128, 2 * NF * 128], BF16).ap()
    VHL = nc.dram_tensor("VHL", [NF, 128, 2 * KVH * 128], BF16).ap()
    ATHL = nc.dram_tensor("ATHL", [128, 2 * NF * 64], BF16).ap()

    A = nc.alloc_sbuf_tensor
    xh = A("xh", [128, NF * CH], F32).ap()
    hbh = A("hbh", [128, NF * CH], BF16).ap()
    hbl = A("hbl", [128, NF * CH], BF16).ap()
    qb = A("qb", [128, NF * CH], F32).ap()
    x2b = A("x2b", [128, NF * CH], F32).ap()
    wcA = A("wcA", [128, 2 * NF * 128], BF16).ap()
    wcB = A("wcB", [128, 2 * NF * 128], BF16).ap()
    wvA = A("wvA", [128, 2 * KVH * 128], BF16).ap()
    wvB = A("wvB", [128, 2 * KVH * 128], BF16).ap()
    wrtb = A("wrtb", [128, NF * 4], F32).ap()
    mb0 = A("mb0", [128, CH], F32).ap()
    mb1 = A("mb1", [128, CH], F32).ap()
    e1b = A("e1b", [128, CH], F32).ap()
    e2b = A("e2b", [128, CH], F32).ap()
    ptok = A("ptok", [128, 2 * 256], F32).ap()
    sqb0 = A("sqb0", [128, CH], F32).ap()
    sqb1 = A("sqb1", [128, CH], F32).ap()
    avt0 = A("avt0", [128, 128], F32).ap()
    avt1 = A("avt1", [128, 128], F32).ap()
    stt_ = A("stats", [1, 13 * CH], F32).ap()
    recb = A("recb", [16, 2 * CH], F32).ap()
    modv = A("modv", [128, NPAN], F32).ap()
    modv2 = A("modv2", [128, 128], F32).ap()
    sct = A("sct", [128, 32], F32).ap()
    cbb = A("cbb", [128, 32], F32).ap()
    gat = A("gat", [128, 96], F32).ap()
    gtt = A("gtt", [4, 2 * CH], F32).ap()
    geb = A("geb", [64, CH], F32).ap()
    gmb = A("gmb", [64, CH], F32).ap()
    cone = A("cone", [128, 255], F32).ap()
    id128 = A("id128", [128, 128], F32).ap()
    id4 = A("id4", [4, 4], F32).ap()
    id1 = A("id1", [1, 1], F32).ap()
    ones128 = A("ones128", [128, 1], F32).ap()
    ones1x = A("ones1x", [1, 128], F32).ap()
    sum8 = A("sum8", [128, 16], F32).ap()
    exp16 = A("exp16", [16, 128], F32).ap()
    exp464 = A("exp464", [4, 64], F32).ap()

    ktb = x2b[:, 0:NMK * CH]
    vtb = x2b[:, NMK * CH:NMK * CH + 2048]
    wcAf = wcA.bitcast(F32)
    wcBf = wcB.bitcast(F32)
    mp0 = wcAf[:, 0:2048]
    mp1 = wcAf[:, 2048:4096]
    spo0 = wcB[:, 0:4096]
    spo1 = wcB[:, 4096:8192]
    bsv0 = wcBf[0:64, 0:2048]
    bsv1 = wcBf[0:64, 2048:4096]
    pab = mb0
    pbb = mb1

    PS = [nc.alloc_psum_tensor(f"ps{i}", [128, 512], F32).ap() for i in range(8)]
    sems = {nm: nc.alloc_semaphore(f"sem_{nm}")
            for nm in ("pe", "dve", "act", "wa", "wb", "va", "vb", "x", "c", "st",
                       "sw")}

    def stv(i):
        return stt_[0:1, i * CH:(i + 1) * CH]

    def schedule(b: Builder):
        b.dma("syn", cone[:, :], ConeD[:, :], key="cst0", sem="c")
        b.dma("syn", id128[:, :], Id128D[:, :], key="cst1", sem="c")
        b.dma("syn", id4[:, :], Id4D[:, :], key="cst2", sem="c")
        b.dma("syn", id1[:, :], Id1D[:, :], key="cst3", sem="c")
        b.dma("syn", ones128[:, :], Ones128D[:, :], key="cst4", sem="c")
        b.dma("syn", ones1x[:, :], Ones1xD[:, :], key="cst5", sem="c")
        b.dma("syn", sum8[:, :], Sum8D[:, :], key="cst6", sem="c")
        b.dma("syn", exp16[:, :], Exp16D[:, :], key="cst7", sem="c")
        b.dma("syn", exp464[:, :], Exp464D[:, :], key="cst8", sem="c")
        b.dma("syn", cbb[:, :], cb2D[:, :], key="ld_c", sem="c")
        b.dma("syn", xh[:, :], X2d[0], key="ld_x_0", sem="x")
        b.op("act", "act", key="silu", waits=("ld_c",),
             out=sct[:, :], in_=cbb[:, :], func=AF.Silu)

        for n in range(NPAN):
            psm = PS[n % 2]
            for half in range(2):
                i2 = 2 * n + half
                sl = mp0 if i2 % 2 == 0 else mp1
                ww = [f"g0h_{i2-2}"] if i2 >= 2 else []
                b.dma("syn", sl[:, :], W2m[n][:, half * 2048:(half + 1) * 2048],
                      key=f"w0_{i2}", waits=ww, sem=("wa" if i2 % 2 == 0 else "wb"))
                mmw = [f"w0_{i2}", "silu"]
                if n >= 2 and half == 0:
                    mmw.append(f"c0_{n-2}")
                for kq in range(4):
                    for j in range(4):
                        kk = kq * 4 + j
                        k = half * 16 + kk
                        b.mm(psm[32 * j:32 * j + 1, 0:128], sct[:, k:k + 1],
                             sl[:, kk * 128:(kk + 1) * 128],
                             start=(half == 0 and kq == 0),
                             stop=(half == 1 and kq == 3),
                             key=(f"g0h_{i2}" if (kq == 3 and j == 3) else None),
                             waits=(mmw if (kq == 0 and j == 0) else ()),
                             tp=(0, 32 * j))
            tf = mb0[:, 0:128] if n % 2 == 0 else mb1[:, 0:128]
            tw = [f"g0h_{2 * n + 1}"]
            if n >= 2:
                tw.append(f"t0_{n-2}")
            for j in range(4):
                b.op("dve", "cp", key=(f"c0_{n}" if j == 3 else None),
                     waits=(tw if j == 0 else ()),
                     out=tf[32 * j:32 * j + 1, 0:128],
                     in_=psm[32 * j:32 * j + 1, 0:128])
            pst = PS[2 + (n % 2)]
            tpw = [f"c0_{n}"]
            if n >= 2:
                tpw.append(f"c1_{n-2}")
            b.tp(pst[0:128, 0:128], tf[:, :], id128[:, :], key=f"t0_{n}", waits=tpw)
            b.op("dve", "red", key=f"c1_{n}", waits=(f"t0_{n}",),
                 out=modv[:, n:n + 1], in_=pst[0:128, 0:128:32], op=OP.add)
        jobs = []
        for m in range(NMQ):
            for hf in range(2):
                jobs.append((f"spq_{m}_{hf}", W2q[m][:, hf * 2048:(hf + 1) * 2048],
                             QHL[m][:, hf * 2048:(hf + 1) * 2048],
                             QHL[m][:, 4096 + hf * 2048:4096 + (hf + 1) * 2048], 2048))
        for m in range(NMK):
            for hf in range(2):
                jobs.append((f"spk_{m}_{hf}", W2k[m][:, hf * 2048:(hf + 1) * 2048],
                             KHL[m][:, hf * 2048:(hf + 1) * 2048],
                             KHL[m][:, 4096 + hf * 2048:4096 + (hf + 1) * 2048], 2048))
        for f in range(NF):
            jobs.append((f"spv_{f}", WvD[f], VHL[f][:, 0:1024],
                         VHL[f][:, 1024:2048], 1024))
        for m in range(NMQ):
            for hf in range(2):
                jobs.append((f"spo_{m}_{hf}", W2o[m][:, hf * 2048:(hf + 1) * 2048],
                             OHL[m][:, hf * 2048:(hf + 1) * 2048],
                             OHL[m][:, 4096 + hf * 2048:4096 + (hf + 1) * 2048], 2048))
        jobs.append(("spat", ATD[:, :], ATHL[:, 0:2048], ATHL[:, 2048:4096], 2048))
        nJ = len(jobs)
        for i, (jk, srcap, hidst, lodst, w) in enumerate(jobs):
            sl = mp0 if i % 2 == 0 else mp1
            so = spo0 if i % 2 == 0 else spo1
            ww = []
            if i < 2:
                ww.append(f"g0h_{2 * NPAN - 2 + i}")
            else:
                ww += [f"spc_{i-2}", f"spl_{i-2}"]
            b.dma("syn", sl[:, 0:w], srcap, key=f"spld_{i}", waits=ww,
                  sem=("wa" if i % 2 == 0 else "wb"))
            cw = [f"spld_{i}"]
            if i >= 2:
                cw.append(f"spst_{i-2}")
            b.op("act", "scp", key=f"spc_{i}", waits=cw,
                 out=so[:, 0:w], in_=sl[:, 0:w])
            b.op("dve", "tt", key=f"spl_{i}", waits=(f"spc_{i}",),
                 out=so[:, w:2 * w], in0=sl[:, 0:w], in1=so[:, 0:w],
                 op=OP.subtract)
            b.dma("gp", hidst, so[:, 0:w], key=f"spsh_{i}", waits=(f"spl_{i}",),
                  sem="sw")
            b.dma("gp", lodst, so[:, w:2 * w], key=f"spst_{i}", sem="sw")
        b.op("dve", "ts", waits=(f"c1_{NPAN-1}",), out=modv2[:, 0:32],
             in0=modv[:, 32:64], s1=1.0, op0=OP.add)
        b.op("dve", "ts", out=modv2[:, 64:96], in0=modv[:, 128:160], s1=1.0, op0=OP.add)
        for (src0, dst0) in ((64, 32), (160, 96)):
            b.op("dve", "tss", out=gat[:, 0:32], in_=modv[:, src0:src0 + 32], s=0.0,
                 op=OP.is_ge)
            b.op("dve", "ts", out=gat[:, 32:64], in0=gat[:, 0:32], s1=2.0, s2=-1.0,
                 op0=OP.mult, op1=OP.add)
            b.op("dve", "tt", out=gat[:, 64:96], in0=modv[:, src0:src0 + 32],
                 in1=gat[:, 32:64], op=OP.mult)
            b.op("dve", "tss", out=gat[:, 0:32], in_=gat[:, 64:96], s=0.25,
                 op=OP.max)
            b.op("dve", "tt", out=modv2[:, dst0:dst0 + 32], in0=gat[:, 0:32],
                 in1=gat[:, 32:64], op=OP.mult)
        b.op("dve", "ts", key="modv2", out=modv2[:, 96:128], in0=modv2[:, 96:128],
             s1=SCALE_FFN, op0=OP.mult)

        sa_col = modv[:, 0:32]
        sb_col = modv[:, 96:128]
        caP = modv2[:, 0:32]
        gaC = modv2[:, 32:64]
        cbP = modv2[:, 64:96]
        gbC4 = modv2[:, 96:128]

        for t in range(NCH):
            stats_w = [f"ld_x_{t}"]
            if t > 0:
                stats_w.append(f"varep2_{t-1}")
            else:
                stats_w.append(f"c0_{NPAN-1}")
            for f in range(NF):
                xt = xh[:, f * CH:(f + 1) * CH]
                sq = sqb0 if f % 2 == 0 else sqb1
                sqw = [f"ld_x_{t}"]
                if f >= 2:
                    sqw.append(f"sqmm_{t}_{f-2}")
                elif t > 0:
                    sqw.append(f"stats2_{t-1}")
                    sqw.append(f"h2_{t-1}")
                b.op("act", "sq", key=f"sq_{t}_{f}", waits=sqw, out=sq[:, :], in_=xt)
                b.mm(PS[0][0:1, 0:CH], ones128[:, 0:1], xt,
                     start=(f == 0), stop=(f == NF - 1),
                     waits=(stats_w if f == 0 else ()))
                b.mm(PS[1][0:1, 0:CH], ones128[:, 0:1], sq[:, :],
                     start=(f == 0), stop=(f == NF - 1),
                     key=(f"stats_{t}" if f == NF - 1 else f"sqmm_{t}_{f}"),
                     waits=(f"sq_{t}_{f}",))
            def rstd_chain(tag, statskey):
                b.op("dve", "ts", waits=(statskey,), out=stv(0), in0=PS[0][0:1, 0:CH],
                     s1=1.0 / DM, op0=OP.mult)
                b.op("dve", "ts", out=stv(1), in0=PS[1][0:1, 0:CH], s1=1.0 / DM,
                     op0=OP.mult)
                b.op("dve", "ts", out=stv(2), in0=stv(0), s1=-1.0, op0=OP.mult)
                b.op("dve", "tt", out=stv(3), in0=stv(0), in1=stv(0), op=OP.mult)
                b.op("dve", "tt", out=stv(4), in0=stv(1), in1=stv(3), op=OP.subtract)
                b.op("dve", "ts", key=f"varep{tag}", out=stv(5), in0=stv(4), s1=EPS,
                     op0=OP.add)
                b.op("act", "act", waits=(f"varep{tag}",), out=stv(6), in_=stv(5),
                     func=AF.Ln)
                b.op("act", "act", key=f"rstd0{tag}", out=stv(7), in_=stv(6),
                     func=AF.Exp, scale=-0.5)
                src = 7
                for it in range(2):
                    b.op("dve", "tt", waits=((f"rstd0{tag}",) if it == 0 else ()),
                         out=stv(8), in0=stv(src), in1=stv(src), op=OP.mult)
                    b.op("dve", "tt", out=stv(9), in0=stv(8), in1=stv(5), op=OP.mult)
                    b.op("dve", "ts", out=stv(10), in0=stv(9), s1=-0.5, s2=1.5,
                         op0=OP.mult, op1=OP.add)
                    dst = 11 if it == 0 else 12
                    b.op("dve", "tt", key=(f"rstd{tag}" if it == 1 else None),
                         out=stv(dst), in0=stv(src), in1=stv(10), op=OP.mult)
                    src = dst
                return src

            src = rstd_chain(f"_{t}", f"stats_{t}")
            bc_w = [f"rstd_{t}", "modv2" if t == 0 else f"gebc_{t-1}"]
            b.mm(PS[2][:, 0:CH], ones1x[0:1, :], stv(2), start=True, stop=True,
                 waits=bc_w)
            b.mm(PS[3][:, 0:CH], ones1x[0:1, :], stv(src), start=True, stop=True,
                 key=f"bcast_{t}")
            hw = ["modv2", f"bcast_{t}"]
            if t == 0:
                hw.append("spst_0")
            else:
                hw.append(f"pemid_{t-1}")
            for f in range(NF):
                xt = xh[:, f * CH:(f + 1) * CH]
                hh = hbh[:, f * CH:(f + 1) * CH]
                hl = hbl[:, f * CH:(f + 1) * CH]
                b.op("dve", "tt", waits=(hw if f == 0 else ()), out=sqb1[:, :], in0=xt,
                     in1=PS[2][:, 0:CH], op=OP.add)
                b.op("dve", "tt", out=sqb0[:, :],
                     in0=sqb1[:, :], in1=PS[3][:, 0:CH], op=OP.mult)
                b.op("dve", "ts", key=f"hf_{t}_{f}", out=sqb1[:, :],
                     in0=sqb0[:, :], s1=caP[:, f:f + 1], s2=sa_col[:, f:f + 1],
                     op0=OP.mult, op1=OP.add)
                b.op("act", "scp", key=f"hh_{t}_{f}", waits=(f"hf_{t}_{f}",),
                     out=hh, in_=sqb1[:, :])
                b.op("dve", "tt", key=(f"h_{t}" if f == NF - 1 else None),
                     waits=(f"hh_{t}_{f}",), out=hl, in0=sqb1[:, :], in1=hh,
                     op=OP.subtract)
            for (npan, WHLs, dst, tag) in ((NMQ, QHL, qb, "q"),
                                           (NMK, KHL, ktb, "k")):
                for m in range(npan):
                    wc = wcA if m % 2 == 0 else wcB
                    whi = wc[:, 0:NF * 128]
                    wlo = wc[:, NF * 128:2 * NF * 128]
                    wkey = f"w{tag}_{t}_{m}"
                    if tag == "q":
                        if m >= 2:
                            ww = [f"peq_{t}_{m-2}"]
                        elif t == 0:
                            ww = [f"spl_{nJ-1}", f"spst_{nJ-1}"]
                        else:
                            ww = [f"peo_{t-1}_{NMQ-2}" if m == 0
                                  else f"ped_{t-1}_{NMQ-1}"]
                    else:
                        ww = [f"peq_{t}_{NMQ-2+m}"] if m < 2 else [f"pek_{t}_{m-2}"]
                    b.dma("syn", wc[:, :], WHLs[m], key=wkey, waits=ww,
                          sem=("wa" if m % 2 == 0 else "wb"))
                    psq = PS[4 + m % 2]
                    mw = [wkey, f"h_{t}"]
                    if tag == "q":
                        if m >= 2:
                            mw.append(f"evq_{t}_{m-2}")
                        elif t > 0:
                            mw.append(f"gmid_{t-1}")
                            mw.append(f"out_{t-1}_{NMQ-1}")
                            mw.append(f"st_{t-1}")
                    else:
                        mw.append(f"evq_{t}_{NMQ-2+m}" if m < 2 else f"evk_{t}_{m-2}")
                        if m < 2:
                            mw.append(f"out_{t-1}_{NMQ-1}" if t > 0 else "modv2")
                    for f in range(NF):
                        hh = hbh[:, f * CH:(f + 1) * CH]
                        hl = hbl[:, f * CH:(f + 1) * CH]
                        wh = whi[:, f * 128:(f + 1) * 128]
                        wl = wlo[:, f * 128:(f + 1) * 128]
                        for pi, (lh, rr) in enumerate(((wh, hh), (wh, hl), (wl, hh))):
                            b.mm(psq[:, 0:CH], lh, rr,
                                 start=(f == 0 and pi == 0),
                                 stop=(f == NF - 1 and pi == 2),
                                 key=(f"pe{tag}_{t}_{m}"
                                      if (f == NF - 1 and pi == 2) else None),
                                 waits=(mw if (f == 0 and pi == 0) else ()))
                    b.op("act", "scp", key=f"ev{tag}_{t}_{m}",
                         waits=(f"pe{tag}_{t}_{m}",),
                         out=dst[:, m * CH:(m + 1) * CH], in_=psq[:, 0:CH])
            for f in range(NF):
                wv = wvA if f % 2 == 0 else wvB
                wvh = wv[:, 0:1024]
                wvl = wv[:, 1024:2048]
                vw = [f"pev_{t}_{f-2}"] if f >= 2 else []
                if f < 2 and t > 0:
                    vw.append(f"pemid_{t-1}")
                b.dma("syn", wv[:, :], VHL[f], key=f"wv_{t}_{f}", waits=vw,
                      sem=("va" if f % 2 == 0 else "vb"))
                mw0 = [f"wv_{t}_{f}"]
                if f == 0:
                    mw0.append(f"evk_{t}_{NMK-1}")
                    if t > 0:
                        mw0.append(f"out_{t-1}_{NMQ-1}")
                        mw0.append(f"av_{t-1}_1")
                for s2 in range(2):
                    lhh = hbh[:, f * CH + s2 * 128: f * CH + (s2 + 1) * 128]
                    lhl = hbl[:, f * CH + s2 * 128: f * CH + (s2 + 1) * 128]
                    for half in range(2):
                        for pi, (lh, rr) in enumerate(
                                ((lhh, wvh), (lhl, wvh), (lhh, wvl))):
                            b.mm(PS[4 + 2 * s2 + half][:, 0:512], lh,
                                 rr[:, half * 512:(half + 1) * 512],
                                 start=(f == 0 and pi == 0),
                                 stop=(f == NF - 1 and pi == 2),
                                 key=(f"pev_{t}_{f}"
                                      if (s2 == 1 and half == 1 and pi == 2) else None),
                                 waits=(mw0 if (s2 == 0 and half == 0 and pi == 0)
                                        else ()))
            for i in range(4):
                s2, half = divmod(i, 2)
                b.op("act", "scp", key=f"evv_{t}_{i}",
                     waits=((f"pev_{t}_{NF-1}",) if i == 0 else ()),
                     out=vtb[:, s2 * 1024 + half * 512: s2 * 1024 + (half + 1) * 512],
                     in_=PS[4 + i][:, 0:512])
            pair = 0
            for h in range(HQ):
                for j in range(KVH):
                    mbuf = mb0 if pair % 2 == 0 else mb1
                    gw = [f"evq_{t}_{h}", f"evk_{t}_{j}"]
                    if pair >= 2:
                        gw.append(f"gmm_{t}_{pair-2}")
                    b.op("dve", "tt", key=f"gm_{t}_{pair}", waits=gw, out=mbuf[:, :],
                         in0=qb[:, h * CH:(h + 1) * CH],
                         in1=ktb[:, j * CH:(j + 1) * CH], op=OP.mult)
                    tile = h // 16
                    r = (h % 16) * 8 + j
                    first = (h % 16 == 0 and j == 0)
                    last = (h % 16 == 15 and j == 7)
                    mmw = [f"gm_{t}_{pair}"]
                    if first:
                        mmw.append(f"varep_{t}")
                    b.mm(PS[tile][:, 0:CH], cone[:, 127 - r:255 - r], mbuf[:, :],
                         start=first, stop=last, key=f"gmm_{t}_{pair}", waits=mmw)
                    pair += 1
            b.op("act", "act", waits=(f"gmm_{t}_{128 * 1 - 1}",), out=e1b[:, :],
                 in_=PS[0][:, 0:CH], func=AF.Exp)
            b.op("act", "act", key=f"exp_{t}", waits=(f"gmm_{t}_{255}",),
                 out=e2b[:, :], in_=PS[1][:, 0:CH], func=AF.Exp)
            b.mm(PS[2][0:16, 0:CH], sum8[:, :], e1b[:, :], start=True, stop=True,
                 waits=[f"exp_{t}", f"h_{t}"])
            b.mm(PS[3][0:16, 0:CH], sum8[:, :], e2b[:, :], start=True, stop=True,
                 key=f"gs_{t}")
            b.op("dve", "rec", waits=(f"gs_{t}",), out=recb[0:16, 0:CH],
                 in_=PS[2][0:16, 0:CH])
            b.op("dve", "rec", key=f"rec_{t}", out=recb[0:16, CH:2 * CH],
                 in_=PS[3][0:16, 0:CH])
            b.mm(PS[4][:, 0:CH], exp16[:, :], recb[0:16, 0:CH], start=True, stop=True,
                 waits=[f"rec_{t}", f"evv_{t}_3"])
            b.mm(PS[5][:, 0:CH], exp16[:, :], recb[0:16, CH:2 * CH], start=True,
                 stop=True, key=f"pexp_{t}")
            b.op("dve", "tt", waits=(f"pexp_{t}",), out=pab[:, :], in0=e1b[:, :],
                 in1=PS[4][:, 0:CH], op=OP.mult)
            b.op("dve", "tt", key=f"pnorm_{t}", out=pbb[:, :], in0=e2b[:, :],
                 in1=PS[5][:, 0:CH], op=OP.mult)
            for s2 in range(2):
                tpa = [f"pnorm_{t}"] if s2 == 0 else [f"ptkb_{t}_0"]
                if s2 == 0:
                    tpa.append(f"evv_{t}_3")
                b.tp(PS[6][:, s2 * 128:(s2 + 1) * 128], pab[:, s2 * 128:(s2 + 1) * 128],
                     id128[:, :], waits=tpa)
                b.tp(PS[7][:, s2 * 128:(s2 + 1) * 128], pbb[:, s2 * 128:(s2 + 1) * 128],
                     id128[:, :], key=f"ptp_{t}_{s2}")
                b.op("dve", "cp", key=f"ptka_{t}_{s2}", waits=(f"ptp_{t}_{s2}",),
                     out=ptok[:, s2 * 256:s2 * 256 + 128],
                     in_=PS[6][:, s2 * 128:(s2 + 1) * 128])
                b.op("dve", "cp", key=f"ptkb_{t}_{s2}",
                     out=ptok[:, s2 * 256 + 128:(s2 + 1) * 256],
                     in_=PS[7][:, s2 * 128:(s2 + 1) * 128])
            for s2 in range(2):
                for h in range(HQ):
                    pcol0 = s2 * 256 + (0 if h < 16 else 128) + (h % 16) * 8
                    avw = [f"ptkb_{t}_1"] if (s2 == 0 and h == 0) else ()
                    b.op("dve", "ts", waits=avw, out=avt0[:, :],
                         in0=vtb[:, s2 * 1024:s2 * 1024 + 128],
                         s1=ptok[:, pcol0:pcol0 + 1], op0=OP.mult)
                    cur = avt0
                    for j in range(1, 8):
                        if j == 7:
                            dsta = qb[:, h * CH + s2 * 128: h * CH + (s2 + 1) * 128]
                        else:
                            dsta = avt1[:, :] if j % 2 == 1 else avt0[:, :]
                        b.op("dve", "stt",
                             key=(f"av_{t}_{s2}" if (j == 7 and h == HQ - 1) else None),
                             out=dsta,
                             in0=vtb[:, s2 * 1024 + j * 128:s2 * 1024 + (j + 1) * 128],
                             s=ptok[:, pcol0 + j:pcol0 + j + 1],
                             in1=cur[:, :], op0=OP.mult, op1=OP.add)
                        cur = avt1 if j % 2 == 1 else avt0
            grp = 0
            for s2 in range(2):
                for hh in range(HQ):
                    g, q4 = divmod(grp, 4)
                    psb = PS[4 + (g % 2)]
                    tw = []
                    if q4 == 0:
                        if g == 0:
                            tw = [f"av_{t}_0", f"pnorm_{t}"]
                        elif g == 8:
                            tw = [f"av_{t}_1"]
                        if g >= 2:
                            tw.append(f"avev_{t}_{g-2}")
                    b.tp(psb[:, q4 * 128:(q4 + 1) * 128],
                         qb[:, hh * CH + s2 * 128: hh * CH + (s2 + 1) * 128],
                         id128[:, :], key=(f"avtp_{t}_{g}" if q4 == 3 else None),
                         waits=tw)
                    if q4 == 3:
                        for u in range(4):
                            gi = g * 4 + u
                            ss2, hh2 = gi // HQ, gi % HQ
                            c0 = hh2 * CH + ss2 * 128
                            b.op("act", "scp",
                                 key=(f"avevh_{t}_{g}" if u == 3 else None),
                                 waits=((f"avtp_{t}_{g}",) if u == 0 else ()),
                                 out=hbh[:, c0:c0 + 128],
                                 in_=psb[:, u * 128:(u + 1) * 128])
                        for u in range(4):
                            gi = g * 4 + u
                            ss2, hh2 = gi // HQ, gi % HQ
                            c0 = hh2 * CH + ss2 * 128
                            b.op("dve", "tt",
                                 key=(f"avev_{t}_{g}" if u == 3 else None),
                                 waits=((f"avevh_{t}_{g}",) if u == 0 else ()),
                                 out=hbl[:, c0:c0 + 128],
                                 in0=psb[:, u * 128:(u + 1) * 128],
                                 in1=hbh[:, c0:c0 + 128], op=OP.subtract)
                    grp += 1
            nG = grp // 4
            for m in range(NMQ):
                wc = wcA if m % 2 == 0 else wcB
                whi = wc[:, 0:NF * 128]
                wlo = wc[:, NF * 128:2 * NF * 128]
                ww = [f"pek_{t}_{NMK-2+m}"] if m < 2 else [f"peo_{t}_{m-2}"]
                b.dma("syn", wc[:, :], OHL[m], key=f"wo_{t}_{m}", waits=ww,
                      sem=("wa" if m % 2 == 0 else "wb"))
                pso = PS[6 + m % 2]
                mw = [f"wo_{t}_{m}"]
                if m < 2:
                    mw += [f"avev_{t}_{nG-1}", f"ptkb_{t}_1"]
                else:
                    mw.append(f"x2_{t}_{m-2}")
                for f in range(NF):
                    aoh = hbh[:, f * CH:(f + 1) * CH]
                    aol = hbl[:, f * CH:(f + 1) * CH]
                    wh = whi[:, f * 128:(f + 1) * 128]
                    wl = wlo[:, f * 128:(f + 1) * 128]
                    for pi, (lh, rr) in enumerate(((wh, aoh), (wh, aol), (wl, aoh))):
                        b.mm(pso[:, 0:CH], lh, rr,
                             start=(f == 0 and pi == 0),
                             stop=(f == NF - 1 and pi == 2),
                             key=(f"peo_{t}_{m}" if (f == NF - 1 and pi == 2) else None),
                             waits=(mw if (f == 0 and pi == 0) else ()))
                x2w = [f"peo_{t}_{m}"]
                if m < 2:
                    x2w.append(f"gmm_{t}_{255}")
                b.op("dve", "stt", key=f"x2_{t}_{m}", waits=x2w,
                     out=x2b[:, m * CH:(m + 1) * CH], in0=pso[:, 0:CH],
                     s=gaC[:, m:m + 1], in1=xh[:, m * CH:(m + 1) * CH],
                     op0=OP.mult, op1=OP.add)
                sq = sqb0 if m % 2 == 0 else sqb1
                sqw = [f"x2_{t}_{m}"]
                if m >= 2:
                    sqw.append(f"sqmm2_{t}_{m-2}")
                b.op("act", "sq", key=f"sq2_{t}_{m}", waits=sqw, out=sq[:, :],
                     in_=x2b[:, m * CH:(m + 1) * CH])
                st_w = [f"x2_{t}_{m}"]
                if m == 0:
                    st_w.append(f"exp_{t}")
                b.mm(PS[0][0:1, 0:CH], ones128[:, 0:1], x2b[:, m * CH:(m + 1) * CH],
                     start=(m == 0), stop=(m == NMQ - 1), waits=st_w)
                b.mm(PS[1][0:1, 0:CH], ones128[:, 0:1], sq[:, :],
                     start=(m == 0), stop=(m == NMQ - 1),
                     key=(f"stats2_{t}" if m == NMQ - 1 else f"sqmm2_{t}_{m}"),
                     waits=(f"sq2_{t}_{m}",))
            src = rstd_chain(f"2_{t}", f"stats2_{t}")
            b.mm(PS[2][:, 0:CH], ones1x[0:1, :], stv(2), start=True, stop=True,
                 waits=[f"rstd2_{t}", f"rec_{t}"])
            b.mm(PS[3][:, 0:CH], ones1x[0:1, :], stv(src), start=True, stop=True,
                 key=f"bcast2_{t}")
            for f in range(NF):
                x2t = x2b[:, f * CH:(f + 1) * CH]
                ht = xh[:, f * CH:(f + 1) * CH]
                b.op("dve", "tt", waits=((f"bcast2_{t}",) if f == 0 else ()),
                     out=sqb0[:, :], in0=x2t, in1=PS[2][:, 0:CH], op=OP.add)
                b.op("dve", "tt", out=sqb1[:, :], in0=sqb0[:, :], in1=PS[3][:, 0:CH],
                     op=OP.mult)
                b.op("dve", "ts", key=f"h2f_{t}_{f}", out=ht,
                     in0=sqb1[:, :], s1=cbP[:, f:f + 1], s2=sb_col[:, f:f + 1],
                     op0=OP.mult, op1=OP.add)
                b.op("act", "scp", key=f"h2h_{t}_{f}", waits=(f"h2f_{t}_{f}",),
                     out=hbh[:, f * CH:(f + 1) * CH], in_=ht)
                b.op("dve", "tt", key=(f"h2_{t}" if f == NF - 1 else None),
                     waits=(f"h2h_{t}_{f}",),
                     out=hbl[:, f * CH:(f + 1) * CH], in0=ht,
                     in1=hbh[:, f * CH:(f + 1) * CH], op=OP.subtract)
            atw = [f"pev_{t}_{NF-1}"]
            b.dma("syn", wvA[:, :], ATHL[:, 0:2048], key=f"at1_{t}", waits=atw,
                  sem="va")
            b.dma("syn", wvB[:, :], ATHL[:, 2048:4096], key=f"at_{t}", waits=atw,
                  sem="vb")
            if t == 0:
                b.dma("syn", wrtb[:, :], WrTD[:, :], key="wrt_0", sem="c")
            bsw = [f"peo_{t}_{NMQ-1}"]
            b.dma("syn", bsv0[:, :], BsD[:, 0:2048], key=f"bs0_{t}", waits=bsw,
                  sem="wa")
            b.dma("syn", bsv1[:, :], BsD[:, 2048:4096], key=f"bs_{t}",
                  waits=bsw, sem="wb")
            mw = [f"h2_{t}", f"at1_{t}", f"at_{t}", f"avev_{t}_{nG-1}",
                  f"pnorm_{t}"]
            if t == 0:
                mw.append("wrt_0")
            for f in range(NF):
                ath = wvA[:, f * 64:(f + 1) * 64]
                atl = wvB[:, f * 64:(f + 1) * 64]
                h2h = hbh[:, f * CH:(f + 1) * CH]
                h2l = hbl[:, f * CH:(f + 1) * CH]
                for pi, (lh, rr) in enumerate(((ath, h2h), (ath, h2l), (atl, h2h))):
                    b.mm(PS[4][0:64, 0:CH], lh, rr,
                         start=(f == 0 and pi == 0), stop=(f == NF - 1 and pi == 2),
                         key=(f"pemid_{t}" if (f == NF - 1 and pi == 2) else None),
                         waits=(mw if (f == 0 and pi == 0) else ()))
                b.mm(PS[5][0:4, 0:CH], wrtb[:, f * 4:(f + 1) * 4],
                     xh[:, f * CH:(f + 1) * CH], start=(f == 0), stop=(f == NF - 1),
                     key=(f"pelg_{t}" if f == NF - 1 else None))
            b.op("dve", "cp", key=f"lsb_{t}", waits=(f"pelg_{t}",),
                 out=gtt[0:4, 0:CH], in_=PS[5][0:4, 0:CH])
            for s2 in range(2):
                tw = [f"lsb_{t}"] if s2 == 0 else [f"gtt_{t}_0"]
                b.tp(PS[5][:, 256 + 4 * s2:256 + 4 * (s2 + 1)],
                     gtt[0:4, s2 * 128:(s2 + 1) * 128], id4[0:4, 0:4],
                     key=f"lgt_{t}_{s2}", waits=tw)
                L = gat[:, 0:4]
                GT = gat[:, 4:8]
                RK = gat[:, 8:12]
                MS = gat[:, 12:16]
                E4 = gat[:, 16:20]
                EM = gat[:, 20:24]
                M4 = gat[:, 24:25]
                S4 = gat[:, 25:26]
                R4 = gat[:, 26:27]
                GS = gat[:, 28 + 4 * s2:32 + 4 * s2]
                b.op("dve", "cp", waits=(f"lgt_{t}_{s2}",), out=L,
                     in_=PS[5][:, 256 + 4 * s2:256 + 4 * (s2 + 1)])
                for e in range(4):
                    b.op("dve", "tss", out=GT, in_=L, s=L[:, e:e + 1], op=OP.is_gt)
                    b.op("dve", "red", out=RK[:, e:e + 1], in_=GT, op=OP.add)
                    if e > 0:
                        b.op("dve", "tss", out=GT[:, 0:e], in_=L[:, 0:e],
                             s=L[:, e:e + 1], op=OP.is_equal)
                        b.op("dve", "red", out=M4, in_=GT[:, 0:e], op=OP.add)
                        b.op("dve", "tt", out=RK[:, e:e + 1], in0=RK[:, e:e + 1],
                             in1=M4, op=OP.add)
                b.op("dve", "tss", out=MS, in_=RK, s=1.5, op=OP.is_lt)
                b.op("dve", "red", key=f"m4_{t}_{s2}", out=M4, in_=L, op=OP.max,
                     negate=True)
                b.op("act", "act", key=f"e4_{t}_{s2}", waits=(f"m4_{t}_{s2}",),
                     out=E4, in_=L, func=AF.Exp, bias=M4)
                b.op("dve", "tt", waits=(f"e4_{t}_{s2}",), out=EM, in0=E4, in1=MS,
                     op=OP.mult)
                b.op("dve", "red", out=S4, in_=EM, op=OP.add)
                b.op("dve", "rec", out=R4, in_=S4)
                b.op("dve", "ts", key=f"gates_{t}_{s2}", out=GS, in0=EM,
                     s1=R4[:, 0:1], op0=OP.mult)
                b.tp(PS[5][0:4, s2 * 128:(s2 + 1) * 128], GS, id128[:, :],
                     key=f"gt_{t}_{s2}", waits=(f"gates_{t}_{s2}",))
                b.op("dve", "cp", key=f"gtt_{t}_{s2}", waits=(f"gt_{t}_{s2}",),
                     out=gtt[0:4, CH + s2 * 128:CH + (s2 + 1) * 128],
                     in_=PS[5][0:4, s2 * 128:(s2 + 1) * 128])
            b.mm(PS[2][0:64, 256:256 + CH], exp464[0:4, :], gtt[0:4, CH:2 * CH],
                 start=True, stop=True, key=f"ge_{t}",
                 waits=[f"gtt_{t}_1", f"h2_{t}"])
            b.op("dve", "cp", key=f"gebc_{t}", waits=(f"ge_{t}",), out=geb[:, :],
                 in_=PS[2][0:64, 256:256 + CH])
            b.op("dve", "tt", key=f"gmid_{t}", waits=(f"pemid_{t}",), out=gmb[:, :],
                 in0=PS[4][0:64, 0:CH], in1=geb[:, :], op=OP.mult)
            for m in range(NMQ):
                psd = PS[6 + m % 2]
                bsl = bsv0 if m < 16 else bsv1
                mw = [f"gmid_{t}", f"bs0_{t}" if m < 16 else f"bs_{t}"]
                mw.append(f"x2_{t}_{NMQ-1}" if m < 2 else f"out_{t}_{m-2}")
                b.mm(psd[:, 0:CH], bsl[:, (m % 16) * 128:(m % 16 + 1) * 128],
                     gmb[:, :], start=True, stop=True, key=f"ped_{t}_{m}", waits=mw)
                ow = [f"ped_{t}_{m}"]
                if m == 0 and t > 0:
                    ow.append(f"st_{t-1}")
                b.op("dve", "stt", key=f"out_{t}_{m}", waits=ow,
                     out=qb[:, m * CH:(m + 1) * CH], in0=psd[:, 0:CH],
                     s=gbC4[:, m:m + 1], in1=x2b[:, m * CH:(m + 1) * CH],
                     op0=OP.mult, op1=OP.add)
            b.dma("gp", OUTd[t], qb[:, :], key=f"st_{t}", waits=(f"out_{t}_{NMQ-1}",), sem="st")
            if t + 1 < NCH:
                b.dma("syn", xh[:, :], X2d[t + 1], key=f"ld_x_{t+1}",
                      waits=(f"pelg_{t}", f"h2_{t}"), sem="x")

    cb = Builder("count")
    schedule(cb)
    events = cb.events

    with nc.Block() as block:

        @block.sync
        def _(sync):
            schedule(Builder("syn", events, {"syn": sync}, sems))

        @block.tensor
        def _(te):
            schedule(Builder("pe", events, {"pe": te}, sems))

        @block.vector
        def _(ve):
            schedule(Builder("dve", events, {"dve": ve}, sems))

        @block.scalar
        def _(sc):
            schedule(Builder("act", events, {"act": sc}, sems))

        @block.gpsimd
        def _(gp):
            schedule(Builder("gp", events, {"gp": gp}, sems))
            gp.wait_ge(sems["st"], events[f"st_{NCH-1}"][1])

    return nc


def _consts():
    cone = np.zeros((128, 255), dtype=np.float32)
    cone[:, 127] = np.float32(RSCALE)
    sum8 = np.zeros((128, 16), dtype=np.float32)
    for p in range(128):
        sum8[p, p // 8] = 1.0
    exp16 = np.zeros((16, 128), dtype=np.float32)
    for p in range(128):
        exp16[p // 8, p] = 1.0
    exp464 = np.zeros((4, 64), dtype=np.float32)
    for e in range(4):
        exp464[e, e * 16:(e + 1) * 16] = 1.0
    return dict(ConeD=cone, Id128D=np.eye(128, dtype=np.float32),
                Id4D=np.eye(4, dtype=np.float32), Id1D=np.eye(1, dtype=np.float32),
                Ones128D=np.ones((128, 1), dtype=np.float32),
                Ones1xD=np.ones((1, 128), dtype=np.float32),
                Sum8D=sum8, Exp16D=exp16, Exp464D=exp464)


def _stage(x, Wq, Wk, Wv, Wo, Af, Bf, Wr, c, wmod):
    W2q = np.ascontiguousarray(
        Wq.reshape(32, 128, 32, 128).transpose(2, 1, 0, 3)).reshape(32, 128, 32 * 128)
    W2k = np.ascontiguousarray(
        Wk.reshape(32, 128, 8, 128).transpose(2, 1, 0, 3)).reshape(8, 128, 32 * 128)
    W2o = np.ascontiguousarray(
        Wo.reshape(32, 128, 32, 128).transpose(2, 1, 0, 3)).reshape(32, 128, 32 * 128)
    W2m = np.ascontiguousarray(
        wmod.reshape(32, 128, 192, 128).transpose(2, 1, 0, 3)).reshape(192, 128, 32 * 128)
    WvD = np.ascontiguousarray(Wv.reshape(32, 128, 1024))
    WrTD = np.ascontiguousarray(
        Wr.reshape(4, 32, 128).transpose(2, 1, 0)).reshape(128, 128)
    ATD = np.ascontiguousarray(
        Af.transpose(2, 0, 1).reshape(32, 128, 64).transpose(1, 0, 2)).reshape(128, 2048)
    BsD = np.ascontiguousarray(Bf.transpose(0, 2, 1).reshape(64, 4096))
    return W2q, W2k, W2o, W2m, WvD, WrTD, ATD, BsD


def kernel(L0_attn_hidden_states, L0_attn_Wq, L0_attn_Wk, L0_attn_Wv, L0_attn_Wo,
           L0_ffn_A, L0_ffn_B, L0_ffn_Wr, L0_c, L0_wmod):
    x = np.ascontiguousarray(L0_attn_hidden_states, dtype=np.float32)
    Wq = np.ascontiguousarray(L0_attn_Wq, dtype=np.float32)
    Wk = np.ascontiguousarray(L0_attn_Wk, dtype=np.float32)
    Wv = np.ascontiguousarray(L0_attn_Wv, dtype=np.float32)
    Wo = np.ascontiguousarray(L0_attn_Wo, dtype=np.float32)
    Af = np.ascontiguousarray(L0_ffn_A, dtype=np.float32)
    Bf = np.ascontiguousarray(L0_ffn_B, dtype=np.float32)
    Wr = np.ascontiguousarray(L0_ffn_Wr, dtype=np.float32)
    c = np.ascontiguousarray(L0_c, dtype=np.float32)
    wmod = np.ascontiguousarray(L0_wmod, dtype=np.float32)

    W2q, W2k, W2o, W2m, WvD, WrTD, ATD, BsD = _stage(x, Wq, Wk, Wv, Wo, Af, Bf, Wr,
                                                     c, wmod)
    consts = _consts()
    nc = build_nc()
    in_maps = []
    for core in range(8):
        bsel, half = core // 2, core % 2
        xs = x[bsel, half * NTOK:(half + 1) * NTOK, :]
        X2 = np.ascontiguousarray(
            xs.reshape(NCH, CH, 32, 128).transpose(0, 3, 2, 1)).reshape(NCH, 128,
                                                                        32 * CH)
        cb2 = np.ascontiguousarray(c[bsel].reshape(32, 128).T)
        in_maps.append(dict(X2=X2, W2q=W2q, W2k=W2k, W2o=W2o, W2m=W2m, WvD=WvD,
                            WrTD=WrTD, ATD=ATD, BsD=BsD, cb2D=cb2, **consts))

    res = run_bass_kernel_spmd(nc, in_maps, core_ids=list(range(8)))
    out = np.empty((B, S, DM), dtype=np.float32)
    for core in range(8):
        bsel, half = core // 2, core % 2
        o = np.asarray(res.results[core]["OUT"]).reshape(NCH, 128, 32, CH)
        out[bsel, half * NTOK:(half + 1) * NTOK, :] = (
            o.transpose(0, 3, 2, 1).reshape(NTOK, DM))
    return out


# version 07
# speedup vs baseline: 1.1222x; 1.0771x over previous
import numpy as np

import concourse.bass as bass
import concourse.mybir as mybir
from concourse.bass_utils import run_bass_kernel_spmd

F32 = mybir.dt.float32
BF16 = mybir.dt.bfloat16
AF = mybir.ActivationFunctionType
OP = mybir.AluOpType
AX = mybir.AxisListType

B = 4
S = 4096
DM = 4096
NTOK = 2048
CH = 256
NCH = NTOK // CH
NF = DM // 128
NMQ = 32
NMK = 8
HQ = 32
KVH = 8
MODC = 6 * DM
NPAN = MODC // 128
RSCALE = float(np.float32(1.0) / np.float32(np.sqrt(np.float32(128.0))))
EPS = 1e-5
SCALE_FFN = 4.0


class Builder:

    def __init__(self, mode, events=None, engines=None, sems=None):
        self.mode = mode
        self.events = events if events is not None else {}
        self.engines = engines
        self.sems = sems
        self.counts = {"pe": 0, "dve": 0, "act": 0, "syn": 0, "gp": 0}

    CHAIN = ("dve", "act")

    def op(self, eng, kind, key=None, waits=(), sem=None, **args):
        track = sem if kind == "dma" else eng
        if self.mode == "count":
            if eng in self.CHAIN:
                self.counts[eng] = self.counts.get(eng, 0) + 1
                if key is not None:
                    assert key not in self.events, key
                    self.events[key] = (eng, self.counts[eng])
            elif key is not None:
                self.counts[track] = self.counts.get(track, 0) + (
                    16 if kind == "dma" else 1)
                assert key not in self.events, key
                self.events[key] = (track, self.counts[track])
            return
        if self.mode != eng:
            return
        h = self.engines[eng]
        for k in waits:
            peng, val = self.events[k]
            if peng != eng or peng not in ("pe", "dve", "act"):
                h.wait_ge(self.sems[peng], val)
        if eng in self.CHAIN:
            pre = self.counts.get(eng, 0)
            self.counts[eng] = pre + 1
            if pre > 0:
                h.wait_ge(self.sems[eng], pre)
        if kind == "dma":
            pre = self.events[key][1] - 16
            if pre > 0:
                h.wait_ge(self.sems[sem], pre)
            ins = h.dma_start(out=args["out"], in_=args["in_"])
        elif kind == "mm":
            ins = h.matmul(args["out"], args["lhsT"], args["rhs"],
                           start=args["start"], stop=args["stop"],
                           tile_position=args.get("tp"))
        elif kind == "tp":
            ins = h.transpose(args["out"], args["in_"], args["ident"])
        elif kind == "act":
            ins = h.activation(args["out"], args["in_"], args["func"],
                               bias=args.get("bias", 0.0), scale=args.get("scale", 1.0))
        elif kind == "sq":
            ins = h.square(args["out"], args["in_"])
        elif kind == "scp":
            ins = h.copy(args["out"], args["in_"])
        elif kind == "cp":
            ins = h.tensor_copy(args["out"], args["in_"])
        elif kind == "tt":
            ins = h.tensor_tensor(args["out"], args["in0"], args["in1"], args["op"])
        elif kind == "ts":
            ins = h.tensor_scalar(args["out"], args["in0"], args["s1"],
                                  args.get("s2"), args["op0"],
                                  args.get("op1", OP.bypass))
        elif kind == "tss":
            ins = h.tensor_single_scalar(args["out"], args["in_"], args["s"], args["op"])
        elif kind == "stt":
            ins = h.scalar_tensor_tensor(args["out"], args["in0"], args["s"],
                                         args["in1"], args["op0"], args["op1"])
        elif kind == "red":
            ins = h.tensor_reduce(args["out"], args["in_"], axis=AX.X, op=args["op"],
                                  negate=args.get("negate", False))
        elif kind == "rec":
            ins = h.reciprocal(args["out"], args["in_"])
        elif kind == "bar":
            return
        else:
            raise ValueError(kind)
        if eng in self.CHAIN:
            ins.then_inc(self.sems[eng], 1)
        elif key is not None:
            ins.then_inc(self.sems[track], 16 if kind == "dma" else 1)

    def dma(self, eng, out, in_, key=None, waits=(), sem=None):
        assert key is not None and sem is not None
        self.op(eng, "dma", key=key, waits=waits, sem=sem, out=out, in_=in_)

    def mm(self, out, lhsT, rhs, start, stop, key=None, waits=(), tp=None):
        self.op("pe", "mm", key=key, waits=waits, out=out, lhsT=lhsT, rhs=rhs,
                start=start, stop=stop, tp=tp)

    def tp(self, out, in_, ident, key=None, waits=()):
        self.op("pe", "tp", key=key, waits=waits, out=out, in_=in_, ident=ident)


def build_nc():
    nc = bass.Bass("TRN2", target_bir_lowering=False)

    X2d = nc.dram_tensor("X2", [NCH, 128, NF * CH], F32, kind="ExternalInput").ap()
    W2q = nc.dram_tensor("W2q", [NMQ, 128, NF * 128], F32, kind="ExternalInput").ap()
    W2k = nc.dram_tensor("W2k", [NMK, 128, NF * 128], F32, kind="ExternalInput").ap()
    W2o = nc.dram_tensor("W2o", [NMQ, 128, NF * 128], F32, kind="ExternalInput").ap()
    W2m = nc.dram_tensor("W2m", [NPAN, 128, NF * 128], F32, kind="ExternalInput").ap()
    WvD = nc.dram_tensor("WvD", [NF, 128, KVH * 128], F32, kind="ExternalInput").ap()
    WrTD = nc.dram_tensor("WrTD", [128, NF * 4], F32, kind="ExternalInput").ap()
    ATD = nc.dram_tensor("ATD", [128, NF * 64], F32, kind="ExternalInput").ap()
    BsD = nc.dram_tensor("BsD", [64, DM], F32, kind="ExternalInput").ap()
    cb2D = nc.dram_tensor("cb2D", [128, 32], F32, kind="ExternalInput").ap()
    ConeD = nc.dram_tensor("ConeD", [128, 255], F32, kind="ExternalInput").ap()
    Id128D = nc.dram_tensor("Id128D", [128, 128], F32, kind="ExternalInput").ap()
    Id4D = nc.dram_tensor("Id4D", [4, 4], F32, kind="ExternalInput").ap()
    Id1D = nc.dram_tensor("Id1D", [1, 1], F32, kind="ExternalInput").ap()
    Ones128D = nc.dram_tensor("Ones128D", [128, 1], F32, kind="ExternalInput").ap()
    Ones1xD = nc.dram_tensor("Ones1xD", [1, 128], F32, kind="ExternalInput").ap()
    Sum8D = nc.dram_tensor("Sum8D", [128, 16], F32, kind="ExternalInput").ap()
    Exp16D = nc.dram_tensor("Exp16D", [16, 128], F32, kind="ExternalInput").ap()
    Exp464D = nc.dram_tensor("Exp464D", [4, 64], F32, kind="ExternalInput").ap()
    OUTd = nc.dram_tensor("OUT", [NCH, 128, NF * CH], F32, kind="ExternalOutput").ap()
    QHL = nc.dram_tensor("QHL", [NMQ, 128, 2 * NF * 128], BF16).ap()
    KHL = nc.dram_tensor("KHL", [NMK, 128, 2 * NF * 128], BF16).ap()
    OHL = nc.dram_tensor("OHL", [NMQ, 128, 2 * NF * 128], BF16).ap()
    VHL = nc.dram_tensor("VHL", [NF, 128, 2 * KVH * 128], BF16).ap()
    ATHL = nc.dram_tensor("ATHL", [128, 2 * NF * 64], BF16).ap()

    A = nc.alloc_sbuf_tensor
    xh = A("xh", [128, NF * CH], F32).ap()
    hbh = A("hbh", [128, NF * CH], BF16).ap()
    hbl = A("hbl", [128, NF * CH], BF16).ap()
    qb = A("qb", [128, NF * CH], F32).ap()
    x2b = A("x2b", [128, NF * CH], F32).ap()
    wcA = A("wcA", [128, 2 * NF * 128], BF16).ap()
    wcB = A("wcB", [128, 2 * NF * 128], BF16).ap()
    wvA = A("wvA", [128, 2 * KVH * 128], BF16).ap()
    wvB = A("wvB", [128, 2 * KVH * 128], BF16).ap()
    wrtb = A("wrtb", [128, NF * 4], F32).ap()
    mb0 = A("mb0", [128, CH], F32).ap()
    mb1 = A("mb1", [128, CH], F32).ap()
    e1b = A("e1b", [128, CH], F32).ap()
    e2b = A("e2b", [128, CH], F32).ap()
    ptok = A("ptok", [128, 2 * 256], F32).ap()
    sqb0 = A("sqb0", [128, CH], F32).ap()
    sqb1 = A("sqb1", [128, CH], F32).ap()
    avt0 = A("avt0", [128, 128], F32).ap()
    avt1 = A("avt1", [128, 128], F32).ap()
    stt_ = A("stats", [1, 13 * CH], F32).ap()
    recb = A("recb", [16, 2 * CH], F32).ap()
    modv = A("modv", [128, NPAN], F32).ap()
    modv2 = A("modv2", [128, 128], F32).ap()
    sct = A("sct", [128, 32], F32).ap()
    cbb = A("cbb", [128, 32], F32).ap()
    gat = A("gat", [128, 96], F32).ap()
    gtt = A("gtt", [4, 2 * CH], F32).ap()
    geb = A("geb", [64, CH], F32).ap()
    gmb = A("gmb", [64, CH], F32).ap()
    cone = A("cone", [128, 255], F32).ap()
    id128 = A("id128", [128, 128], F32).ap()
    id4 = A("id4", [4, 4], F32).ap()
    id1 = A("id1", [1, 1], F32).ap()
    ones128 = A("ones128", [128, 1], F32).ap()
    ones1x = A("ones1x", [1, 128], F32).ap()
    sum8 = A("sum8", [128, 16], F32).ap()
    exp16 = A("exp16", [16, 128], F32).ap()
    exp464 = A("exp464", [4, 64], F32).ap()

    ktb = x2b[:, 0:NMK * CH]
    vtb = x2b[:, NMK * CH:NMK * CH + 2048]
    wcBf = wcB.bitcast(F32)
    mp0 = x2b[:, 0:2048]
    mp1 = x2b[:, 2048:4096]
    spi0 = wcA.bitcast(F32)[:, 0:2048]
    spi1 = wcA.bitcast(F32)[:, 2048:4096]
    spo0 = wcB[:, 0:4096]
    spo1 = wcB[:, 4096:8192]
    bsv0 = wcBf[0:64, 0:2048]
    bsv1 = wcBf[0:64, 2048:4096]
    pab = mb0
    pbb = mb1

    PS = [nc.alloc_psum_tensor(f"ps{i}", [128, 512], F32).ap() for i in range(8)]
    sems = {nm: nc.alloc_semaphore(f"sem_{nm}")
            for nm in ("pe", "dve", "act", "wa", "wb", "va", "vb", "x", "c", "st",
                       "sw", "sw2", "ma", "mb")}

    def stv(i):
        return stt_[0:1, i * CH:(i + 1) * CH]

    def schedule(b: Builder):
        b.dma("syn", cone[:, :], ConeD[:, :], key="cst0", sem="c")
        b.dma("syn", id128[:, :], Id128D[:, :], key="cst1", sem="c")
        b.dma("syn", id4[:, :], Id4D[:, :], key="cst2", sem="c")
        b.dma("syn", id1[:, :], Id1D[:, :], key="cst3", sem="c")
        b.dma("syn", ones128[:, :], Ones128D[:, :], key="cst4", sem="c")
        b.dma("syn", ones1x[:, :], Ones1xD[:, :], key="cst5", sem="c")
        b.dma("syn", sum8[:, :], Sum8D[:, :], key="cst6", sem="c")
        b.dma("syn", exp16[:, :], Exp16D[:, :], key="cst7", sem="c")
        b.dma("syn", exp464[:, :], Exp464D[:, :], key="cst8", sem="c")
        b.dma("syn", cbb[:, :], cb2D[:, :], key="ld_c", sem="c")
        b.dma("syn", xh[:, :], X2d[0], key="ld_x_0", sem="x")
        b.op("act", "act", key="silu", waits=("ld_c",),
             out=sct[:, :], in_=cbb[:, :], func=AF.Silu)

        jobs = []
        for m in range(NMQ):
            for hf in range(2):
                jobs.append((f"spq_{m}_{hf}", W2q[m][:, hf * 2048:(hf + 1) * 2048],
                             QHL[m][:, hf * 2048:(hf + 1) * 2048],
                             QHL[m][:, 4096 + hf * 2048:4096 + (hf + 1) * 2048], 2048))
        for m in range(NMK):
            for hf in range(2):
                jobs.append((f"spk_{m}_{hf}", W2k[m][:, hf * 2048:(hf + 1) * 2048],
                             KHL[m][:, hf * 2048:(hf + 1) * 2048],
                             KHL[m][:, 4096 + hf * 2048:4096 + (hf + 1) * 2048], 2048))
        for f in range(NF):
            jobs.append((f"spv_{f}", WvD[f], VHL[f][:, 0:1024],
                         VHL[f][:, 1024:2048], 1024))
        for m in range(NMQ):
            for hf in range(2):
                jobs.append((f"spo_{m}_{hf}", W2o[m][:, hf * 2048:(hf + 1) * 2048],
                             OHL[m][:, hf * 2048:(hf + 1) * 2048],
                             OHL[m][:, 4096 + hf * 2048:4096 + (hf + 1) * 2048], 2048))
        jobs.append(("spat", ATD[:, :], ATHL[:, 0:2048], ATHL[:, 2048:4096], 2048))
        nJ = len(jobs)

        def split_job(i):
            jk, srcap, hidst, lodst, w = jobs[i]
            sl = spi0 if i % 2 == 0 else spi1
            so = spo0 if i % 2 == 0 else spo1
            ww = []
            if i >= 2:
                ww += [f"spc_{i-2}", f"spl_{i-2}"]
            b.dma("syn", sl[:, 0:w], srcap, key=f"spld_{i}", waits=ww,
                  sem=("wa" if i % 2 == 0 else "wb"))
            cw = [f"spld_{i}"]
            if i >= 2:
                cw.append(f"spst_{i-2}")
            b.op("act", "scp", key=f"spc_{i}", waits=cw,
                 out=so[:, 0:w], in_=sl[:, 0:w])
            b.op("dve", "tt", key=f"spl_{i}", waits=(f"spc_{i}",),
                 out=so[:, w:2 * w], in0=sl[:, 0:w], in1=so[:, 0:w],
                 op=OP.subtract)
            b.dma("gp", hidst, so[:, 0:w], key=f"spsh_{i}", waits=(f"spl_{i}",),
                  sem=("sw" if i % 2 == 0 else "sw2"))
            b.dma("gp", lodst, so[:, w:2 * w], key=f"spst_{i}",
                  sem=("sw" if i % 2 == 0 else "sw2"))

        def mod_panel(n):
            psm = PS[n % 2]
            for half in range(2):
                i2 = 2 * n + half
                sl = mp0 if i2 % 2 == 0 else mp1
                ww = [f"g0h_{i2-2}"] if i2 >= 2 else []
                b.dma("syn", sl[:, :], W2m[n][:, half * 2048:(half + 1) * 2048],
                      key=f"w0_{i2}", waits=ww, sem=("ma" if i2 % 2 == 0 else "mb"))
                mmw = [f"w0_{i2}", "silu"]
                if n >= 2 and half == 0:
                    mmw.append(f"c0_{n-2}")
                for kq in range(4):
                    for j in range(4):
                        kk = kq * 4 + j
                        k = half * 16 + kk
                        b.mm(psm[32 * j:32 * j + 1, 0:128], sct[:, k:k + 1],
                             sl[:, kk * 128:(kk + 1) * 128],
                             start=(half == 0 and kq == 0),
                             stop=(half == 1 and kq == 3),
                             key=(f"g0h_{i2}" if (kq == 3 and j == 3) else None),
                             waits=(mmw if (kq == 0 and j == 0) else ()),
                             tp=(0, 32 * j))
            tf = mb0[:, 0:128] if n % 2 == 0 else mb1[:, 0:128]
            tw = [f"g0h_{2 * n + 1}"]
            if n >= 2:
                tw.append(f"t0_{n-2}")
            for j in range(4):
                b.op("dve", "cp", key=(f"c0_{n}" if j == 3 else None),
                     waits=(tw if j == 0 else ()),
                     out=tf[32 * j:32 * j + 1, 0:128],
                     in_=psm[32 * j:32 * j + 1, 0:128])
            pst = PS[2 + (n % 2)]
            tpw = [f"c0_{n}"]
            if n >= 2:
                tpw.append(f"c1_{n-2}")
            b.tp(pst[0:128, 0:128], tf[:, :], id128[:, :], key=f"t0_{n}", waits=tpw)
            b.op("dve", "red", key=f"c1_{n}", waits=(f"t0_{n}",),
                 out=modv[:, n:n + 1], in_=pst[0:128, 0:128:32], op=OP.add)

        for i in range(max(nJ, NPAN)):
            if i < NPAN:
                mod_panel(i)
            if i < nJ:
                split_job(i)
        b.op("dve", "ts", waits=(f"c1_{NPAN-1}",), out=modv2[:, 0:32],
             in0=modv[:, 32:64], s1=1.0, op0=OP.add)
        b.op("dve", "ts", out=modv2[:, 64:96], in0=modv[:, 128:160], s1=1.0, op0=OP.add)
        for (src0, dst0) in ((64, 32), (160, 96)):
            b.op("dve", "tss", out=gat[:, 0:32], in_=modv[:, src0:src0 + 32], s=0.0,
                 op=OP.is_ge)
            b.op("dve", "ts", out=gat[:, 32:64], in0=gat[:, 0:32], s1=2.0, s2=-1.0,
                 op0=OP.mult, op1=OP.add)
            b.op("dve", "tt", out=gat[:, 64:96], in0=modv[:, src0:src0 + 32],
                 in1=gat[:, 32:64], op=OP.mult)
            b.op("dve", "tss", out=gat[:, 0:32], in_=gat[:, 64:96], s=0.25,
                 op=OP.max)
            b.op("dve", "tt", out=modv2[:, dst0:dst0 + 32], in0=gat[:, 0:32],
                 in1=gat[:, 32:64], op=OP.mult)
        b.op("dve", "ts", key="modv2", out=modv2[:, 96:128], in0=modv2[:, 96:128],
             s1=SCALE_FFN, op0=OP.mult)

        sa_col = modv[:, 0:32]
        sb_col = modv[:, 96:128]
        caP = modv2[:, 0:32]
        gaC = modv2[:, 32:64]
        cbP = modv2[:, 64:96]
        gbC4 = modv2[:, 96:128]

        for t in range(NCH):
            stats_w = [f"ld_x_{t}"]
            if t > 0:
                stats_w.append(f"varep2_{t-1}")
            else:
                stats_w.append(f"c0_{NPAN-1}")
            for f in range(NF):
                xt = xh[:, f * CH:(f + 1) * CH]
                sq = sqb0 if f % 2 == 0 else sqb1
                sqw = [f"ld_x_{t}"]
                if f >= 2:
                    sqw.append(f"sqmm_{t}_{f-2}")
                elif t > 0:
                    sqw.append(f"stats2_{t-1}")
                    sqw.append(f"h2_{t-1}")
                b.op("act", "sq", key=f"sq_{t}_{f}", waits=sqw, out=sq[:, :], in_=xt)
                b.mm(PS[0][0:1, 0:CH], ones128[:, 0:1], xt,
                     start=(f == 0), stop=(f == NF - 1),
                     waits=(stats_w if f == 0 else ()))
                b.mm(PS[1][0:1, 0:CH], ones128[:, 0:1], sq[:, :],
                     start=(f == 0), stop=(f == NF - 1),
                     key=(f"stats_{t}" if f == NF - 1 else f"sqmm_{t}_{f}"),
                     waits=(f"sq_{t}_{f}",))
            def rstd_chain(tag, statskey):
                b.op("dve", "ts", waits=(statskey,), out=stv(0), in0=PS[0][0:1, 0:CH],
                     s1=1.0 / DM, op0=OP.mult)
                b.op("dve", "ts", out=stv(1), in0=PS[1][0:1, 0:CH], s1=1.0 / DM,
                     op0=OP.mult)
                b.op("dve", "ts", out=stv(2), in0=stv(0), s1=-1.0, op0=OP.mult)
                b.op("dve", "tt", out=stv(3), in0=stv(0), in1=stv(0), op=OP.mult)
                b.op("dve", "tt", out=stv(4), in0=stv(1), in1=stv(3), op=OP.subtract)
                b.op("dve", "ts", key=f"varep{tag}", out=stv(5), in0=stv(4), s1=EPS,
                     op0=OP.add)
                b.op("act", "act", waits=(f"varep{tag}",), out=stv(6), in_=stv(5),
                     func=AF.Ln)
                b.op("act", "act", key=f"rstd0{tag}", out=stv(7), in_=stv(6),
                     func=AF.Exp, scale=-0.5)
                src = 7
                for it in range(2):
                    b.op("dve", "tt", waits=((f"rstd0{tag}",) if it == 0 else ()),
                         out=stv(8), in0=stv(src), in1=stv(src), op=OP.mult)
                    b.op("dve", "tt", out=stv(9), in0=stv(8), in1=stv(5), op=OP.mult)
                    b.op("dve", "ts", out=stv(10), in0=stv(9), s1=-0.5, s2=1.5,
                         op0=OP.mult, op1=OP.add)
                    dst = 11 if it == 0 else 12
                    b.op("dve", "tt", key=(f"rstd{tag}" if it == 1 else None),
                         out=stv(dst), in0=stv(src), in1=stv(10), op=OP.mult)
                    src = dst
                return src

            src = rstd_chain(f"_{t}", f"stats_{t}")
            bc_w = [f"rstd_{t}", "modv2" if t == 0 else f"gebc_{t-1}"]
            b.mm(PS[2][:, 0:CH], ones1x[0:1, :], stv(2), start=True, stop=True,
                 waits=bc_w)
            b.mm(PS[3][:, 0:CH], ones1x[0:1, :], stv(src), start=True, stop=True,
                 key=f"bcast_{t}")
            hw = ["modv2", f"bcast_{t}"]
            if t > 0:
                hw.append(f"pemid_{t-1}")
            for f in range(NF):
                xt = xh[:, f * CH:(f + 1) * CH]
                hh = hbh[:, f * CH:(f + 1) * CH]
                hl = hbl[:, f * CH:(f + 1) * CH]
                b.op("dve", "tt", waits=(hw if f == 0 else ()), out=sqb1[:, :], in0=xt,
                     in1=PS[2][:, 0:CH], op=OP.add)
                b.op("dve", "tt", out=sqb0[:, :],
                     in0=sqb1[:, :], in1=PS[3][:, 0:CH], op=OP.mult)
                b.op("dve", "ts", key=f"hf_{t}_{f}", out=sqb1[:, :],
                     in0=sqb0[:, :], s1=caP[:, f:f + 1], s2=sa_col[:, f:f + 1],
                     op0=OP.mult, op1=OP.add)
                b.op("act", "scp", key=f"hh_{t}_{f}", waits=(f"hf_{t}_{f}",),
                     out=hh, in_=sqb1[:, :])
                b.op("dve", "tt", key=(f"h_{t}" if f == NF - 1 else None),
                     waits=(f"hh_{t}_{f}",), out=hl, in0=sqb1[:, :], in1=hh,
                     op=OP.subtract)
            for (npan, WHLs, dst, tag) in ((NMQ, QHL, qb, "q"),
                                           (NMK, KHL, ktb, "k")):
                for m in range(npan):
                    wc = wcA if m % 2 == 0 else wcB
                    whi = wc[:, 0:NF * 128]
                    wlo = wc[:, NF * 128:2 * NF * 128]
                    wkey = f"w{tag}_{t}_{m}"
                    jb = 2 * m if tag == "q" else 2 * NMQ + 2 * m
                    if tag == "q":
                        if m >= 2:
                            ww = [f"peq_{t}_{m-2}"]
                        elif t == 0:
                            ww = [f"spl_{nJ-1}", f"spst_{nJ-1}", f"spst_{nJ-2}"]
                        else:
                            ww = [f"peo_{t-1}_{NMQ-2}" if m == 0
                                  else f"ped_{t-1}_{NMQ-1}"]
                    else:
                        ww = [f"peq_{t}_{NMQ-2+m}"] if m < 2 else [f"pek_{t}_{m-2}"]
                    if t == 0:
                        ww += [f"spst_{jb}", f"spst_{jb+1}"]
                    b.dma("syn", wc[:, :], WHLs[m], key=wkey, waits=ww,
                          sem=("wa" if m % 2 == 0 else "wb"))
                    psq = PS[4 + m % 2]
                    mw = [wkey, f"h_{t}"]
                    if tag == "q":
                        if m >= 2:
                            mw.append(f"evq_{t}_{m-2}")
                        elif t > 0:
                            mw.append(f"gmid_{t-1}")
                            mw.append(f"out_{t-1}_{NMQ-1}")
                            mw.append(f"st_{t-1}")
                    else:
                        mw.append(f"evq_{t}_{NMQ-2+m}" if m < 2 else f"evk_{t}_{m-2}")
                        if m < 2:
                            mw.append(f"out_{t-1}_{NMQ-1}" if t > 0 else "modv2")
                    for f in range(NF):
                        hh = hbh[:, f * CH:(f + 1) * CH]
                        hl = hbl[:, f * CH:(f + 1) * CH]
                        wh = whi[:, f * 128:(f + 1) * 128]
                        wl = wlo[:, f * 128:(f + 1) * 128]
                        for pi, (lh, rr) in enumerate(((wh, hh), (wh, hl), (wl, hh))):
                            b.mm(psq[:, 0:CH], lh, rr,
                                 start=(f == 0 and pi == 0),
                                 stop=(f == NF - 1 and pi == 2),
                                 key=(f"pe{tag}_{t}_{m}"
                                      if (f == NF - 1 and pi == 2) else None),
                                 waits=(mw if (f == 0 and pi == 0) else ()))
                    b.op("act", "scp", key=f"ev{tag}_{t}_{m}",
                         waits=(f"pe{tag}_{t}_{m}",),
                         out=dst[:, m * CH:(m + 1) * CH], in_=psq[:, 0:CH])
            for f in range(NF):
                wv = wvA if f % 2 == 0 else wvB
                wvh = wv[:, 0:1024]
                wvl = wv[:, 1024:2048]
                vw = [f"pev_{t}_{f-2}"] if f >= 2 else []
                if f < 2 and t > 0:
                    vw.append(f"pemid_{t-1}")
                if t == 0:
                    vw.append(f"spst_{2 * NMQ + 2 * NMK + f}")
                b.dma("syn", wv[:, :], VHL[f], key=f"wv_{t}_{f}", waits=vw,
                      sem=("va" if f % 2 == 0 else "vb"))
                mw0 = [f"wv_{t}_{f}"]
                if f == 0:
                    mw0.append(f"evk_{t}_{NMK-1}")
                    if t > 0:
                        mw0.append(f"out_{t-1}_{NMQ-1}")
                        mw0.append(f"av_{t-1}_1")
                for s2 in range(2):
                    lhh = hbh[:, f * CH + s2 * 128: f * CH + (s2 + 1) * 128]
                    lhl = hbl[:, f * CH + s2 * 128: f * CH + (s2 + 1) * 128]
                    for half in range(2):
                        for pi, (lh, rr) in enumerate(
                                ((lhh, wvh), (lhl, wvh), (lhh, wvl))):
                            b.mm(PS[4 + 2 * s2 + half][:, 0:512], lh,
                                 rr[:, half * 512:(half + 1) * 512],
                                 start=(f == 0 and pi == 0),
                                 stop=(f == NF - 1 and pi == 2),
                                 key=(f"pev_{t}_{f}"
                                      if (s2 == 1 and half == 1 and pi == 2) else None),
                                 waits=(mw0 if (s2 == 0 and half == 0 and pi == 0)
                                        else ()))
            for i in range(4):
                s2, half = divmod(i, 2)
                b.op("act", "scp", key=f"evv_{t}_{i}",
                     waits=((f"pev_{t}_{NF-1}",) if i == 0 else ()),
                     out=vtb[:, s2 * 1024 + half * 512: s2 * 1024 + (half + 1) * 512],
                     in_=PS[4 + i][:, 0:512])
            pair = 0
            for h in range(HQ):
                for j in range(KVH):
                    mbuf = mb0 if pair % 2 == 0 else mb1
                    gw = [f"evq_{t}_{h}", f"evk_{t}_{j}"]
                    if pair >= 2:
                        gw.append(f"gmm_{t}_{pair-2}")
                    b.op("dve", "tt", key=f"gm_{t}_{pair}", waits=gw, out=mbuf[:, :],
                         in0=qb[:, h * CH:(h + 1) * CH],
                         in1=ktb[:, j * CH:(j + 1) * CH], op=OP.mult)
                    tile = h // 16
                    r = (h % 16) * 8 + j
                    first = (h % 16 == 0 and j == 0)
                    last = (h % 16 == 15 and j == 7)
                    mmw = [f"gm_{t}_{pair}"]
                    if first:
                        mmw.append(f"varep_{t}")
                    b.mm(PS[tile][:, 0:CH], cone[:, 127 - r:255 - r], mbuf[:, :],
                         start=first, stop=last, key=f"gmm_{t}_{pair}", waits=mmw)
                    pair += 1
            b.op("act", "act", waits=(f"gmm_{t}_{128 * 1 - 1}",), out=e1b[:, :],
                 in_=PS[0][:, 0:CH], func=AF.Exp)
            b.op("act", "act", key=f"exp_{t}", waits=(f"gmm_{t}_{255}",),
                 out=e2b[:, :], in_=PS[1][:, 0:CH], func=AF.Exp)
            b.mm(PS[2][0:16, 0:CH], sum8[:, :], e1b[:, :], start=True, stop=True,
                 waits=[f"exp_{t}", f"h_{t}"])
            b.mm(PS[3][0:16, 0:CH], sum8[:, :], e2b[:, :], start=True, stop=True,
                 key=f"gs_{t}")
            b.op("dve", "rec", waits=(f"gs_{t}",), out=recb[0:16, 0:CH],
                 in_=PS[2][0:16, 0:CH])
            b.op("dve", "rec", key=f"rec_{t}", out=recb[0:16, CH:2 * CH],
                 in_=PS[3][0:16, 0:CH])
            b.mm(PS[4][:, 0:CH], exp16[:, :], recb[0:16, 0:CH], start=True, stop=True,
                 waits=[f"rec_{t}", f"evv_{t}_3"])
            b.mm(PS[5][:, 0:CH], exp16[:, :], recb[0:16, CH:2 * CH], start=True,
                 stop=True, key=f"pexp_{t}")
            b.op("dve", "tt", waits=(f"pexp_{t}",), out=pab[:, :], in0=e1b[:, :],
                 in1=PS[4][:, 0:CH], op=OP.mult)
            b.op("dve", "tt", key=f"pnorm_{t}", out=pbb[:, :], in0=e2b[:, :],
                 in1=PS[5][:, 0:CH], op=OP.mult)
            for s2 in range(2):
                tpa = [f"pnorm_{t}"] if s2 == 0 else [f"ptkb_{t}_0"]
                if s2 == 0:
                    tpa.append(f"evv_{t}_3")
                b.tp(PS[6][:, s2 * 128:(s2 + 1) * 128], pab[:, s2 * 128:(s2 + 1) * 128],
                     id128[:, :], waits=tpa)
                b.tp(PS[7][:, s2 * 128:(s2 + 1) * 128], pbb[:, s2 * 128:(s2 + 1) * 128],
                     id128[:, :], key=f"ptp_{t}_{s2}")
                b.op("dve", "cp", key=f"ptka_{t}_{s2}", waits=(f"ptp_{t}_{s2}",),
                     out=ptok[:, s2 * 256:s2 * 256 + 128],
                     in_=PS[6][:, s2 * 128:(s2 + 1) * 128])
                b.op("dve", "cp", key=f"ptkb_{t}_{s2}",
                     out=ptok[:, s2 * 256 + 128:(s2 + 1) * 256],
                     in_=PS[7][:, s2 * 128:(s2 + 1) * 128])
            for s2 in range(2):
                for h in range(HQ):
                    pcol0 = s2 * 256 + (0 if h < 16 else 128) + (h % 16) * 8
                    avw = [f"ptkb_{t}_1"] if (s2 == 0 and h == 0) else ()
                    b.op("dve", "ts", waits=avw, out=avt0[:, :],
                         in0=vtb[:, s2 * 1024:s2 * 1024 + 128],
                         s1=ptok[:, pcol0:pcol0 + 1], op0=OP.mult)
                    cur = avt0
                    for j in range(1, 8):
                        if j == 7:
                            dsta = qb[:, h * CH + s2 * 128: h * CH + (s2 + 1) * 128]
                        else:
                            dsta = avt1[:, :] if j % 2 == 1 else avt0[:, :]
                        b.op("dve", "stt",
                             key=(f"av_{t}_{s2}" if (j == 7 and h == HQ - 1) else None),
                             out=dsta,
                             in0=vtb[:, s2 * 1024 + j * 128:s2 * 1024 + (j + 1) * 128],
                             s=ptok[:, pcol0 + j:pcol0 + j + 1],
                             in1=cur[:, :], op0=OP.mult, op1=OP.add)
                        cur = avt1 if j % 2 == 1 else avt0
            grp = 0
            for s2 in range(2):
                for hh in range(HQ):
                    g, q4 = divmod(grp, 4)
                    psb = PS[4 + (g % 2)]
                    tw = []
                    if q4 == 0:
                        if g == 0:
                            tw = [f"av_{t}_0", f"pnorm_{t}"]
                        elif g == 8:
                            tw = [f"av_{t}_1"]
                        if g >= 2:
                            tw.append(f"avev_{t}_{g-2}")
                    b.tp(psb[:, q4 * 128:(q4 + 1) * 128],
                         qb[:, hh * CH + s2 * 128: hh * CH + (s2 + 1) * 128],
                         id128[:, :], key=(f"avtp_{t}_{g}" if q4 == 3 else None),
                         waits=tw)
                    if q4 == 3:
                        for u in range(4):
                            gi = g * 4 + u
                            ss2, hh2 = gi // HQ, gi % HQ
                            c0 = hh2 * CH + ss2 * 128
                            b.op("act", "scp",
                                 key=(f"avevh_{t}_{g}" if u == 3 else None),
                                 waits=((f"avtp_{t}_{g}",) if u == 0 else ()),
                                 out=hbh[:, c0:c0 + 128],
                                 in_=psb[:, u * 128:(u + 1) * 128])
                        for u in range(4):
                            gi = g * 4 + u
                            ss2, hh2 = gi // HQ, gi % HQ
                            c0 = hh2 * CH + ss2 * 128
                            b.op("dve", "tt",
                                 key=(f"avev_{t}_{g}" if u == 3 else None),
                                 waits=((f"avevh_{t}_{g}",) if u == 0 else ()),
                                 out=hbl[:, c0:c0 + 128],
                                 in0=psb[:, u * 128:(u + 1) * 128],
                                 in1=hbh[:, c0:c0 + 128], op=OP.subtract)
                    grp += 1
            nG = grp // 4
            for m in range(NMQ):
                wc = wcA if m % 2 == 0 else wcB
                whi = wc[:, 0:NF * 128]
                wlo = wc[:, NF * 128:2 * NF * 128]
                ww = [f"pek_{t}_{NMK-2+m}"] if m < 2 else [f"peo_{t}_{m-2}"]
                if t == 0:
                    jb = 2 * NMQ + 2 * NMK + NF + 2 * m
                    ww += [f"spst_{jb}", f"spst_{jb+1}"]
                b.dma("syn", wc[:, :], OHL[m], key=f"wo_{t}_{m}", waits=ww,
                      sem=("wa" if m % 2 == 0 else "wb"))
                pso = PS[6 + m % 2]
                mw = [f"wo_{t}_{m}"]
                if m < 2:
                    mw += [f"avev_{t}_{nG-1}", f"ptkb_{t}_1"]
                else:
                    mw.append(f"x2_{t}_{m-2}")
                for f in range(NF):
                    aoh = hbh[:, f * CH:(f + 1) * CH]
                    aol = hbl[:, f * CH:(f + 1) * CH]
                    wh = whi[:, f * 128:(f + 1) * 128]
                    wl = wlo[:, f * 128:(f + 1) * 128]
                    for pi, (lh, rr) in enumerate(((wh, aoh), (wh, aol), (wl, aoh))):
                        b.mm(pso[:, 0:CH], lh, rr,
                             start=(f == 0 and pi == 0),
                             stop=(f == NF - 1 and pi == 2),
                             key=(f"peo_{t}_{m}" if (f == NF - 1 and pi == 2) else None),
                             waits=(mw if (f == 0 and pi == 0) else ()))
                x2w = [f"peo_{t}_{m}"]
                if m < 2:
                    x2w.append(f"gmm_{t}_{255}")
                b.op("dve", "stt", key=f"x2_{t}_{m}", waits=x2w,
                     out=x2b[:, m * CH:(m + 1) * CH], in0=pso[:, 0:CH],
                     s=gaC[:, m:m + 1], in1=xh[:, m * CH:(m + 1) * CH],
                     op0=OP.mult, op1=OP.add)
                sq = sqb0 if m % 2 == 0 else sqb1
                sqw = [f"x2_{t}_{m}"]
                if m >= 2:
                    sqw.append(f"sqmm2_{t}_{m-2}")
                b.op("act", "sq", key=f"sq2_{t}_{m}", waits=sqw, out=sq[:, :],
                     in_=x2b[:, m * CH:(m + 1) * CH])
                st_w = [f"x2_{t}_{m}"]
                if m == 0:
                    st_w.append(f"exp_{t}")
                b.mm(PS[0][0:1, 0:CH], ones128[:, 0:1], x2b[:, m * CH:(m + 1) * CH],
                     start=(m == 0), stop=(m == NMQ - 1), waits=st_w)
                b.mm(PS[1][0:1, 0:CH], ones128[:, 0:1], sq[:, :],
                     start=(m == 0), stop=(m == NMQ - 1),
                     key=(f"stats2_{t}" if m == NMQ - 1 else f"sqmm2_{t}_{m}"),
                     waits=(f"sq2_{t}_{m}",))
            src = rstd_chain(f"2_{t}", f"stats2_{t}")
            b.mm(PS[2][:, 0:CH], ones1x[0:1, :], stv(2), start=True, stop=True,
                 waits=[f"rstd2_{t}", f"rec_{t}"])
            b.mm(PS[3][:, 0:CH], ones1x[0:1, :], stv(src), start=True, stop=True,
                 key=f"bcast2_{t}")
            for f in range(NF):
                x2t = x2b[:, f * CH:(f + 1) * CH]
                ht = xh[:, f * CH:(f + 1) * CH]
                b.op("dve", "tt", waits=((f"bcast2_{t}",) if f == 0 else ()),
                     out=sqb0[:, :], in0=x2t, in1=PS[2][:, 0:CH], op=OP.add)
                b.op("dve", "tt", out=sqb1[:, :], in0=sqb0[:, :], in1=PS[3][:, 0:CH],
                     op=OP.mult)
                b.op("dve", "ts", key=f"h2f_{t}_{f}", out=ht,
                     in0=sqb1[:, :], s1=cbP[:, f:f + 1], s2=sb_col[:, f:f + 1],
                     op0=OP.mult, op1=OP.add)
                b.op("act", "scp", key=f"h2h_{t}_{f}", waits=(f"h2f_{t}_{f}",),
                     out=hbh[:, f * CH:(f + 1) * CH], in_=ht)
                b.op("dve", "tt", key=(f"h2_{t}" if f == NF - 1 else None),
                     waits=(f"h2h_{t}_{f}",),
                     out=hbl[:, f * CH:(f + 1) * CH], in0=ht,
                     in1=hbh[:, f * CH:(f + 1) * CH], op=OP.subtract)
            atw = [f"pev_{t}_{NF-1}"]
            if t == 0:
                atw.append(f"spst_{nJ-1}")
            b.dma("syn", wvA[:, :], ATHL[:, 0:2048], key=f"at1_{t}", waits=atw,
                  sem="va")
            b.dma("syn", wvB[:, :], ATHL[:, 2048:4096], key=f"at_{t}", waits=atw,
                  sem="vb")
            if t == 0:
                b.dma("syn", wrtb[:, :], WrTD[:, :], key="wrt_0", sem="c")
            bsw = [f"peo_{t}_{NMQ-1}"]
            b.dma("syn", bsv0[:, :], BsD[:, 0:2048], key=f"bs0_{t}", waits=bsw,
                  sem="wa")
            b.dma("syn", bsv1[:, :], BsD[:, 2048:4096], key=f"bs_{t}",
                  waits=bsw, sem="wb")
            mw = [f"h2_{t}", f"at1_{t}", f"at_{t}", f"avev_{t}_{nG-1}",
                  f"pnorm_{t}"]
            if t == 0:
                mw.append("wrt_0")
            for f in range(NF):
                ath = wvA[:, f * 64:(f + 1) * 64]
                atl = wvB[:, f * 64:(f + 1) * 64]
                h2h = hbh[:, f * CH:(f + 1) * CH]
                h2l = hbl[:, f * CH:(f + 1) * CH]
                for pi, (lh, rr) in enumerate(((ath, h2h), (ath, h2l), (atl, h2h))):
                    b.mm(PS[4][0:64, 0:CH], lh, rr,
                         start=(f == 0 and pi == 0), stop=(f == NF - 1 and pi == 2),
                         key=(f"pemid_{t}" if (f == NF - 1 and pi == 2) else None),
                         waits=(mw if (f == 0 and pi == 0) else ()))
                b.mm(PS[5][0:4, 0:CH], wrtb[:, f * 4:(f + 1) * 4],
                     xh[:, f * CH:(f + 1) * CH], start=(f == 0), stop=(f == NF - 1),
                     key=(f"pelg_{t}" if f == NF - 1 else None))
            b.op("dve", "cp", key=f"lsb_{t}", waits=(f"pelg_{t}",),
                 out=gtt[0:4, 0:CH], in_=PS[5][0:4, 0:CH])
            for s2 in range(2):
                tw = [f"lsb_{t}"] if s2 == 0 else [f"gtt_{t}_0"]
                b.tp(PS[5][:, 256 + 4 * s2:256 + 4 * (s2 + 1)],
                     gtt[0:4, s2 * 128:(s2 + 1) * 128], id4[0:4, 0:4],
                     key=f"lgt_{t}_{s2}", waits=tw)
                L = gat[:, 0:4]
                GT = gat[:, 4:8]
                RK = gat[:, 8:12]
                MS = gat[:, 12:16]
                E4 = gat[:, 16:20]
                EM = gat[:, 20:24]
                M4 = gat[:, 24:25]
                S4 = gat[:, 25:26]
                R4 = gat[:, 26:27]
                GS = gat[:, 28 + 4 * s2:32 + 4 * s2]
                b.op("dve", "cp", waits=(f"lgt_{t}_{s2}",), out=L,
                     in_=PS[5][:, 256 + 4 * s2:256 + 4 * (s2 + 1)])
                for e in range(4):
                    b.op("dve", "tss", out=GT, in_=L, s=L[:, e:e + 1], op=OP.is_gt)
                    b.op("dve", "red", out=RK[:, e:e + 1], in_=GT, op=OP.add)
                    if e > 0:
                        b.op("dve", "tss", out=GT[:, 0:e], in_=L[:, 0:e],
                             s=L[:, e:e + 1], op=OP.is_equal)
                        b.op("dve", "red", out=M4, in_=GT[:, 0:e], op=OP.add)
                        b.op("dve", "tt", out=RK[:, e:e + 1], in0=RK[:, e:e + 1],
                             in1=M4, op=OP.add)
                b.op("dve", "tss", out=MS, in_=RK, s=1.5, op=OP.is_lt)
                b.op("dve", "red", key=f"m4_{t}_{s2}", out=M4, in_=L, op=OP.max,
                     negate=True)
                b.op("act", "act", key=f"e4_{t}_{s2}", waits=(f"m4_{t}_{s2}",),
                     out=E4, in_=L, func=AF.Exp, bias=M4)
                b.op("dve", "tt", waits=(f"e4_{t}_{s2}",), out=EM, in0=E4, in1=MS,
                     op=OP.mult)
                b.op("dve", "red", out=S4, in_=EM, op=OP.add)
                b.op("dve", "rec", out=R4, in_=S4)
                b.op("dve", "ts", key=f"gates_{t}_{s2}", out=GS, in0=EM,
                     s1=R4[:, 0:1], op0=OP.mult)
                b.tp(PS[5][0:4, s2 * 128:(s2 + 1) * 128], GS, id128[:, :],
                     key=f"gt_{t}_{s2}", waits=(f"gates_{t}_{s2}",))
                b.op("dve", "cp", key=f"gtt_{t}_{s2}", waits=(f"gt_{t}_{s2}",),
                     out=gtt[0:4, CH + s2 * 128:CH + (s2 + 1) * 128],
                     in_=PS[5][0:4, s2 * 128:(s2 + 1) * 128])
            b.mm(PS[2][0:64, 256:256 + CH], exp464[0:4, :], gtt[0:4, CH:2 * CH],
                 start=True, stop=True, key=f"ge_{t}",
                 waits=[f"gtt_{t}_1", f"h2_{t}"])
            b.op("dve", "cp", key=f"gebc_{t}", waits=(f"ge_{t}",), out=geb[:, :],
                 in_=PS[2][0:64, 256:256 + CH])
            b.op("dve", "tt", key=f"gmid_{t}", waits=(f"pemid_{t}",), out=gmb[:, :],
                 in0=PS[4][0:64, 0:CH], in1=geb[:, :], op=OP.mult)
            for m in range(NMQ):
                psd = PS[6 + m % 2]
                bsl = bsv0 if m < 16 else bsv1
                mw = [f"gmid_{t}", f"bs0_{t}" if m < 16 else f"bs_{t}"]
                mw.append(f"x2_{t}_{NMQ-1}" if m < 2 else f"out_{t}_{m-2}")
                b.mm(psd[:, 0:CH], bsl[:, (m % 16) * 128:(m % 16 + 1) * 128],
                     gmb[:, :], start=True, stop=True, key=f"ped_{t}_{m}", waits=mw)
                ow = [f"ped_{t}_{m}"]
                if m == 0 and t > 0:
                    ow.append(f"st_{t-1}")
                b.op("dve", "stt", key=f"out_{t}_{m}", waits=ow,
                     out=qb[:, m * CH:(m + 1) * CH], in0=psd[:, 0:CH],
                     s=gbC4[:, m:m + 1], in1=x2b[:, m * CH:(m + 1) * CH],
                     op0=OP.mult, op1=OP.add)
            b.dma("gp", OUTd[t], qb[:, :], key=f"st_{t}", waits=(f"out_{t}_{NMQ-1}",), sem="st")
            if t + 1 < NCH:
                b.dma("syn", xh[:, :], X2d[t + 1], key=f"ld_x_{t+1}",
                      waits=(f"pelg_{t}", f"h2_{t}"), sem="x")

    cb = Builder("count")
    schedule(cb)
    events = cb.events

    with nc.Block() as block:

        @block.sync
        def _(sync):
            schedule(Builder("syn", events, {"syn": sync}, sems))

        @block.tensor
        def _(te):
            schedule(Builder("pe", events, {"pe": te}, sems))

        @block.vector
        def _(ve):
            schedule(Builder("dve", events, {"dve": ve}, sems))

        @block.scalar
        def _(sc):
            schedule(Builder("act", events, {"act": sc}, sems))

        @block.gpsimd
        def _(gp):
            schedule(Builder("gp", events, {"gp": gp}, sems))
            gp.wait_ge(sems["st"], events[f"st_{NCH-1}"][1])

    return nc


def _consts():
    cone = np.zeros((128, 255), dtype=np.float32)
    cone[:, 127] = np.float32(RSCALE)
    sum8 = np.zeros((128, 16), dtype=np.float32)
    for p in range(128):
        sum8[p, p // 8] = 1.0
    exp16 = np.zeros((16, 128), dtype=np.float32)
    for p in range(128):
        exp16[p // 8, p] = 1.0
    exp464 = np.zeros((4, 64), dtype=np.float32)
    for e in range(4):
        exp464[e, e * 16:(e + 1) * 16] = 1.0
    return dict(ConeD=cone, Id128D=np.eye(128, dtype=np.float32),
                Id4D=np.eye(4, dtype=np.float32), Id1D=np.eye(1, dtype=np.float32),
                Ones128D=np.ones((128, 1), dtype=np.float32),
                Ones1xD=np.ones((1, 128), dtype=np.float32),
                Sum8D=sum8, Exp16D=exp16, Exp464D=exp464)


def _stage(x, Wq, Wk, Wv, Wo, Af, Bf, Wr, c, wmod):
    W2q = np.ascontiguousarray(
        Wq.reshape(32, 128, 32, 128).transpose(2, 1, 0, 3)).reshape(32, 128, 32 * 128)
    W2k = np.ascontiguousarray(
        Wk.reshape(32, 128, 8, 128).transpose(2, 1, 0, 3)).reshape(8, 128, 32 * 128)
    W2o = np.ascontiguousarray(
        Wo.reshape(32, 128, 32, 128).transpose(2, 1, 0, 3)).reshape(32, 128, 32 * 128)
    W2m = np.ascontiguousarray(
        wmod.reshape(32, 128, 192, 128).transpose(2, 1, 0, 3)).reshape(192, 128, 32 * 128)
    WvD = np.ascontiguousarray(Wv.reshape(32, 128, 1024))
    WrTD = np.ascontiguousarray(
        Wr.reshape(4, 32, 128).transpose(2, 1, 0)).reshape(128, 128)
    ATD = np.ascontiguousarray(
        Af.transpose(2, 0, 1).reshape(32, 128, 64).transpose(1, 0, 2)).reshape(128, 2048)
    BsD = np.ascontiguousarray(Bf.transpose(0, 2, 1).reshape(64, 4096))
    return W2q, W2k, W2o, W2m, WvD, WrTD, ATD, BsD


def kernel(L0_attn_hidden_states, L0_attn_Wq, L0_attn_Wk, L0_attn_Wv, L0_attn_Wo,
           L0_ffn_A, L0_ffn_B, L0_ffn_Wr, L0_c, L0_wmod):
    x = np.ascontiguousarray(L0_attn_hidden_states, dtype=np.float32)
    Wq = np.ascontiguousarray(L0_attn_Wq, dtype=np.float32)
    Wk = np.ascontiguousarray(L0_attn_Wk, dtype=np.float32)
    Wv = np.ascontiguousarray(L0_attn_Wv, dtype=np.float32)
    Wo = np.ascontiguousarray(L0_attn_Wo, dtype=np.float32)
    Af = np.ascontiguousarray(L0_ffn_A, dtype=np.float32)
    Bf = np.ascontiguousarray(L0_ffn_B, dtype=np.float32)
    Wr = np.ascontiguousarray(L0_ffn_Wr, dtype=np.float32)
    c = np.ascontiguousarray(L0_c, dtype=np.float32)
    wmod = np.ascontiguousarray(L0_wmod, dtype=np.float32)

    W2q, W2k, W2o, W2m, WvD, WrTD, ATD, BsD = _stage(x, Wq, Wk, Wv, Wo, Af, Bf, Wr,
                                                     c, wmod)
    consts = _consts()
    nc = build_nc()
    in_maps = []
    for core in range(8):
        bsel, half = core // 2, core % 2
        xs = x[bsel, half * NTOK:(half + 1) * NTOK, :]
        X2 = np.ascontiguousarray(
            xs.reshape(NCH, CH, 32, 128).transpose(0, 3, 2, 1)).reshape(NCH, 128,
                                                                        32 * CH)
        cb2 = np.ascontiguousarray(c[bsel].reshape(32, 128).T)
        in_maps.append(dict(X2=X2, W2q=W2q, W2k=W2k, W2o=W2o, W2m=W2m, WvD=WvD,
                            WrTD=WrTD, ATD=ATD, BsD=BsD, cb2D=cb2, **consts))

    res = run_bass_kernel_spmd(nc, in_maps, core_ids=list(range(8)))
    out = np.empty((B, S, DM), dtype=np.float32)
    for core in range(8):
        bsel, half = core // 2, core % 2
        o = np.asarray(res.results[core]["OUT"]).reshape(NCH, 128, 32, CH)
        out[bsel, half * NTOK:(half + 1) * NTOK, :] = (
            o.transpose(0, 3, 2, 1).reshape(NTOK, DM))
    return out


# version 08 (final)
# speedup vs baseline: 1.1843x; 1.0553x over previous
import numpy as np

import concourse.bass as bass
import concourse.mybir as mybir
from concourse.bass_utils import run_bass_kernel_spmd

F32 = mybir.dt.float32
BF16 = mybir.dt.bfloat16
AF = mybir.ActivationFunctionType
OP = mybir.AluOpType
AX = mybir.AxisListType

B = 4
S = 4096
DM = 4096
NTOK = 2048
CH = 256
NCH = NTOK // CH
NF = DM // 128
NMQ = 32
NMK = 8
HQ = 32
KVH = 8
MODC = 6 * DM
NPAN = MODC // 128
RSCALE = float(np.float32(1.0) / np.float32(np.sqrt(np.float32(128.0))))
EPS = 1e-5
SCALE_FFN = 4.0


class Builder:

    def __init__(self, mode, events=None, engines=None, sems=None):
        self.mode = mode
        self.events = events if events is not None else {}
        self.engines = engines
        self.sems = sems
        self.counts = {"pe": 0, "dve": 0, "act": 0, "syn": 0, "gp": 0}

    CHAIN = ("dve", "act")

    def op(self, eng, kind, key=None, waits=(), sem=None, **args):
        track = sem if kind == "dma" else eng
        if self.mode == "count":
            if eng in self.CHAIN:
                self.counts[eng] = self.counts.get(eng, 0) + 1
                if key is not None:
                    assert key not in self.events, key
                    self.events[key] = (eng, self.counts[eng])
            elif key is not None:
                self.counts[track] = self.counts.get(track, 0) + (
                    16 if kind == "dma" else 1)
                assert key not in self.events, key
                self.events[key] = (track, self.counts[track])
            return
        if self.mode != eng:
            return
        h = self.engines[eng]
        for k in waits:
            peng, val = self.events[k]
            if peng != eng or peng not in ("pe", "dve", "act"):
                h.wait_ge(self.sems[peng], val)
        if eng in self.CHAIN:
            pre = self.counts.get(eng, 0)
            self.counts[eng] = pre + 1
            if pre > 0:
                h.wait_ge(self.sems[eng], pre)
        if kind == "dma":
            pre = self.events[key][1] - 16
            if pre > 0:
                h.wait_ge(self.sems[sem], pre)
            ins = h.dma_start(out=args["out"], in_=args["in_"])
        elif kind == "mm":
            ins = h.matmul(args["out"], args["lhsT"], args["rhs"],
                           start=args["start"], stop=args["stop"],
                           tile_position=args.get("tp"))
        elif kind == "tp":
            ins = h.transpose(args["out"], args["in_"], args["ident"])
        elif kind == "act":
            ins = h.activation(args["out"], args["in_"], args["func"],
                               bias=args.get("bias", 0.0), scale=args.get("scale", 1.0))
        elif kind == "sq":
            ins = h.square(args["out"], args["in_"])
        elif kind == "scp":
            ins = h.copy(args["out"], args["in_"])
        elif kind == "cp":
            ins = h.tensor_copy(args["out"], args["in_"])
        elif kind == "tt":
            ins = h.tensor_tensor(args["out"], args["in0"], args["in1"], args["op"])
        elif kind == "ts":
            ins = h.tensor_scalar(args["out"], args["in0"], args["s1"],
                                  args.get("s2"), args["op0"],
                                  args.get("op1", OP.bypass))
        elif kind == "tss":
            ins = h.tensor_single_scalar(args["out"], args["in_"], args["s"], args["op"])
        elif kind == "stt":
            ins = h.scalar_tensor_tensor(args["out"], args["in0"], args["s"],
                                         args["in1"], args["op0"], args["op1"])
        elif kind == "red":
            ins = h.tensor_reduce(args["out"], args["in_"], axis=AX.X, op=args["op"],
                                  negate=args.get("negate", False))
        elif kind == "rec":
            ins = h.reciprocal(args["out"], args["in_"])
        elif kind == "bar":
            return
        else:
            raise ValueError(kind)
        if eng in self.CHAIN:
            ins.then_inc(self.sems[eng], 1)
        elif key is not None:
            ins.then_inc(self.sems[track], 16 if kind == "dma" else 1)

    def dma(self, eng, out, in_, key=None, waits=(), sem=None):
        assert key is not None and sem is not None
        self.op(eng, "dma", key=key, waits=waits, sem=sem, out=out, in_=in_)

    def mm(self, out, lhsT, rhs, start, stop, key=None, waits=(), tp=None):
        self.op("pe", "mm", key=key, waits=waits, out=out, lhsT=lhsT, rhs=rhs,
                start=start, stop=stop, tp=tp)

    def tp(self, out, in_, ident, key=None, waits=()):
        self.op("pe", "tp", key=key, waits=waits, out=out, in_=in_, ident=ident)


def build_nc():
    nc = bass.Bass("TRN2", target_bir_lowering=False)

    X2d = nc.dram_tensor("X2", [NCH, 128, NF * CH], F32, kind="ExternalInput").ap()
    W2q = nc.dram_tensor("W2q", [NMQ, 128, NF * 128], F32, kind="ExternalInput").ap()
    W2k = nc.dram_tensor("W2k", [NMK, 128, NF * 128], F32, kind="ExternalInput").ap()
    W2o = nc.dram_tensor("W2o", [NMQ, 128, NF * 128], F32, kind="ExternalInput").ap()
    W2m = nc.dram_tensor("W2m", [NPAN, 128, NF * 128], F32, kind="ExternalInput").ap()
    WvD = nc.dram_tensor("WvD", [NF, 128, KVH * 128], F32, kind="ExternalInput").ap()
    WrTD = nc.dram_tensor("WrTD", [128, NF * 4], F32, kind="ExternalInput").ap()
    ATD = nc.dram_tensor("ATD", [128, NF * 64], F32, kind="ExternalInput").ap()
    BsD = nc.dram_tensor("BsD", [64, DM], F32, kind="ExternalInput").ap()
    cb2D = nc.dram_tensor("cb2D", [128, 32], F32, kind="ExternalInput").ap()
    ConeD = nc.dram_tensor("ConeD", [128, 255], F32, kind="ExternalInput").ap()
    Id128D = nc.dram_tensor("Id128D", [128, 128], F32, kind="ExternalInput").ap()
    Id4D = nc.dram_tensor("Id4D", [4, 4], F32, kind="ExternalInput").ap()
    Id1D = nc.dram_tensor("Id1D", [1, 1], F32, kind="ExternalInput").ap()
    Ones128D = nc.dram_tensor("Ones128D", [128, 1], F32, kind="ExternalInput").ap()
    Ones1xD = nc.dram_tensor("Ones1xD", [1, 128], F32, kind="ExternalInput").ap()
    Sum8D = nc.dram_tensor("Sum8D", [128, 16], F32, kind="ExternalInput").ap()
    Exp16D = nc.dram_tensor("Exp16D", [16, 128], F32, kind="ExternalInput").ap()
    Exp464D = nc.dram_tensor("Exp464D", [4, 64], F32, kind="ExternalInput").ap()
    OUTd = nc.dram_tensor("OUT", [NCH, 128, NF * CH], F32, kind="ExternalOutput").ap()
    QHL = nc.dram_tensor("QHL", [NMQ, 128, 2 * NF * 128], BF16).ap()
    KHL = nc.dram_tensor("KHL", [NMK, 128, 2 * NF * 128], BF16).ap()
    OHL = nc.dram_tensor("OHL", [NMQ, 128, 2 * NF * 128], BF16).ap()
    VHL = nc.dram_tensor("VHL", [NF, 128, 2 * KVH * 128], BF16).ap()
    ATHL = nc.dram_tensor("ATHL", [128, 2 * NF * 64], BF16).ap()

    A = nc.alloc_sbuf_tensor
    xh = A("xh", [128, NF * CH], F32).ap()
    hbh = A("hbh", [128, NF * CH], BF16).ap()
    hbl = A("hbl", [128, NF * CH], BF16).ap()
    qb = A("qb", [128, NF * CH], F32).ap()
    x2b = A("x2b", [128, NF * CH], F32).ap()
    wcA = A("wcA", [128, 2 * NF * 128], BF16).ap()
    wcB = A("wcB", [128, 2 * NF * 128], BF16).ap()
    wvA = A("wvA", [128, 2 * KVH * 128], BF16).ap()
    wvB = A("wvB", [128, 2 * KVH * 128], BF16).ap()
    wrtb = A("wrtb", [128, NF * 4], F32).ap()
    mb0 = A("mb0", [128, CH], F32).ap()
    mb1 = A("mb1", [128, CH], F32).ap()
    e1b = A("e1b", [128, CH], F32).ap()
    e2b = A("e2b", [128, CH], F32).ap()
    ptok = A("ptok", [128, 2 * 256], F32).ap()
    sqb0 = A("sqb0", [128, CH], F32).ap()
    sqb1 = A("sqb1", [128, CH], F32).ap()
    avt0 = A("avt0", [128, 128], F32).ap()
    avt1 = A("avt1", [128, 128], F32).ap()
    stt_ = A("stats", [1, 13 * CH], F32).ap()
    recb = A("recb", [16, 2 * CH], F32).ap()
    modv = A("modv", [128, NPAN], F32).ap()
    modv2 = A("modv2", [128, 128], F32).ap()
    sct = A("sct", [128, 32], F32).ap()
    cbb = A("cbb", [128, 32], F32).ap()
    gat = A("gat", [128, 96], F32).ap()
    gtt = A("gtt", [4, 2 * CH], F32).ap()
    geb = A("geb", [64, CH], F32).ap()
    gmb = A("gmb", [64, CH], F32).ap()
    cone = A("cone", [128, 255], F32).ap()
    id128 = A("id128", [128, 128], F32).ap()
    id4 = A("id4", [4, 4], F32).ap()
    id1 = A("id1", [1, 1], F32).ap()
    ones128 = A("ones128", [128, 1], F32).ap()
    ones1x = A("ones1x", [1, 128], F32).ap()
    sum8 = A("sum8", [128, 16], F32).ap()
    exp16 = A("exp16", [16, 128], F32).ap()
    exp464 = A("exp464", [4, 64], F32).ap()

    ktb = x2b[:, 0:NMK * CH]
    vtb = x2b[:, NMK * CH:NMK * CH + 2048]
    wcBf = wcB.bitcast(F32)
    mp0 = x2b[:, 0:2048]
    mp1 = x2b[:, 2048:4096]
    spi0 = wcA.bitcast(F32)[:, 0:2048]
    spi1 = wcA.bitcast(F32)[:, 2048:4096]
    spo0 = wcB[:, 0:4096]
    spo1 = wcB[:, 4096:8192]
    bsv0 = wcBf[0:64, 0:2048]
    bsv1 = wcBf[0:64, 2048:4096]
    pab = mb0
    pbb = mb1

    PS = [nc.alloc_psum_tensor(f"ps{i}", [128, 512], F32).ap() for i in range(8)]
    sems = {nm: nc.alloc_semaphore(f"sem_{nm}")
            for nm in ("pe", "dve", "act", "wa", "wb", "va", "vb", "x", "c", "st",
                       "sw", "sw2", "ma", "mb")}

    def stv(i):
        return stt_[0:1, i * CH:(i + 1) * CH]

    def schedule(b: Builder):
        b.dma("syn", cone[:, :], ConeD[:, :], key="cst0", sem="c")
        b.dma("syn", id128[:, :], Id128D[:, :], key="cst1", sem="c")
        b.dma("syn", id4[:, :], Id4D[:, :], key="cst2", sem="c")
        b.dma("syn", id1[:, :], Id1D[:, :], key="cst3", sem="c")
        b.dma("syn", ones128[:, :], Ones128D[:, :], key="cst4", sem="c")
        b.dma("syn", ones1x[:, :], Ones1xD[:, :], key="cst5", sem="c")
        b.dma("syn", sum8[:, :], Sum8D[:, :], key="cst6", sem="c")
        b.dma("syn", exp16[:, :], Exp16D[:, :], key="cst7", sem="c")
        b.dma("syn", exp464[:, :], Exp464D[:, :], key="cst8", sem="c")
        b.dma("syn", cbb[:, :], cb2D[:, :], key="ld_c", sem="c")
        b.dma("syn", xh[:, :], X2d[0], key="ld_x_0", sem="x")
        b.op("act", "act", key="silu", waits=("ld_c",),
             out=sct[:, :], in_=cbb[:, :], func=AF.Silu)

        jobs = []
        for m in range(NMQ):
            for hf in range(2):
                jobs.append((f"spq_{m}_{hf}", W2q[m][:, hf * 2048:(hf + 1) * 2048],
                             QHL[m][:, hf * 2048:(hf + 1) * 2048],
                             QHL[m][:, 4096 + hf * 2048:4096 + (hf + 1) * 2048], 2048))
        for m in range(NMK):
            for hf in range(2):
                jobs.append((f"spk_{m}_{hf}", W2k[m][:, hf * 2048:(hf + 1) * 2048],
                             KHL[m][:, hf * 2048:(hf + 1) * 2048],
                             KHL[m][:, 4096 + hf * 2048:4096 + (hf + 1) * 2048], 2048))
        for f in range(NF):
            jobs.append((f"spv_{f}", WvD[f], VHL[f][:, 0:1024],
                         VHL[f][:, 1024:2048], 1024))
        for m in range(NMQ):
            for hf in range(2):
                jobs.append((f"spo_{m}_{hf}", W2o[m][:, hf * 2048:(hf + 1) * 2048],
                             OHL[m][:, hf * 2048:(hf + 1) * 2048],
                             OHL[m][:, 4096 + hf * 2048:4096 + (hf + 1) * 2048], 2048))
        jobs.append(("spat", ATD[:, :], ATHL[:, 0:2048], ATHL[:, 2048:4096], 2048))
        nJ = len(jobs)

        def split_job(i):
            jk, srcap, hidst, lodst, w = jobs[i]
            sl = spi0 if i % 2 == 0 else spi1
            so = spo0 if i % 2 == 0 else spo1
            ww = []
            if i >= 2:
                ww += [f"spc_{i-2}", f"spl_{i-2}"]
            b.dma("syn", sl[:, 0:w], srcap, key=f"spld_{i}", waits=ww,
                  sem=("wa" if i % 2 == 0 else "wb"))
            cw = [f"spld_{i}"]
            if i >= 2:
                cw.append(f"spst_{i-2}")
            b.op("act", "scp", key=f"spc_{i}", waits=cw,
                 out=so[:, 0:w], in_=sl[:, 0:w])
            b.op("dve", "tt", key=f"spl_{i}", waits=(f"spc_{i}",),
                 out=so[:, w:2 * w], in0=sl[:, 0:w], in1=so[:, 0:w],
                 op=OP.subtract)
            b.dma("gp", hidst, so[:, 0:w], key=f"spsh_{i}", waits=(f"spl_{i}",),
                  sem=("sw" if i % 2 == 0 else "sw2"))
            b.dma("gp", lodst, so[:, w:2 * w], key=f"spst_{i}",
                  sem=("sw" if i % 2 == 0 else "sw2"))

        def mod_panel(n):
            psm = PS[n % 2]
            for half in range(2):
                i2 = 2 * n + half
                sl = mp0 if i2 % 2 == 0 else mp1
                ww = [f"g0h_{i2-2}"] if i2 >= 2 else []
                b.dma("syn", sl[:, :], W2m[n][:, half * 2048:(half + 1) * 2048],
                      key=f"w0_{i2}", waits=ww, sem=("ma" if i2 % 2 == 0 else "mb"))
                mmw = [f"w0_{i2}", "silu"]
                if n >= 2 and half == 0:
                    mmw.append(f"c0_{n-2}")
                for kq in range(4):
                    for j in range(4):
                        kk = kq * 4 + j
                        k = half * 16 + kk
                        b.mm(psm[32 * j:32 * j + 1, 0:128], sct[:, k:k + 1],
                             sl[:, kk * 128:(kk + 1) * 128],
                             start=(half == 0 and kq == 0),
                             stop=(half == 1 and kq == 3),
                             key=(f"g0h_{i2}" if (kq == 3 and j == 3) else None),
                             waits=(mmw if (kq == 0 and j == 0) else ()),
                             tp=(0, 32 * j))
            tf = mb0[:, 0:128] if n % 2 == 0 else mb1[:, 0:128]
            tw = [f"g0h_{2 * n + 1}"]
            if n >= 2:
                tw.append(f"t0_{n-2}")
            for j in range(4):
                b.op("dve", "cp", key=(f"c0_{n}" if j == 3 else None),
                     waits=(tw if j == 0 else ()),
                     out=tf[32 * j:32 * j + 1, 0:128],
                     in_=psm[32 * j:32 * j + 1, 0:128])
            pst = PS[2 + (n % 2)]
            tpw = [f"c0_{n}"]
            if n >= 2:
                tpw.append(f"c1_{n-2}")
            b.tp(pst[0:128, 0:128], tf[:, :], id128[:, :], key=f"t0_{n}", waits=tpw)
            b.op("dve", "red", key=f"c1_{n}", waits=(f"t0_{n}",),
                 out=modv[:, n:n + 1], in_=pst[0:128, 0:128:32], op=OP.add)

        for i in range(max(nJ, NPAN)):
            if i < NPAN:
                mod_panel(i)
            if i < nJ:
                split_job(i)
        b.op("dve", "ts", waits=(f"c1_{NPAN-1}",), out=modv2[:, 0:32],
             in0=modv[:, 32:64], s1=1.0, op0=OP.add)
        b.op("dve", "ts", out=modv2[:, 64:96], in0=modv[:, 128:160], s1=1.0, op0=OP.add)
        for (src0, dst0) in ((64, 32), (160, 96)):
            b.op("dve", "tss", out=gat[:, 0:32], in_=modv[:, src0:src0 + 32], s=0.0,
                 op=OP.is_ge)
            b.op("dve", "ts", out=gat[:, 32:64], in0=gat[:, 0:32], s1=2.0, s2=-1.0,
                 op0=OP.mult, op1=OP.add)
            b.op("dve", "tt", out=gat[:, 64:96], in0=modv[:, src0:src0 + 32],
                 in1=gat[:, 32:64], op=OP.mult)
            b.op("dve", "tss", out=gat[:, 0:32], in_=gat[:, 64:96], s=0.25,
                 op=OP.max)
            b.op("dve", "tt", out=modv2[:, dst0:dst0 + 32], in0=gat[:, 0:32],
                 in1=gat[:, 32:64], op=OP.mult)
        b.op("dve", "ts", key="modv2", out=modv2[:, 96:128], in0=modv2[:, 96:128],
             s1=SCALE_FFN, op0=OP.mult)

        sa_col = modv[:, 0:32]
        sb_col = modv[:, 96:128]
        caP = modv2[:, 0:32]
        gaC = modv2[:, 32:64]
        cbP = modv2[:, 64:96]
        gbC4 = modv2[:, 96:128]

        for t in range(NCH):
            stats_w = [f"ld_x_{t}"]
            if t > 0:
                stats_w.append(f"varep2_{t-1}")
            else:
                stats_w.append(f"c0_{NPAN-1}")
            for f in range(NF):
                xt = xh[:, f * CH:(f + 1) * CH]
                sq = sqb0 if f % 2 == 0 else sqb1
                sqw = [f"ld_x_{t}"]
                if f >= 2:
                    sqw.append(f"sqmm_{t}_{f-2}")
                elif t > 0:
                    sqw.append(f"stats2_{t-1}")
                    sqw.append(f"h2_{t-1}_{NF-1}")
                b.op("act", "sq", key=f"sq_{t}_{f}", waits=sqw, out=sq[:, :], in_=xt)
                b.mm(PS[0][0:1, 0:CH], ones128[:, 0:1], xt,
                     start=(f == 0), stop=(f == NF - 1),
                     waits=(stats_w if f == 0 else ()))
                b.mm(PS[1][0:1, 0:CH], ones128[:, 0:1], sq[:, :],
                     start=(f == 0), stop=(f == NF - 1),
                     key=(f"stats_{t}" if f == NF - 1 else f"sqmm_{t}_{f}"),
                     waits=(f"sq_{t}_{f}",))
            def rstd_chain(tag, statskey):
                b.op("dve", "ts", waits=(statskey,), out=stv(0), in0=PS[0][0:1, 0:CH],
                     s1=1.0 / DM, op0=OP.mult)
                b.op("dve", "ts", out=stv(1), in0=PS[1][0:1, 0:CH], s1=1.0 / DM,
                     op0=OP.mult)
                b.op("dve", "ts", out=stv(2), in0=stv(0), s1=-1.0, op0=OP.mult)
                b.op("dve", "tt", out=stv(3), in0=stv(0), in1=stv(0), op=OP.mult)
                b.op("dve", "tt", out=stv(4), in0=stv(1), in1=stv(3), op=OP.subtract)
                b.op("dve", "ts", key=f"varep{tag}", out=stv(5), in0=stv(4), s1=EPS,
                     op0=OP.add)
                b.op("act", "act", waits=(f"varep{tag}",), out=stv(6), in_=stv(5),
                     func=AF.Ln)
                b.op("act", "act", key=f"rstd0{tag}", out=stv(7), in_=stv(6),
                     func=AF.Exp, scale=-0.5)
                src = 7
                for it in range(2):
                    b.op("dve", "tt", waits=((f"rstd0{tag}",) if it == 0 else ()),
                         out=stv(8), in0=stv(src), in1=stv(src), op=OP.mult)
                    b.op("dve", "tt", out=stv(9), in0=stv(8), in1=stv(5), op=OP.mult)
                    b.op("dve", "ts", out=stv(10), in0=stv(9), s1=-0.5, s2=1.5,
                         op0=OP.mult, op1=OP.add)
                    dst = 11 if it == 0 else 12
                    b.op("dve", "tt", key=(f"rstd{tag}" if it == 1 else None),
                         out=stv(dst), in0=stv(src), in1=stv(10), op=OP.mult)
                    src = dst
                return src

            src = rstd_chain(f"_{t}", f"stats_{t}")
            bc_w = [f"rstd_{t}", "modv2" if t == 0 else f"gebc_{t-1}"]
            b.mm(PS[2][:, 0:CH], ones1x[0:1, :], stv(2), start=True, stop=True,
                 waits=bc_w)
            b.mm(PS[3][:, 0:CH], ones1x[0:1, :], stv(src), start=True, stop=True,
                 key=f"bcast_{t}")
            hw = ["modv2", f"bcast_{t}"]
            if t > 0:
                hw.append(f"pemid_{t-1}")
            for f in range(NF):
                xt = xh[:, f * CH:(f + 1) * CH]
                hh = hbh[:, f * CH:(f + 1) * CH]
                hl = hbl[:, f * CH:(f + 1) * CH]
                b.op("dve", "tt", waits=(hw if f == 0 else ()), out=sqb1[:, :], in0=xt,
                     in1=PS[2][:, 0:CH], op=OP.add)
                b.op("dve", "tt", out=sqb0[:, :],
                     in0=sqb1[:, :], in1=PS[3][:, 0:CH], op=OP.mult)
                b.op("dve", "ts", key=f"hf_{t}_{f}", out=sqb1[:, :],
                     in0=sqb0[:, :], s1=caP[:, f:f + 1], s2=sa_col[:, f:f + 1],
                     op0=OP.mult, op1=OP.add)
                b.op("act", "scp", key=f"hh_{t}_{f}", waits=(f"hf_{t}_{f}",),
                     out=hh, in_=sqb1[:, :])
                b.op("dve", "tt", key=f"h_{t}_{f}",
                     waits=(f"hh_{t}_{f}",), out=hl, in0=sqb1[:, :], in1=hh,
                     op=OP.subtract)
            for (npan, WHLs, dst, tag) in ((NMQ, QHL, qb, "q"),
                                           (NMK, KHL, ktb, "k")):
                for m in range(npan):
                    wc = wcA if m % 2 == 0 else wcB
                    whi = wc[:, 0:NF * 128]
                    wlo = wc[:, NF * 128:2 * NF * 128]
                    wkey = f"w{tag}_{t}_{m}"
                    jb = 2 * m if tag == "q" else 2 * NMQ + 2 * m
                    if tag == "q":
                        if m >= 2:
                            ww = [f"peq_{t}_{m-2}"]
                        elif t == 0:
                            ww = [f"spl_{nJ-1}", f"spst_{nJ-1}", f"spst_{nJ-2}"]
                        else:
                            ww = [f"peo_{t-1}_{NMQ-2}" if m == 0
                                  else f"ped_{t-1}_{NMQ-1}"]
                    else:
                        ww = [f"peq_{t}_{NMQ-2+m}"] if m < 2 else [f"pek_{t}_{m-2}"]
                    if t == 0:
                        ww += [f"spst_{jb}", f"spst_{jb+1}"]
                    b.dma("syn", wc[:, :], WHLs[m], key=wkey, waits=ww,
                          sem=("wa" if m % 2 == 0 else "wb"))
                    psq = PS[4 + m % 2]
                    mw = [wkey] + ([] if (tag == "q" and m == 0)
                                   else [f"h_{t}_{NF-1}"])
                    if tag == "q":
                        if m >= 2:
                            mw.append(f"evq_{t}_{m-2}")
                        elif t > 0:
                            mw.append(f"gmid_{t-1}")
                            mw.append(f"out_{t-1}_{NMQ-1}")
                            mw.append(f"st_{t-1}")
                    else:
                        mw.append(f"evq_{t}_{NMQ-2+m}" if m < 2 else f"evk_{t}_{m-2}")
                        if m < 2:
                            mw.append(f"out_{t-1}_{NMQ-1}" if t > 0 else "modv2")
                    for f in range(NF):
                        hh = hbh[:, f * CH:(f + 1) * CH]
                        hl = hbl[:, f * CH:(f + 1) * CH]
                        wh = whi[:, f * 128:(f + 1) * 128]
                        wl = wlo[:, f * 128:(f + 1) * 128]
                        for pi, (lh, rr) in enumerate(((wh, hh), (wh, hl), (wl, hh))):
                            if f == 0 and pi == 0:
                                mmw = mw + ([f"h_{t}_{0}"]
                                            if (tag == "q" and m == 0) else [])
                            elif tag == "q" and m == 0 and pi == 0:
                                mmw = [f"h_{t}_{f}"]
                            else:
                                mmw = ()
                            b.mm(psq[:, 0:CH], lh, rr,
                                 start=(f == 0 and pi == 0),
                                 stop=(f == NF - 1 and pi == 2),
                                 key=(f"pe{tag}_{t}_{m}"
                                      if (f == NF - 1 and pi == 2) else None),
                                 waits=mmw)
                    b.op("act", "scp", key=f"ev{tag}_{t}_{m}",
                         waits=(f"pe{tag}_{t}_{m}",),
                         out=dst[:, m * CH:(m + 1) * CH], in_=psq[:, 0:CH])
            for f in range(NF):
                wv = wvA if f % 2 == 0 else wvB
                wvh = wv[:, 0:1024]
                wvl = wv[:, 1024:2048]
                vw = [f"pev_{t}_{f-2}"] if f >= 2 else []
                if f < 2 and t > 0:
                    vw.append(f"pemid_{t-1}")
                if t == 0:
                    vw.append(f"spst_{2 * NMQ + 2 * NMK + f}")
                b.dma("syn", wv[:, :], VHL[f], key=f"wv_{t}_{f}", waits=vw,
                      sem=("va" if f % 2 == 0 else "vb"))
                mw0 = [f"wv_{t}_{f}"]
                if f == 0:
                    mw0.append(f"evk_{t}_{NMK-1}")
                    if t > 0:
                        mw0.append(f"out_{t-1}_{NMQ-1}")
                        mw0.append(f"av_{t-1}_{HQ-1}")
                for s2 in range(2):
                    lhh = hbh[:, f * CH + s2 * 128: f * CH + (s2 + 1) * 128]
                    lhl = hbl[:, f * CH + s2 * 128: f * CH + (s2 + 1) * 128]
                    for half in range(2):
                        for pi, (lh, rr) in enumerate(
                                ((lhh, wvh), (lhl, wvh), (lhh, wvl))):
                            b.mm(PS[4 + 2 * s2 + half][:, 0:512], lh,
                                 rr[:, half * 512:(half + 1) * 512],
                                 start=(f == 0 and pi == 0),
                                 stop=(f == NF - 1 and pi == 2),
                                 key=(f"pev_{t}_{f}"
                                      if (s2 == 1 and half == 1 and pi == 2) else None),
                                 waits=(mw0 if (s2 == 0 and half == 0 and pi == 0)
                                        else ()))
            for i in range(4):
                s2, half = divmod(i, 2)
                b.op("act", "scp", key=f"evv_{t}_{i}",
                     waits=((f"pev_{t}_{NF-1}",) if i == 0 else ()),
                     out=vtb[:, s2 * 1024 + half * 512: s2 * 1024 + (half + 1) * 512],
                     in_=PS[4 + i][:, 0:512])
            pair = 0
            for h in range(HQ):
                for j in range(KVH):
                    mbuf = mb0 if pair % 2 == 0 else mb1
                    gw = [f"evq_{t}_{h}", f"evk_{t}_{j}"]
                    if pair >= 2:
                        gw.append(f"gmm_{t}_{pair-2}")
                    b.op("dve", "tt", key=f"gm_{t}_{pair}", waits=gw, out=mbuf[:, :],
                         in0=qb[:, h * CH:(h + 1) * CH],
                         in1=ktb[:, j * CH:(j + 1) * CH], op=OP.mult)
                    tile = h // 16
                    r = (h % 16) * 8 + j
                    first = (h % 16 == 0 and j == 0)
                    last = (h % 16 == 15 and j == 7)
                    mmw = [f"gm_{t}_{pair}"]
                    if first:
                        mmw.append(f"varep_{t}")
                    b.mm(PS[tile][:, 0:CH], cone[:, 127 - r:255 - r], mbuf[:, :],
                         start=first, stop=last, key=f"gmm_{t}_{pair}", waits=mmw)
                    pair += 1
            b.op("act", "act", waits=(f"gmm_{t}_{128 * 1 - 1}",), out=e1b[:, :],
                 in_=PS[0][:, 0:CH], func=AF.Exp)
            b.op("act", "act", key=f"exp_{t}", waits=(f"gmm_{t}_{255}",),
                 out=e2b[:, :], in_=PS[1][:, 0:CH], func=AF.Exp)
            b.mm(PS[2][0:16, 0:CH], sum8[:, :], e1b[:, :], start=True, stop=True,
                 waits=[f"exp_{t}", f"h_{t}_{NF-1}"])
            b.mm(PS[3][0:16, 0:CH], sum8[:, :], e2b[:, :], start=True, stop=True,
                 key=f"gs_{t}")
            b.op("dve", "rec", waits=(f"gs_{t}",), out=recb[0:16, 0:CH],
                 in_=PS[2][0:16, 0:CH])
            b.op("dve", "rec", key=f"rec_{t}", out=recb[0:16, CH:2 * CH],
                 in_=PS[3][0:16, 0:CH])
            b.mm(PS[4][:, 0:CH], exp16[:, :], recb[0:16, 0:CH], start=True, stop=True,
                 waits=[f"rec_{t}", f"evv_{t}_3"])
            b.mm(PS[5][:, 0:CH], exp16[:, :], recb[0:16, CH:2 * CH], start=True,
                 stop=True, key=f"pexp_{t}")
            b.op("dve", "tt", waits=(f"pexp_{t}",), out=pab[:, :], in0=e1b[:, :],
                 in1=PS[4][:, 0:CH], op=OP.mult)
            b.op("dve", "tt", key=f"pnorm_{t}", out=pbb[:, :], in0=e2b[:, :],
                 in1=PS[5][:, 0:CH], op=OP.mult)
            for s2 in range(2):
                tpa = [f"pnorm_{t}"] if s2 == 0 else [f"ptkb_{t}_0"]
                if s2 == 0:
                    tpa.append(f"evv_{t}_3")
                b.tp(PS[6][:, s2 * 128:(s2 + 1) * 128], pab[:, s2 * 128:(s2 + 1) * 128],
                     id128[:, :], waits=tpa)
                b.tp(PS[7][:, s2 * 128:(s2 + 1) * 128], pbb[:, s2 * 128:(s2 + 1) * 128],
                     id128[:, :], key=f"ptp_{t}_{s2}")
                b.op("dve", "cp", key=f"ptka_{t}_{s2}", waits=(f"ptp_{t}_{s2}",),
                     out=ptok[:, s2 * 256:s2 * 256 + 128],
                     in_=PS[6][:, s2 * 128:(s2 + 1) * 128])
                b.op("dve", "cp", key=f"ptkb_{t}_{s2}",
                     out=ptok[:, s2 * 256 + 128:(s2 + 1) * 256],
                     in_=PS[7][:, s2 * 128:(s2 + 1) * 128])
            for h in range(HQ):
                for s2 in range(2):
                    pcol0 = s2 * 256 + (0 if h < 16 else 128) + (h % 16) * 8
                    avw = [f"ptkb_{t}_1"] if (s2 == 0 and h == 0) else ()
                    b.op("dve", "ts", waits=avw, out=avt0[:, :],
                         in0=vtb[:, s2 * 1024:s2 * 1024 + 128],
                         s1=ptok[:, pcol0:pcol0 + 1], op0=OP.mult)
                    cur = avt0
                    for j in range(1, 8):
                        if j == 7:
                            dsta = qb[:, h * CH + s2 * 128: h * CH + (s2 + 1) * 128]
                        else:
                            dsta = avt1[:, :] if j % 2 == 1 else avt0[:, :]
                        b.op("dve", "stt",
                             key=(f"av_{t}_{h}" if (j == 7 and s2 == 1) else None),
                             out=dsta,
                             in0=vtb[:, s2 * 1024 + j * 128:s2 * 1024 + (j + 1) * 128],
                             s=ptok[:, pcol0 + j:pcol0 + j + 1],
                             in1=cur[:, :], op0=OP.mult, op1=OP.add)
                        cur = avt1 if j % 2 == 1 else avt0
            for hh in range(HQ):
                psb = PS[4 + (hh % 2)]
                tw = [f"av_{t}_{hh}"]
                if hh == 0:
                    tw.append(f"pnorm_{t}")
                if hh >= 2:
                    tw.append(f"avevl_{t}_{hh-2}")
                for s2 in range(2):
                    b.tp(psb[:, s2 * 128:(s2 + 1) * 128],
                         qb[:, hh * CH + s2 * 128: hh * CH + (s2 + 1) * 128],
                         id128[:, :], key=(f"avtp_{t}_{hh}" if s2 == 1 else None),
                         waits=(tw if s2 == 0 else ()))
                for s2 in range(2):
                    b.op("act", "scp",
                         key=(f"avevh_{t}_{hh}" if s2 == 1 else None),
                         waits=((f"avtp_{t}_{hh}",) if s2 == 0 else ()),
                         out=hbh[:, hh * CH + s2 * 128: hh * CH + (s2 + 1) * 128],
                         in_=psb[:, s2 * 128:(s2 + 1) * 128])
                for s2 in range(2):
                    b.op("dve", "tt",
                         key=(f"avevl_{t}_{hh}" if s2 == 1 else None),
                         waits=((f"avevh_{t}_{hh}",) if s2 == 0 else ()),
                         out=hbl[:, hh * CH + s2 * 128: hh * CH + (s2 + 1) * 128],
                         in0=psb[:, s2 * 128:(s2 + 1) * 128],
                         in1=hbh[:, hh * CH + s2 * 128: hh * CH + (s2 + 1) * 128],
                         op=OP.subtract)
            for m in range(NMQ):
                wc = wcA if m % 2 == 0 else wcB
                whi = wc[:, 0:NF * 128]
                wlo = wc[:, NF * 128:2 * NF * 128]
                ww = [f"pek_{t}_{NMK-2+m}"] if m < 2 else [f"peo_{t}_{m-2}"]
                if t == 0:
                    jb = 2 * NMQ + 2 * NMK + NF + 2 * m
                    ww += [f"spst_{jb}", f"spst_{jb+1}"]
                b.dma("syn", wc[:, :], OHL[m], key=f"wo_{t}_{m}", waits=ww,
                      sem=("wa" if m % 2 == 0 else "wb"))
                pso = PS[6 + m % 2]
                mw = [f"wo_{t}_{m}"]
                if m < 2:
                    mw.append(f"ptkb_{t}_1")
                    if m == 1:
                        mw.append(f"avevl_{t}_{NF-1}")
                else:
                    mw.append(f"x2_{t}_{m-2}")
                for f in range(NF):
                    aoh = hbh[:, f * CH:(f + 1) * CH]
                    aol = hbl[:, f * CH:(f + 1) * CH]
                    wh = whi[:, f * 128:(f + 1) * 128]
                    wl = wlo[:, f * 128:(f + 1) * 128]
                    for pi, (lh, rr) in enumerate(((wh, aoh), (wh, aol), (wl, aoh))):
                        if f == 0 and pi == 0:
                            mmw = mw + ([f"avevl_{t}_{0}"] if m == 0 else [])
                        elif m == 0 and pi == 0:
                            mmw = [f"avevl_{t}_{f}"]
                        else:
                            mmw = ()
                        b.mm(pso[:, 0:CH], lh, rr,
                             start=(f == 0 and pi == 0),
                             stop=(f == NF - 1 and pi == 2),
                             key=(f"peo_{t}_{m}" if (f == NF - 1 and pi == 2) else None),
                             waits=mmw)
                x2w = [f"peo_{t}_{m}"]
                if m < 2:
                    x2w.append(f"gmm_{t}_{255}")
                b.op("dve", "stt", key=f"x2_{t}_{m}", waits=x2w,
                     out=x2b[:, m * CH:(m + 1) * CH], in0=pso[:, 0:CH],
                     s=gaC[:, m:m + 1], in1=xh[:, m * CH:(m + 1) * CH],
                     op0=OP.mult, op1=OP.add)
                sq = sqb0 if m % 2 == 0 else sqb1
                sqw = [f"x2_{t}_{m}"]
                if m >= 2:
                    sqw.append(f"sqmm2_{t}_{m-2}")
                b.op("act", "sq", key=f"sq2_{t}_{m}", waits=sqw, out=sq[:, :],
                     in_=x2b[:, m * CH:(m + 1) * CH])
                st_w = [f"x2_{t}_{m}"]
                if m == 0:
                    st_w.append(f"exp_{t}")
                b.mm(PS[0][0:1, 0:CH], ones128[:, 0:1], x2b[:, m * CH:(m + 1) * CH],
                     start=(m == 0), stop=(m == NMQ - 1), waits=st_w)
                b.mm(PS[1][0:1, 0:CH], ones128[:, 0:1], sq[:, :],
                     start=(m == 0), stop=(m == NMQ - 1),
                     key=(f"stats2_{t}" if m == NMQ - 1 else f"sqmm2_{t}_{m}"),
                     waits=(f"sq2_{t}_{m}",))
            src = rstd_chain(f"2_{t}", f"stats2_{t}")
            b.mm(PS[2][:, 0:CH], ones1x[0:1, :], stv(2), start=True, stop=True,
                 waits=[f"rstd2_{t}", f"rec_{t}"])
            b.mm(PS[3][:, 0:CH], ones1x[0:1, :], stv(src), start=True, stop=True,
                 key=f"bcast2_{t}")
            for f in range(NF):
                x2t = x2b[:, f * CH:(f + 1) * CH]
                ht = xh[:, f * CH:(f + 1) * CH]
                b.op("dve", "tt", waits=((f"bcast2_{t}",) if f == 0 else ()),
                     out=sqb0[:, :], in0=x2t, in1=PS[2][:, 0:CH], op=OP.add)
                b.op("dve", "tt", out=sqb1[:, :], in0=sqb0[:, :], in1=PS[3][:, 0:CH],
                     op=OP.mult)
                b.op("dve", "ts", key=f"h2f_{t}_{f}", out=ht,
                     in0=sqb1[:, :], s1=cbP[:, f:f + 1], s2=sb_col[:, f:f + 1],
                     op0=OP.mult, op1=OP.add)
                b.op("act", "scp", key=f"h2h_{t}_{f}", waits=(f"h2f_{t}_{f}",),
                     out=hbh[:, f * CH:(f + 1) * CH], in_=ht)
                b.op("dve", "tt", key=f"h2_{t}_{f}",
                     waits=(f"h2h_{t}_{f}",),
                     out=hbl[:, f * CH:(f + 1) * CH], in0=ht,
                     in1=hbh[:, f * CH:(f + 1) * CH], op=OP.subtract)
            atw = [f"pev_{t}_{NF-1}"]
            if t == 0:
                atw.append(f"spst_{nJ-1}")
            b.dma("syn", wvA[:, :], ATHL[:, 0:2048], key=f"at1_{t}", waits=atw,
                  sem="va")
            b.dma("syn", wvB[:, :], ATHL[:, 2048:4096], key=f"at_{t}", waits=atw,
                  sem="vb")
            if t == 0:
                b.dma("syn", wrtb[:, :], WrTD[:, :], key="wrt_0", sem="c")
            bsw = [f"peo_{t}_{NMQ-1}"]
            b.dma("syn", bsv0[:, :], BsD[:, 0:2048], key=f"bs0_{t}", waits=bsw,
                  sem="wa")
            b.dma("syn", bsv1[:, :], BsD[:, 2048:4096], key=f"bs_{t}",
                  waits=bsw, sem="wb")
            mw = [f"at1_{t}", f"at_{t}", f"avevl_{t}_{NF-1}", f"pnorm_{t}"]
            if t == 0:
                mw.append("wrt_0")
            for f in range(NF):
                ath = wvA[:, f * 64:(f + 1) * 64]
                atl = wvB[:, f * 64:(f + 1) * 64]
                h2h = hbh[:, f * CH:(f + 1) * CH]
                h2l = hbl[:, f * CH:(f + 1) * CH]
                for pi, (lh, rr) in enumerate(((ath, h2h), (ath, h2l), (atl, h2h))):
                    if f == 0 and pi == 0:
                        mmw = mw + [f"h2_{t}_{0}"]
                    elif pi == 0:
                        mmw = [f"h2_{t}_{f}"]
                    else:
                        mmw = ()
                    b.mm(PS[4][0:64, 0:CH], lh, rr,
                         start=(f == 0 and pi == 0), stop=(f == NF - 1 and pi == 2),
                         key=(f"pemid_{t}" if (f == NF - 1 and pi == 2) else None),
                         waits=mmw)
                b.mm(PS[5][0:4, 0:CH], wrtb[:, f * 4:(f + 1) * 4],
                     xh[:, f * CH:(f + 1) * CH], start=(f == 0), stop=(f == NF - 1),
                     key=(f"pelg_{t}" if f == NF - 1 else None))
            b.op("dve", "cp", key=f"lsb_{t}", waits=(f"pelg_{t}",),
                 out=gtt[0:4, 0:CH], in_=PS[5][0:4, 0:CH])
            for s2 in range(2):
                tw = [f"lsb_{t}"] if s2 == 0 else [f"gtt_{t}_0"]
                b.tp(PS[5][:, 256 + 4 * s2:256 + 4 * (s2 + 1)],
                     gtt[0:4, s2 * 128:(s2 + 1) * 128], id4[0:4, 0:4],
                     key=f"lgt_{t}_{s2}", waits=tw)
                L = gat[:, 0:4]
                GT = gat[:, 4:8]
                RK = gat[:, 8:12]
                MS = gat[:, 12:16]
                E4 = gat[:, 16:20]
                EM = gat[:, 20:24]
                M4 = gat[:, 24:25]
                S4 = gat[:, 25:26]
                R4 = gat[:, 26:27]
                GS = gat[:, 28 + 4 * s2:32 + 4 * s2]
                b.op("dve", "cp", waits=(f"lgt_{t}_{s2}",), out=L,
                     in_=PS[5][:, 256 + 4 * s2:256 + 4 * (s2 + 1)])
                for e in range(4):
                    b.op("dve", "tss", out=GT, in_=L, s=L[:, e:e + 1], op=OP.is_gt)
                    b.op("dve", "red", out=RK[:, e:e + 1], in_=GT, op=OP.add)
                    if e > 0:
                        b.op("dve", "tss", out=GT[:, 0:e], in_=L[:, 0:e],
                             s=L[:, e:e + 1], op=OP.is_equal)
                        b.op("dve", "red", out=M4, in_=GT[:, 0:e], op=OP.add)
                        b.op("dve", "tt", out=RK[:, e:e + 1], in0=RK[:, e:e + 1],
                             in1=M4, op=OP.add)
                b.op("dve", "tss", out=MS, in_=RK, s=1.5, op=OP.is_lt)
                b.op("dve", "red", key=f"m4_{t}_{s2}", out=M4, in_=L, op=OP.max,
                     negate=True)
                b.op("act", "act", key=f"e4_{t}_{s2}", waits=(f"m4_{t}_{s2}",),
                     out=E4, in_=L, func=AF.Exp, bias=M4)
                b.op("dve", "tt", waits=(f"e4_{t}_{s2}",), out=EM, in0=E4, in1=MS,
                     op=OP.mult)
                b.op("dve", "red", out=S4, in_=EM, op=OP.add)
                b.op("dve", "rec", out=R4, in_=S4)
                b.op("dve", "ts", key=f"gates_{t}_{s2}", out=GS, in0=EM,
                     s1=R4[:, 0:1], op0=OP.mult)
                b.tp(PS[5][0:4, s2 * 128:(s2 + 1) * 128], GS, id128[:, :],
                     key=f"gt_{t}_{s2}", waits=(f"gates_{t}_{s2}",))
                b.op("dve", "cp", key=f"gtt_{t}_{s2}", waits=(f"gt_{t}_{s2}",),
                     out=gtt[0:4, CH + s2 * 128:CH + (s2 + 1) * 128],
                     in_=PS[5][0:4, s2 * 128:(s2 + 1) * 128])
            b.mm(PS[2][0:64, 256:256 + CH], exp464[0:4, :], gtt[0:4, CH:2 * CH],
                 start=True, stop=True, key=f"ge_{t}",
                 waits=[f"gtt_{t}_1", f"h2_{t}_{NF-1}"])
            b.op("dve", "cp", key=f"gebc_{t}", waits=(f"ge_{t}",), out=geb[:, :],
                 in_=PS[2][0:64, 256:256 + CH])
            b.op("dve", "tt", key=f"gmid_{t}", waits=(f"pemid_{t}",), out=gmb[:, :],
                 in0=PS[4][0:64, 0:CH], in1=geb[:, :], op=OP.mult)
            for m in range(NMQ):
                psd = PS[6 + m % 2]
                bsl = bsv0 if m < 16 else bsv1
                mw = [f"gmid_{t}", f"bs0_{t}" if m < 16 else f"bs_{t}"]
                mw.append(f"x2_{t}_{NMQ-1}" if m < 2 else f"out_{t}_{m-2}")
                b.mm(psd[:, 0:CH], bsl[:, (m % 16) * 128:(m % 16 + 1) * 128],
                     gmb[:, :], start=True, stop=True, key=f"ped_{t}_{m}", waits=mw)
                ow = [f"ped_{t}_{m}"]
                if m == 0 and t > 0:
                    ow.append(f"st_{t-1}")
                b.op("dve", "stt", key=f"out_{t}_{m}", waits=ow,
                     out=qb[:, m * CH:(m + 1) * CH], in0=psd[:, 0:CH],
                     s=gbC4[:, m:m + 1], in1=x2b[:, m * CH:(m + 1) * CH],
                     op0=OP.mult, op1=OP.add)
            b.dma("gp", OUTd[t], qb[:, :], key=f"st_{t}", waits=(f"out_{t}_{NMQ-1}",), sem="st")
            if t + 1 < NCH:
                b.dma("syn", xh[:, :], X2d[t + 1], key=f"ld_x_{t+1}",
                      waits=(f"pelg_{t}", f"h2_{t}_{NF-1}"), sem="x")

    cb = Builder("count")
    schedule(cb)
    events = cb.events

    with nc.Block() as block:

        @block.sync
        def _(sync):
            schedule(Builder("syn", events, {"syn": sync}, sems))

        @block.tensor
        def _(te):
            schedule(Builder("pe", events, {"pe": te}, sems))

        @block.vector
        def _(ve):
            schedule(Builder("dve", events, {"dve": ve}, sems))

        @block.scalar
        def _(sc):
            schedule(Builder("act", events, {"act": sc}, sems))

        @block.gpsimd
        def _(gp):
            schedule(Builder("gp", events, {"gp": gp}, sems))
            gp.wait_ge(sems["st"], events[f"st_{NCH-1}"][1])

    return nc


def _consts():
    cone = np.zeros((128, 255), dtype=np.float32)
    cone[:, 127] = np.float32(RSCALE)
    sum8 = np.zeros((128, 16), dtype=np.float32)
    for p in range(128):
        sum8[p, p // 8] = 1.0
    exp16 = np.zeros((16, 128), dtype=np.float32)
    for p in range(128):
        exp16[p // 8, p] = 1.0
    exp464 = np.zeros((4, 64), dtype=np.float32)
    for e in range(4):
        exp464[e, e * 16:(e + 1) * 16] = 1.0
    return dict(ConeD=cone, Id128D=np.eye(128, dtype=np.float32),
                Id4D=np.eye(4, dtype=np.float32), Id1D=np.eye(1, dtype=np.float32),
                Ones128D=np.ones((128, 1), dtype=np.float32),
                Ones1xD=np.ones((1, 128), dtype=np.float32),
                Sum8D=sum8, Exp16D=exp16, Exp464D=exp464)


def _stage(x, Wq, Wk, Wv, Wo, Af, Bf, Wr, c, wmod):
    W2q = np.ascontiguousarray(
        Wq.reshape(32, 128, 32, 128).transpose(2, 1, 0, 3)).reshape(32, 128, 32 * 128)
    W2k = np.ascontiguousarray(
        Wk.reshape(32, 128, 8, 128).transpose(2, 1, 0, 3)).reshape(8, 128, 32 * 128)
    W2o = np.ascontiguousarray(
        Wo.reshape(32, 128, 32, 128).transpose(2, 1, 0, 3)).reshape(32, 128, 32 * 128)
    W2m = np.ascontiguousarray(
        wmod.reshape(32, 128, 192, 128).transpose(2, 1, 0, 3)).reshape(192, 128, 32 * 128)
    WvD = np.ascontiguousarray(Wv.reshape(32, 128, 1024))
    WrTD = np.ascontiguousarray(
        Wr.reshape(4, 32, 128).transpose(2, 1, 0)).reshape(128, 128)
    ATD = np.ascontiguousarray(
        Af.transpose(2, 0, 1).reshape(32, 128, 64).transpose(1, 0, 2)).reshape(128, 2048)
    BsD = np.ascontiguousarray(Bf.transpose(0, 2, 1).reshape(64, 4096))
    return W2q, W2k, W2o, W2m, WvD, WrTD, ATD, BsD


def kernel(L0_attn_hidden_states, L0_attn_Wq, L0_attn_Wk, L0_attn_Wv, L0_attn_Wo,
           L0_ffn_A, L0_ffn_B, L0_ffn_Wr, L0_c, L0_wmod):
    x = np.ascontiguousarray(L0_attn_hidden_states, dtype=np.float32)
    Wq = np.ascontiguousarray(L0_attn_Wq, dtype=np.float32)
    Wk = np.ascontiguousarray(L0_attn_Wk, dtype=np.float32)
    Wv = np.ascontiguousarray(L0_attn_Wv, dtype=np.float32)
    Wo = np.ascontiguousarray(L0_attn_Wo, dtype=np.float32)
    Af = np.ascontiguousarray(L0_ffn_A, dtype=np.float32)
    Bf = np.ascontiguousarray(L0_ffn_B, dtype=np.float32)
    Wr = np.ascontiguousarray(L0_ffn_Wr, dtype=np.float32)
    c = np.ascontiguousarray(L0_c, dtype=np.float32)
    wmod = np.ascontiguousarray(L0_wmod, dtype=np.float32)

    W2q, W2k, W2o, W2m, WvD, WrTD, ATD, BsD = _stage(x, Wq, Wk, Wv, Wo, Af, Bf, Wr,
                                                     c, wmod)
    consts = _consts()
    nc = build_nc()
    in_maps = []
    for core in range(8):
        bsel, half = core // 2, core % 2
        xs = x[bsel, half * NTOK:(half + 1) * NTOK, :]
        X2 = np.ascontiguousarray(
            xs.reshape(NCH, CH, 32, 128).transpose(0, 3, 2, 1)).reshape(NCH, 128,
                                                                        32 * CH)
        cb2 = np.ascontiguousarray(c[bsel].reshape(32, 128).T)
        in_maps.append(dict(X2=X2, W2q=W2q, W2k=W2k, W2o=W2o, W2m=W2m, WvD=WvD,
                            WrTD=WrTD, ATD=ATD, BsD=BsD, cb2D=cb2, **consts))

    res = run_bass_kernel_spmd(nc, in_maps, core_ids=list(range(8)))
    out = np.empty((B, S, DM), dtype=np.float32)
    for core in range(8):
        bsel, half = core // 2, core % 2
        o = np.asarray(res.results[core]["OUT"]).reshape(NCH, 128, 32, CH)
        out[bsel, half * NTOK:(half + 1) * NTOK, :] = (
            o.transpose(0, 3, 2, 1).reshape(NTOK, DM))
    return out
